# version 14
# baseline (speedup 1.0000x reference)
import contextlib
import math

import numpy as np

import concourse.bass as bass
import concourse.mybir as mybir
from concourse.bass_utils import run_bass_kernel_spmd

F32 = mybir.dt.float32
BF16 = mybir.dt.bfloat16
AF = mybir.ActivationFunctionType
ALU = mybir.AluOpType
AX = mybir.AxisListType

NCORES = 8
B, S, D = 4, 4096, 4096
NH, HD = 32, 128
T = 2048
NSTRIP = 8
QSCALE = 1.0 / math.sqrt(HD)
MSHIFT = 20.0
NEG = -1e30

NEXP, TOPK, RDIM, RANK = 64, 16, 16, 8
MOE_SCALING = 2.0
HFF = 11008
HPAD = 11264
JFOLD = 2752
JPAD = 2816
LORA_SC = 0.5


def _alibi_slopes(n: int) -> np.ndarray:
    cp2 = 2 ** math.floor(math.log2(n))
    base = 2.0 ** (-(2.0 ** (-(math.log2(cp2) - 3))))
    slopes = base ** np.arange(1, cp2 + 1, dtype=np.float64)
    if cp2 != n:
        extra_base = 2.0 ** (-(2.0 ** (-(math.log2(2 * cp2) - 3))))
        rem = min(cp2, n - cp2)
        extra = extra_base ** np.arange(1, 1 + 2 * rem, 2, dtype=np.float64)
        slopes = np.concatenate([slopes, extra])
    return slopes.astype(np.float32)



def _own_rows(c: int) -> np.ndarray:
    b, a = c // 2, c % 2
    idx = []
    for s in range(NSTRIP):
        st = 512 * s + 256 * a
        idx.extend(range(st, st + 256))
    return np.asarray(idx)


def _l1_perm_pad_rows(w: np.ndarray) -> np.ndarray:
    out = np.zeros((HPAD,) + w.shape[1:], dtype=w.dtype)
    for jc in range(22):
        npj = 128 if jc < 21 else 64
        for r in range(4):
            slot0 = (jc * 4 + r) * 128
            src0 = JFOLD * r + 128 * jc
            out[slot0:slot0 + npj] = w[src0:src0 + npj]
    return out


def _w2_pad_rows(w2qT: np.ndarray) -> np.ndarray:
    out = np.zeros((JPAD,) + w2qT.shape[1:], dtype=w2qT.dtype)
    out[:JFOLD] = w2qT
    return out


def host_prep(inputs: dict, c: int) -> dict:
    b, a = c // 2, c % 2
    f32 = lambda x: np.ascontiguousarray(np.asarray(x), dtype=np.float32)
    d = {}
    x0 = f32(inputs["L0_attn_hidden_states"])
    d["x0"] = np.ascontiguousarray(x0[b][_own_rows(c)])
    wqkv = f32(inputs["L0_attn_Wqkv"])
    d["w0q"] = np.ascontiguousarray(wqkv[:, 2048 * a:2048 * a + 2048])
    d["w0kv"] = np.ascontiguousarray(wqkv[:, 4096:4352])
    d["w0o"] = f32(inputs["L0_attn_Wo"])
    d["awT"] = np.ascontiguousarray(f32(inputs["L0_ffn_A_w"]).T)
    d["bwT"] = np.ascontiguousarray(f32(inputs["L0_ffn_B_w"]).T)
    d["wqwT"] = np.ascontiguousarray(f32(inputs["L0_ffn_Wq_w"]).T)
    d["keysT"] = np.ascontiguousarray(f32(inputs["L0_ffn_keys"]).T)
    d["w1q_attn"] = np.ascontiguousarray(f32(inputs["L1_attn_Wq"])[:, 2048 * a:2048 * a + 2048])
    d["w1kv"] = np.ascontiguousarray(f32(inputs["L1_attn_Wkv"])[:, 256 * c:256 * c + 256])
    d["w1o"] = f32(inputs["L1_attn_Wo"])
    d["bo_b"] = np.ascontiguousarray(
        np.broadcast_to(f32(inputs["L1_attn_bo"])[None, :], (1, D)))
    w1pp = _l1_perm_pad_rows(f32(inputs["L1_ffn_w1q"]))
    w3pp = _l1_perm_pad_rows(f32(inputs["L1_ffn_w3q"]))
    b1pp = _l1_perm_pad_rows(f32(inputs["L1_ffn_b1"]))
    b3pp = _l1_perm_pad_rows(f32(inputs["L1_ffn_b3"]))
    d["w1ppT"] = np.ascontiguousarray(w1pp.T)
    d["w3ppT"] = np.ascontiguousarray(w3pp.T)
    d["b1ppT"] = np.ascontiguousarray(b1pp.T)
    d["b3ppT"] = np.ascontiguousarray(b3pp.T)
    a1T = np.ascontiguousarray(f32(inputs["L1_ffn_a1"]).T)
    a3T = np.ascontiguousarray(f32(inputs["L1_ffn_a3"]).T)
    a13 = np.zeros((128, 128), np.float32)
    for ch in range(32):
        a13[:, 4 * ch:4 * ch + 2] = a1T[128 * ch:128 * (ch + 1)]
        a13[:, 4 * ch + 2:4 * ch + 4] = a3T[128 * ch:128 * (ch + 1)]
    d["a13sb"] = np.ascontiguousarray(a13)
    a2pp = _l1_perm_pad_rows(np.ascontiguousarray(f32(inputs["L1_ffn_a2"]).T))
    d["a2sb"] = np.ascontiguousarray(
        a2pp.reshape(88, 128, 2).transpose(1, 0, 2).reshape(128, 176))
    d["w2ppT"] = _w2_pad_rows(np.ascontiguousarray(f32(inputs["L1_ffn_w2q"]).T))
    d["b2T"] = np.ascontiguousarray(f32(inputs["L1_ffn_b2"]).T)

    slopes = _alibi_slopes(NH)[16 * a:16 * a + 16]
    d["slopes"] = np.ascontiguousarray(
        np.broadcast_to(slopes[None, :], (128, 16)).astype(np.float32))
    hh, ss, kk = np.meshgrid(np.arange(16), np.arange(8), np.arange(32), indexing="ij")
    ctab = (128 * kk - 512 * ss + 127).astype(np.float32)
    btab = slopes[hh.astype(np.int64)] * ctab - MSHIFT
    d["biastab"] = np.ascontiguousarray(
        np.broadcast_to(btab.reshape(1, -1), (128, 16 * 8 * 32)).astype(np.float32))
    kp = np.arange(128)[None, :, None]
    qf = np.arange(512)[None, None, :]
    rr = np.arange(4)[:, None, None]
    d["diagmask"] = np.ascontiguousarray(
        (kp <= qf - 128 * rr).astype(np.float32))
    d["identf"] = np.eye(128, dtype=np.float32)
    cvals = np.array([-MSHIFT, 1e-5, 1.0 / D, 1.0], dtype=np.float32)
    d["consts"] = np.ascontiguousarray(np.broadcast_to(cvals[None, :], (128, 4)))
    ee = np.arange(NEXP)[:, None]
    nn = np.arange(NEXP * RANK)[None, :]
    d["expmat"] = np.ascontiguousarray((nn // RANK == ee).astype(np.float32))
    return d


def assemble_out(results) -> np.ndarray:
    out = np.empty((B, S, D), dtype=np.float32)
    for c in range(NCORES):
        out[c // 2].reshape(S, D)[_own_rows(c)] = results[c]["x_out"]
    return out



class Sem:
    __slots__ = ("name", "n", "h")

    def __init__(self, name):
        self.name = name
        self.n = 0
        self.h = None

    def add(self, k=1):
        self.n += k
        return self.n


class Prog:

    ENGS = ("sync", "gpsimd", "tensor", "vector", "scalar")

    def __init__(self, nc):
        self.nc = nc
        self.q = {e: [] for e in self.ENGS}
        self.sems = []

    def sem(self, name):
        s = Sem(name)
        self.sems.append(s)
        return s

    def raw(self, eng, fn):
        self.q[eng].append(fn)

    def wait(self, eng, sem, val=None):
        v = sem.n if val is None else val
        if v <= 0:
            return
        self.q[eng].append(lambda e, s=sem, v=v: e.wait_ge(s.h, v))

    def op(self, eng, fn, inc=None, k=1):
        if inc is None:
            self.q[eng].append(fn)
            return None
        v = inc.add(k)
        self.q[eng].append(lambda e, f=fn, s=inc, kk=k: f(e).then_inc(s.h, kk))
        return v

    def dma(self, eng, out, in_, inc, wait=None, **kw):
        if wait:
            for s, v in wait:
                self.wait(eng, s, v)
        v = inc.add(16)
        self.q[eng].append(
            lambda e, o=out, i=in_, s=inc, kw2=kw: e.dma_start(out=o, in_=i, **kw2).then_inc(s.h, 16))
        return v

    def build_into(self, block):
        for eng in self.ENGS:
            ops = self.q[eng]
            if not ops:
                continue
            def make(ops_):
                def body(e):
                    for f in ops_:
                        f(e)
                return body
            getattr(block, eng)(make(ops))


def ceil_div(a, b):
    return (a + b - 1) // b


class Builder:
    def __init__(self, nc, cut=0):
        self.nc = nc
        self.p = Prog(nc)
        self.cut = cut
        self.es = contextlib.ExitStack()
        self._mk_dram()
        self._mk_sbuf()
        self._mk_sems()

    def _mk_dram(self):
        nc = self.nc
        dt = {}
        def ein(name, shape, dtype=F32):
            dt[name] = nc.dram_tensor(name, shape, dtype, kind="ExternalInput")
        def internal(name, shape, dtype):
            dt[name] = nc.dram_tensor(name, shape, dtype)

        want_l0 = self.cut in (0, 1)
        want_l1 = self.cut in (0, 2)

        if want_l0:
            ein("x0", [T, D])
            ein("w0q", [D, 2048]); ein("w0kv", [D, 256]); ein("w0o", [D, D])
            ein("awT", [D, 512]); ein("bwT", [512, D])
            ein("wqwT", [D, RDIM]); ein("keysT", [RDIM, NEXP])
            ein("expmat", [NEXP, NEXP * RANK])
            ein("biastab", [128, 16 * 8 * 32])
        if want_l1:
            ein("w1q_attn", [D, 2048]); ein("w1kv", [D, 256]); ein("w1o", [D, D])
            ein("bo_b", [1, D])
            ein("w1ppT", [1024, HPAD]); ein("w3ppT", [1024, HPAD])
            ein("b1ppT", [2, HPAD]); ein("b3ppT", [2, HPAD])
            ein("a13sb", [128, 128])
            ein("a2sb", [128, 176]); ein("w2ppT", [JPAD, D]); ein("b2T", [2, D])
        ein("slopes", [128, 16])
        ein("diagmask", [4, 128, 512])
        ein("identf", [128, 128])
        ein("consts", [128, 4])
        if self.cut == 2:
            ein("xB", [T, D])

        if self.cut == 1:
            dt["xB_out"] = nc.dram_tensor("xB_out", [T, D], F32, kind="ExternalOutput")
        if want_l1:
            dt["x_out"] = nc.dram_tensor("x_out", [T, D], F32, kind="ExternalOutput")

        for k in range(4):
            internal(f"lnxp{k}", [D, 512], BF16)
            dt[f"LNXALLp{k}"] = nc.dram_tensor(f"LNXALLp{k}", [NCORES * D, 512],
                                               BF16, addr_space="Shared")
            internal(f"LNXPAIRp{k}", [2 * D, 512], BF16)
            internal(f"ctxp{k}", [4 * NSTRIP * 2 * 128, 256], BF16)
            dt[f"CTXALLp{k}"] = nc.dram_tensor(
                f"CTXALLp{k}", [NCORES * 4 * NSTRIP * 2 * 128, 256], BF16,
                addr_space="Shared")
            internal(f"CTXSELFp{k}", [2 * 4 * NSTRIP * 128, 256], BF16)
        internal("dbgstats", [16 * 128, 8], F32)
        internal("dbgp", [32 * 128, 512], BF16)
        internal("qtm", [S, 2048], BF16)
        internal("kvK0", [128, S], BF16); internal("kvV0", [128, S], BF16)
        if want_l0:
            internal("xA", [T, D], F32)
            if self.cut == 0:
                internal("xB", [T, D], F32)
        if want_l1:
            internal("kv1K", [4 * 128, S], BF16)
            internal("kv1V", [4 * 128, S], BF16)
            internal("xC", [T, D], F32)
        self.wb = {}
        def wcast(name, shape):
            self.wb[name] = nc.dram_tensor(name + "_b", shape, BF16)
        if want_l0:
            wcast("w0q", [D, 2048]); wcast("w0kv", [D, 256]); wcast("w0o", [D, D])
            wcast("awT", [D, 512]); wcast("bwT", [512, D])
            wcast("expmat", [NEXP, NEXP * RANK])
        if want_l1:
            wcast("w1q_attn", [D, 2048]); wcast("w1kv", [D, 256]); wcast("w1o", [D, D])
            wcast("w1ppT", [1024, HPAD]); wcast("w3ppT", [1024, HPAD])
            wcast("b1ppT", [2, HPAD]); wcast("b3ppT", [2, HPAD])
            wcast("a13sb", [128, 128])
            wcast("a2sb", [128, 176]); wcast("w2ppT", [JPAD, D]); wcast("b2T", [2, D])
            wcast("bo_b", [1, D])
        self.dt = dt

    def _mk_sbuf(self):
        nc, es = self.nc, self.es
        self.aF = es.enter_context(nc.sbuf_tensor("arenaF", [128, 14336], F32))
        self.aB = es.enter_context(nc.sbuf_tensor("arenaB", [128, 55296], BF16))
        self.cF = es.enter_context(nc.sbuf_tensor("constF", [128, 640 + 16 + 4 + 4096], F32))
        self.cB = es.enter_context(nc.sbuf_tensor("constB", [128, 6016], BF16))
        self.idF = es.enter_context(nc.sbuf_tensor("idF", [128, 128], F32))
        self.idB = es.enter_context(nc.sbuf_tensor("idB", [128, 128], BF16))

    def _mk_sems(self):
        p = self.p
        self.sLd = p.sem("sLd")
        self.sLd2 = p.sem("sLd2")
        self.sCast = p.sem("sCast")
        self.sSt = p.sem("sSt")
        self.sPE = p.sem("sPE")
        self.sV = p.sem("sV")
        self.sA = p.sem("sA")
        self.sCC = p.sem("sCC")
        self.sG = p.sem("sG")
        self.sFree = p.sem("sFree")
        self.sMisc = p.sem("sMisc")

    @property
    def dm(self):
        return self.cF[:, 0:512]
    @property
    def slopes_sb(self):
        return self.cF[:, 512:528]
    @property
    def consts_sb(self):
        return self.cF[:, 528:532]
    @property
    def biastab_sb(self):
        return self.cF[:, 532:532 + 4096]
    def btile(self, i):
        return self.cB[:, 512 * i:512 * (i + 1)]
    @property
    def ones_col_b(self):
        return self.cB[:, 2560:2561]
    @property
    def ones_row_f(self):
        return self.cF[0:1, 4628:4756]
    @property
    def ones_full_b(self):
        return self.cB[:, 5888:6016]
    @property
    def ones_row_b(self):
        return self.cB[0:1, 2561:2689]
    def diagmask_b(self, r):
        return self.cB[:, 2689 + 512 * r: 2689 + 512 * (r + 1)]

    def prologue(self):
        p, dt = self.p, self.dt
        p.dma("sync", self.idF[:, :], dt["identf"][:, :], self.sLd)
        p.dma("sync", self.cF[:, 512:528], dt["slopes"][:, :], self.sLd)
        p.dma("sync", self.cF[:, 528:532], dt["consts"][:, :], self.sLd)
        p.dma("gpsimd", self.idB[:, :], dt["identf"][:, :], self.sCast)
        for r in range(4):
            p.dma("gpsimd", self.diagmask_b(r), dt["diagmask"][r, :, :], self.sCast)
        def mkregs(e):
            rv = e.partition_id()
            self._pb = e.snap(e.scalar_reg_alu(ALU.bitwise_and, rv, 0xFFFFFFFE),
                              min_val=0, max_val=6)
            self._aa = e.snap(e.scalar_reg_alu(ALU.bitwise_and, rv, 1),
                              min_val=0, max_val=1)
            self._pbD = e.snap(e.scalar_reg_alu(ALU.mult, self._pb, D),
                               min_val=0, max_val=6 * D)
            return e.nop()
        p.raw("sync", mkregs)
        p.op("gpsimd", lambda e: e.iota(self.dm, [[-1, 512]], base=-127,
                                        channel_multiplier=1,
                                        allow_small_or_imprecise_dtypes=True),
             inc=self.sG)
        p.op("vector", lambda e: e.memset(self.ones_col_b, 1.0), inc=self.sV)
        p.op("vector", lambda e: e.memset(self.cB[0:1, 2561:2689], 1.0), inc=self.sV)
        p.op("vector", lambda e: e.memset(self.cF[0:1, 4628:4756], 1.0), inc=self.sV)
        p.op("vector", lambda e: e.memset(self.cB[:, 5888:6016], 1.0), inc=self.sV)
        self.cast_consts = self.sCast.n
        self.wcast_done = {}
        order_l0 = ["w0q", "w0kv", "w0o", "awT", "bwT", "expmat"]
        order_l1 = ["w1q_attn", "w1kv", "w1o", "bo_b",
                    "w1ppT", "w3ppT", "b1ppT", "b3ppT", "a13sb",
                    "a2sb", "w2ppT", "b2T"]
        order = []
        if self.cut in (0, 1):
            order += order_l0
        if self.cut in (0, 2):
            order += order_l1
        self._order_l0, self._order_l1 = order_l0, order_l1
        import os
        skip = os.environ.get("SKIP_WCAST", "") == "1"
        self._wcast_order = [] if skip else order
        self._wcast_emitted = set()

    def cast_chunks(self, names):
        dt = self.dt
        out = []
        for name in names:
            if name not in self.wb or name in self._wcast_emitted \
                    or name not in self._wcast_order:
                continue
            self._wcast_emitted.add(name)
            srct = dt[name]
            rows = srct.shape[0]
            step = max(128, min(rows, (1 << 22) // max(1, srct.shape[1])))
            spans = [(r, min(rows, r + step)) for r in range(0, rows, step)]
            for idx, span in enumerate(spans):
                out.append((name, idx == len(spans) - 1, span))
        return out

    def emit_cast_chunk(self, chunk):
        name, is_last, (r, r2) = chunk
        v = self.p.dma("gpsimd", self.wb[name][r:r2, :],
                       self.dt[name][r:r2, :], self.sCast)
        if is_last:
            self.wcast_done[name] = v

    def emit_casts(self, names):
        for chunk in self.cast_chunks(names):
            self.emit_cast_chunk(chunk)

    def fence(self, include_cast=False):
        p = self.p
        sems = [self.sLd, self.sLd2, self.sSt, self.sPE, self.sV,
                self.sA, self.sCC, self.sG, self.sMisc]
        if include_cast:
            sems.append(self.sCast)
        for eng in Prog.ENGS:
            for s in sems:
                p.wait(eng, s)

    def wait_wcast(self, eng, *names):
        vals = [self.wcast_done.get(n, 0) for n in names]
        v = max(vals) if vals else 0
        if v:
            self.p.wait(eng, self.sCast, v)

    def ln_tile_A(self, xf, stats, junk):
        p = self.p
        v0 = p.op("vector", lambda e: e.tensor_scalar(
            junk, xf, 1.0, None, ALU.mult, ALU.add, accum_out=stats[:, 0:1]),
            inc=self.sV)
        a0 = p.op("scalar", lambda e: e.activation(
            junk, xf, AF.Square, accum_out=stats[:, 1:2]), inc=self.sA)
        return (v0, a0)

    def ln_tile_B(self, xf, stats, out_bf, va, out_f32=None):
        p = self.p
        sV, sA = self.sV, self.sA
        v0, a0 = va
        p.wait("vector", sA, a0)
        p.wait("vector", sV, v0)
        v1 = p.op("vector", lambda e: e.tensor_scalar(
            stats[:, 2:3], stats[:, 0:1], 1.0 / D, None, ALU.mult), inc=sV)
        p.wait("vector", sV, v1)
        v2 = p.op("vector", lambda e: e.tensor_scalar(
            stats[:, 3:4], stats[:, 1:2], 1.0 / D, None, ALU.mult), inc=sV)
        p.wait("vector", sV, v2)
        v3 = p.op("vector", lambda e: e.tensor_tensor(
            stats[:, 4:5], stats[:, 2:3], stats[:, 2:3], ALU.mult), inc=sV)
        p.wait("vector", sV, v3)
        v4 = p.op("vector", lambda e: e.tensor_tensor(
            stats[:, 5:6], stats[:, 3:4], stats[:, 4:5], ALU.subtract), inc=sV)
        p.wait("vector", sV, v4)
        v5 = p.op("vector", lambda e: e.tensor_scalar(
            stats[:, 5:6], stats[:, 5:6], 1e-5, None, ALU.add), inc=sV)
        p.wait("scalar", sV, v5)
        a1 = p.op("scalar", lambda e: e.activation(
            stats[:, 6:7], stats[:, 5:6], AF.Sqrt), inc=sA)
        p.wait("vector", sA, a1)
        v6 = p.op("vector", lambda e: e.reciprocal(stats[:, 7:8], stats[:, 6:7]),
                  inc=sV)
        p.wait("vector", sV, v6)
        last = p.op("vector", lambda e: e.tensor_scalar(
            out_bf, xf, stats[:, 2:3], stats[:, 7:8], ALU.subtract, ALU.mult), inc=sV)
        if out_f32 is not None:
            last = p.op("vector", lambda e: e.tensor_scalar(
                out_f32, xf, stats[:, 2:3], stats[:, 7:8], ALU.subtract, ALU.mult),
                inc=sV)
        return last

    def ln_tile(self, xf, stats, junk, out_bf, out_f32=None):
        va = self.ln_tile_A(xf, stats, junk)
        return self.ln_tile_B(xf, stats, out_bf, va, out_f32=out_f32)

    def transpose_tile_to_staging(self, src_bf, staging, tt, wait_v):
        p = self.p
        p.wait("tensor", self.sV, wait_v)
        p.wait("tensor", self.sA, getattr(self, "_tts_prevA", 0))
        ev_done = {}
        for grp in range(4):
            pT = self.pT2[grp % 2]
            if grp >= 2:
                s2, c2 = ev_done[grp - 2]
                p.wait("tensor", s2, c2)
            pe_last = None
            for j in range(8):
                ch = grp * 8 + j
                pe_last = p.op("tensor", lambda e, ch=ch, j=j, pT=pT: e.transpose(
                    pT[:, 128 * j:128 * (j + 1)],
                    src_bf[:, 128 * ch:128 * (ch + 1)], self.idB[:, :]), inc=self.sPE)
            if grp % 2 == 0:
                p.wait("vector", self.sPE, pe_last)
                for j in range(8):
                    ch = grp * 8 + j
                    p.op("vector", lambda e, ch=ch, j=j, pT=pT: e.tensor_copy(
                        staging[:, 512 * ch + 128 * tt: 512 * ch + 128 * (tt + 1)],
                        pT[:, 128 * j:128 * (j + 1)]), inc=self.sV)
                ev_done[grp] = (self.sV, self.sV.n)
            else:
                p.wait("scalar", self.sPE, pe_last)
                for j in range(8):
                    ch = grp * 8 + j
                    p.op("scalar", lambda e, ch=ch, j=j, pT=pT: e.activation(
                        staging[:, 512 * ch + 128 * tt: 512 * ch + 128 * (tt + 1)],
                        pT[:, 128 * j:128 * (j + 1)], AF.Copy), inc=self.sA)
                ev_done[grp] = (self.sA, self.sA.n)
        self._tts_prevA = self.sA.n
        return (self.sV.n, self.sA.n)

    def ln_phase(self, src_name):
        self.fence()
        self.lnx_pair_ready = {}
        self.lnx_all_ready = {}
        self.pT2 = self.psum_open([("pT", [128, 1024], BF16), ("pT_b", [128, 1024], BF16)])
        p, dt = self.p, self.dt
        p.wait("tensor", self.sCast, self.cast_consts)
        src = dt[src_name]
        af, ab = self.aF, self.aB
        xf = [af[:, 0:4096], af[:, 4096:8192]]
        junk = af[:, 8192:12288]
        stats2 = [af[:, 12288:12296], af[:, 12304:12312]]
        lnb = [ab[:, 0:4096], ab[:, 4096:8192]]
        staging = [ab[:, 8192 + 16384 * i:8192 + 16384 * (i + 1)] for i in range(2)]
        tr_done = {}
        st_done = {}
        va_at = {}
        def emit_B(i):
            pi = i % 2
            s, tt = i // 4, i % 4
            sp = s % 2
            av = self.ln_tile_B(xf[pi], stats2[pi], lnb[pi], va_at[i])
            p.wait("tensor", self.sV, av)
            if tt == 0 and s >= 2:
                p.wait("tensor", self.sSt, st_done[s - 2])
            self.transpose_tile_to_staging(lnb[pi], staging[sp], tt, av)
            tr_done[i] = (self.sV.n, self.sA.n)

        def emit_strip_out(s):
            sp = s % 2
            vstore, astore = self.sV.n, self.sA.n
            piece = dt[f"lnxp{s}"]
            for ch in range(32):
                p.dma("scalar", piece[128 * ch:128 * (ch + 1), :],
                      staging[sp][:, 512 * ch:512 * (ch + 1)], self.sSt,
                      wait=[(self.sV, vstore), (self.sA, astore)] if ch == 0 else None)
            st_done[s] = self.sSt.n
            p.wait("gpsimd", self.sSt, st_done[s])
            cc = p.op("gpsimd", lambda e, s=s: e.collective_compute(
                "AllGather", ALU.bypass, replica_groups=[list(range(NCORES))],
                ins=[dt[f"lnxp{s}"].ap().opt()],
                outs=[dt[f"LNXALLp{s}"].ap().opt()]), inc=self.sCC)
            p.wait("sync", self.sCC, cc)
            def dyncopy(e, s=s):
                return e.dma_start(out=dt[f"LNXPAIRp{s}"][:, :],
                                   in_=dt[f"LNXALLp{s}"][bass.ds(self._pbD, 2 * D), :])
            pr = p.op("sync", dyncopy, inc=self.sLd, k=16)
            self.lnx_pair_ready[s] = pr
            self.lnx_all_ready[s] = cc

        for i in range(16):
            pi = i % 2
            wl = []
            if i >= 2:
                wl.append((self.sV, tr_done[i - 2][0]))
                wl.append((self.sA, tr_done[i - 2][1]))
            p.dma("sync", xf[pi][:, :], src[128 * i:128 * (i + 1), :],
                  self.sLd, wait=wl or None)
            ld = self.sLd.n
            if i >= 1:
                emit_B(i - 1)
                if (i - 1) % 4 == 3:
                    emit_strip_out((i - 1) // 4)
            p.wait("vector", self.sLd, ld)
            p.wait("scalar", self.sLd, ld)
            va_at[i] = self.ln_tile_A(xf[pi], stats2[pi], junk)
        emit_B(15)
        emit_strip_out(3)
        self.psum_close()
        return self.sSt.n

    def exchange_lnx(self):
        self.fence()
        p, dt = self.p, self.dt
        cc = p.op("gpsimd", lambda e: e.collective_compute(
            "AllGather", ALU.bypass, replica_groups=[list(range(NCORES))],
            ins=[dt["lnxfm"].ap().opt()], outs=[dt["LNXALL"].ap().opt()]),
            inc=self.sCC)
        p.wait("sync", self.sCC, cc)
        def dyncopy(e):
            rv = e.partition_id()
            pairbase = e.scalar_reg_alu(ALU.bitwise_and, rv, 0xFFFFFFFE)
            off = e.scalar_reg_alu(ALU.mult, pairbase, D)
            return e.dma_start(out=dt["LNXPAIR"][:, :],
                               in_=dt["LNXALL"][bass.ds(off, 2 * D), :])
        p.op("sync", dyncopy, inc=self.sLd, k=16)
        return self.sLd.n

    _psum_uid = 0

    def psum_open(self, specs):
        self._psum_cms = []
        out = []
        for name, shape, dtype in specs:
            Builder._psum_uid += 1
            cm = self.nc.psum_tensor(f"{name}_u{Builder._psum_uid}", shape, dtype)
            out.append(cm.__enter__())
            self._psum_cms.append(cm)
        return out

    def psum_close(self):
        for cm in reversed(self._psum_cms):
            cm.__exit__(None, None, None)
        self._psum_cms = []

    def qproj(self, wq_b, qtm, wname=None):
        self.fence()
        p, dt = self.p, self.dt
        if wname:
            self.wait_wcast("scalar", wname)
        ab = self.aB
        stg = ab[:, 0:16384]
        staging = [stg, stg]
        wt = [ab[:, 16384 + 16384 * i: 16384 + 16384 * (i + 1)] for i in range(2)]
        qev = [ab[:, 49152 + 512 * i: 49152 + 512 * (i + 1)] for i in range(2)]
        (pq0, pq1) = self.psum_open([("pq0", [128, 512], F32), ("pq1", [128, 512], F32)])
        mm_s = {}
        mm_fg = {}
        stage_at = {}
        for s in range(NSTRIP):
            sp = 0
            if s >= 1:
                p.wait("sync", self.sPE, mm_s.get(s - 1, 0))
            pair = dt[f"LNXPAIRp{s // 2}"]
            coff = 256 * (s % 2)
            p.wait("sync", self.sLd, self.lnx_pair_ready[s // 2])
            for ch in range(32):
                for r in range(2):
                    p.dma("sync", staging[sp][:, 512 * ch + 256 * r:
                                              512 * ch + 256 * (r + 1)],
                          pair[4096 * r + 128 * ch: 4096 * r + 128 * (ch + 1),
                               coff: coff + 256], self.sLd)
            stage_at[s] = self.sLd.n
            for fg in range(4):
                gk = 4 * s + fg
                p.wait("scalar", self.sPE, mm_fg.get(gk - 2, 0))
                for ch in range(32):
                    p.dma("scalar", wt[fg % 2][:, 512 * ch: 512 * (ch + 1)],
                          wq_b[128 * ch: 128 * (ch + 1), 512 * fg: 512 * (fg + 1)],
                          self.sLd2)
                wdone = self.sLd2.n
                p.wait("tensor", self.sLd, stage_at[s])
                p.wait("tensor", self.sLd2, wdone)
                for tt in range(4):
                    pq = (pq0, pq1)[tt % 2]
                    p.wait("tensor", self.sA, self.sA.n)
                    for ch in range(32):
                        p.op("tensor", lambda e, ch=ch, tt=tt, pq=pq, fg=fg, sp=sp:
                             e.matmul(
                                 pq[:, :],
                                 staging[sp][:, 512 * ch + 128 * tt:
                                             512 * ch + 128 * (tt + 1)],
                                 wt[fg % 2][:, 512 * ch: 512 * (ch + 1)],
                                 start=(ch == 0), stop=(ch == 31),
                                 skip_group_check=True),
                             inc=self.sPE if ch == 31 else None)
                    mmv = self.sPE.n
                    p.wait("scalar", self.sPE, mmv)
                    p.wait("scalar", self.sSt, self.sSt.n)
                    ev = p.op("scalar", lambda e, tt=tt, pq=pq: e.activation(
                        qev[tt % 2], pq[:, :], AF.Copy, scale=QSCALE), inc=self.sA)
                    p.dma("sync", qtm[512 * s + 128 * tt: 512 * s + 128 * (tt + 1),
                                      512 * fg: 512 * (fg + 1)],
                          qev[tt % 2], self.sSt, wait=[(self.sA, ev)])
                mm_fg[gk] = self.sPE.n
            mm_s[s] = self.sPE.n
        self.psum_close()

    def kvproj(self, wkv_b, pieces, ranks, outK, outV, wname=None):
        self.fence()
        p = self.p
        if wname:
            self.wait_wcast("scalar", wname)
        for k in range(4):
            p.wait("sync", self.sCC, self.lnx_all_ready[k])
            p.wait("sync", self.sLd, self.lnx_pair_ready[k])
        ab = self.aB
        page = [ab[:, 0:2048], ab[:, 2048:4096]]
        wkv = ab[:, 4096:4096 + 32 * 256]
        kev = [ab[:, 12288 + 512 * i:12288 + 512 * (i + 1)] for i in range(2)]
        vev = [ab[:, 13312 + 512 * i:13312 + 512 * (i + 1)] for i in range(2)]
        ps = self.psum_open([(f"pkv{i}", [128, 512], F32) for i in range(8)])
        for ch in range(32):
            p.dma("scalar", wkv[:, 256 * ch: 256 * (ch + 1)],
                  wkv_b[128 * ch: 128 * (ch + 1), :], self.sLd2)
        wv = self.sLd2.n
        p.wait("tensor", self.sLd2, wv)
        for ri, r in enumerate(ranks):
            mm_at = {}
            for ch in range(32):
                pi = ch % 2
                wl = []
                if ch >= 2:
                    wl.append((self.sPE, mm_at[ch - 2]))
                elif ri > 0:
                    wl.append((self.sPE, self.sPE.n))
                for k in range(4):
                    p.dma("sync", page[pi][:, 512 * k:512 * (k + 1)],
                          pieces[k][4096 * r + 128 * ch: 4096 * r + 128 * (ch + 1), :],
                          self.sLd, wait=(wl or None) if k == 0 else None)
                p.wait("tensor", self.sLd, self.sLd.n)
                for k in range(4):
                    for fc in range(2):
                        p.op("tensor", lambda e, ch=ch, k=k, fc=fc, pi=pi: e.matmul(
                            ps[fc * 4 + k][:, :],
                            wkv[:, 256 * ch + 128 * fc: 256 * ch + 128 * (fc + 1)],
                            page[pi][:, 512 * k: 512 * (k + 1)],
                            start=(ch == 0), stop=(ch == 31)),
                            inc=self.sPE if (k == 3 and fc == 1) else None)
                mm_at[ch] = self.sPE.n
            mmv = self.sPE.n
            p.wait("scalar", self.sPE, mmv)
            a = r % 2
            for k in range(4):
                for fc in range(2):
                    ev = (kev, vev)[fc][k % 2]
                    dst = (outK, outV)[fc]
                    p.wait("scalar", self.sSt, self.sSt.n)
                    e1 = p.op("scalar", lambda e, k=k, fc=fc, ev=ev: e.activation(
                        ev, ps[fc * 4 + k][:, :], AF.Copy), inc=self.sA)
                    for h2 in range(2):
                        col = 1024 * k + 512 * h2 + 256 * a
                        p.dma("sync", dst[:, col:col + 256],
                              ev[:, 256 * h2: 256 * (h2 + 1)], self.sSt,
                              wait=[(self.sA, e1)] if h2 == 0 else None)
        self.psum_close()

    def attention(self, layer, cast_pend=None):
        self.fence()
        self.ctx_ready = {}
        p, dt = self.p, self.dt
        p.wait("tensor", self.sCast, self.cast_consts)
        p.wait("vector", self.sCast, self.cast_consts)
        ab, af = self.aB, self.aF
        qt = [ab[:, 512 * i:512 * (i + 1)] for i in range(3)]
        ptile = [ab[:, 1536 + 512 * i:1536 + 512 * (i + 1)] for i in range(4)]
        bset = [[ab[:, 3584 + 512 * (5 * pp + k): 3584 + 512 * (5 * pp + k + 1)]
                 for k in range(5)] for pp in range(2)]
        ksb = [ab[:, 8704 + 4096 * i: 8704 + 4096 * (i + 1)] for i in range(2)]
        vsb = [ab[:, 16896 + 4096 * i: 16896 + 4096 * (i + 1)] for i in range(2)]
        ctev = [ab[:, 25088 + 512 * i: 25088 + 512 * (i + 1)] for i in range(2)]
        bscr = [af[:, 512 * i:512 * (i + 1)] for i in range(2)]
        rrow = af[:, 1024:1536]
        rb = [af[:, 1536 + 512 * i:1536 + 512 * (i + 1)] for i in range(2)]
        pq0, pq1, pq2, pq3, pc0, pc1, psb0, psb1 = self.psum_open(
            [("pqA", [128, 512], F32), ("pqB", [128, 512], F32),
             ("pqC", [128, 512], F32), ("pqD", [128, 512], F32),
             ("pcA", [128, 512], F32), ("pcB", [128, 512], F32),
             ("psA", [128, 512], F32), ("psB", [128, 512], F32)])
        pqk = (pq0, pq1, pq2, pq3)
        pctx = (pc0, pc1)
        psumb = (psb0, psb1)
        kvready = 0
        if layer == 0:
            p.dma("sync", self.cF[:, 532:532 + 4096], dt["biastab"][:, :], self.sLd)
            p.dma("sync", ksb[0][:, :], dt["kvK0"][:, :], self.sLd)
            for kt in range(32):
                p.op("sync", lambda e, kt=kt: e.dma_start(
                    out=vsb[0][:, 128 * kt: 128 * (kt + 1)],
                    in_=dt["kvV0"][:, 128 * kt: 128 * (kt + 1)], transpose=True),
                    inc=self.sLd, k=16)
            kvready = self.sLd.n

        qk_last = {}
        ctx_mul = {}
        recip_c = {}
        kv_pi = 0
        oi = 0
        LEAD = 3
        for hl in range(16):
            pp = hl % 2
            if layer == 0:
                bs = bset[pp]
                p.wait("vector", self.sA, self.sA.n)
                bv = p.op("vector", lambda e, hl=hl, pp=pp: e.tensor_scalar(
                    bscr[pp], self.dm, self.slopes_sb[:, hl:hl + 1], None, ALU.mult),
                    inc=self.sV)
                p.wait("scalar", self.sV, bv)
                av = p.op("scalar", lambda e, bs=bs, pp=pp: e.activation(
                    bs[0], bscr[pp], AF.Exp), inc=self.sA)
                for r in range(4):
                    p.wait("vector", self.sA, av)
                    bv = p.op("vector", lambda e, r=r, hl=hl, pp=pp: e.tensor_scalar(
                        bscr[pp], self.dm, float(127 + 128 * r),
                        self.slopes_sb[:, hl:hl + 1], ALU.add, ALU.mult), inc=self.sV)
                    p.wait("vector", self.sV, bv)
                    bv = p.op("vector", lambda e, pp=pp: e.tensor_scalar(
                        bscr[pp], bscr[pp], 0.0, None, ALU.min), inc=self.sV)
                    p.wait("scalar", self.sV, bv)
                    av = p.op("scalar", lambda e, r=r, bs=bs, pp=pp: e.activation(
                        bs[1 + r], bscr[pp], AF.Exp), inc=self.sA)
                    p.wait("vector", self.sA, av)
                    p.op("vector", lambda e, r=r, bs=bs: e.tensor_tensor(
                        bs[1 + r], bs[1 + r], self.diagmask_b(r), ALU.mult),
                        inc=self.sV)
                bready = (self.sV.n, self.sA.n)
            if layer == 1 and hl % 4 == 0:
                bi = hl // 4
                kv_pi = bi % 2
                wl = [(self.sPE, self.sPE.n)] if bi >= 2 else None
                p.dma("sync", ksb[kv_pi][:, :],
                      dt["kv1K"][128 * bi:128 * (bi + 1), :], self.sLd, wait=wl)
                for kt in range(32):
                    p.op("sync", lambda e, kt=kt, bi=bi, pi=kv_pi: e.dma_start(
                        out=vsb[pi][:, 128 * kt: 128 * (kt + 1)],
                        in_=dt["kv1V"][128 * bi: 128 * bi + 128,
                                       128 * kt: 128 * (kt + 1)], transpose=True),
                        inc=self.sLd, k=16)
                kvready = self.sLd.n
            for s in range(NSTRIP):
                E = 4 * (s + 1)
                qi = oi % 3
                it2 = oi % 2
                if oi >= 3:
                    p.wait("sync", self.sPE, qk_last[oi - 3])
                p.op("sync", lambda e, s=s, hl=hl, qi=qi: e.dma_start(
                    out=qt[qi][:, :],
                    in_=dt["qtm"][512 * s: 512 * (s + 1), 128 * hl: 128 * (hl + 1)],
                    transpose=True), inc=self.sLd, k=16)
                qready = self.sLd.n
                p.wait("tensor", self.sLd, max(qready, kvready))
                if layer == 0:
                    p.wait("tensor", self.sV, bready[0])
                exp_done = {}
                sum_done = {}
                pready = {}
                war_ctx = ctx_mul.get(oi - 2, 0)
                war_sum = recip_c.get(oi - 2, 0)

                def emit_pv(kt):
                    bank = kt % 4
                    prs, prv = pready[kt]
                    p.wait("tensor", prs, prv)
                    if kt == 0:
                        p.wait("tensor", self.sV, war_ctx)
                        p.wait("tensor", self.sA, war_sum)
                    p.op("tensor", lambda e, kt=kt, bank=bank, pi=kv_pi, E=E,
                         it2=it2: e.matmul(
                             pctx[it2][:, :], vsb[pi][:, 128 * kt: 128 * (kt + 1)],
                             ptile[bank][:, :], start=(kt == 0), stop=(kt == E - 1),
                             skip_group_check=True),
                         inc=self.sPE if kt == E - 1 else None)
                    sm = p.op("tensor", lambda e, kt=kt, bank=bank, E=E, it2=it2:
                              e.matmul(psumb[it2][:, :], self.ones_full_b,
                                       ptile[bank][:, :], start=(kt == 0),
                                       stop=(kt == E - 1), skip_group_check=True),
                              inc=self.sPE)
                    sum_done[kt] = sm

                for kt in range(E):
                    bank = kt % 4
                    if kt >= 4:
                        p.wait("tensor", self.sA, exp_done[kt - 4])
                    qkv = p.op("tensor", lambda e, kt=kt, bank=bank, qi=qi,
                               pi=kv_pi: e.matmul(
                                   pqk[bank][:, :],
                                   ksb[pi][:, 128 * kt: 128 * (kt + 1)],
                                   qt[qi][:, :], start=True, stop=True,
                                   skip_group_check=True), inc=self.sPE)
                    p.wait("scalar", self.sPE, qkv)
                    if kt >= 4:
                        p.wait("scalar", self.sPE, sum_done.get(kt - 4, 0))
                    r = kt - 4 * s
                    if layer == 0 and r < 0:
                        bias_ap = self.biastab_sb[:, hl * 256 + s * 32 + kt:
                                                  hl * 256 + s * 32 + kt + 1]
                    else:
                        bias_ap = self.consts_sb[:, 0:1]
                    ev = p.op("scalar", lambda e, bank=bank, bias_ap=bias_ap:
                              e.activation(ptile[bank], pqk[bank][:, :], AF.Exp,
                                           bias=bias_ap), inc=self.sA)
                    exp_done[kt] = ev
                    if layer == 0:
                        btile = bset[pp][1 + r] if r >= 0 else bset[pp][0]
                        p.wait("vector", self.sA, ev)
                        mv = p.op("vector", lambda e, bank=bank, bt=btile:
                                  e.tensor_tensor(ptile[bank], ptile[bank], bt,
                                                  ALU.mult), inc=self.sV)
                        pready[kt] = (self.sV, mv)
                    elif r >= 0:
                        p.wait("vector", self.sA, ev)
                        mv = p.op("vector", lambda e, bank=bank, r=r:
                                  e.tensor_tensor(ptile[bank], ptile[bank],
                                                  self.diagmask_b(r), ALU.mult),
                                  inc=self.sV)
                        pready[kt] = (self.sV, mv)
                    else:
                        pready[kt] = (self.sA, ev)
                    if kt >= LEAD:
                        emit_pv(kt - LEAD)
                for kt in range(max(0, E - LEAD), E):
                    emit_pv(kt)
                sm = sum_done[E - 1]
                qk_last[oi] = sm
                p.wait("scalar", self.sPE, sm)
                p.wait("scalar", self.sV, ctx_mul.get(oi - 2, 0))
                l1 = p.op("scalar", lambda e, it2=it2: e.activation(
                    rrow, psumb[it2][:, :], AF.Ln), inc=self.sA)
                p.wait("scalar", self.sA, l1)
                rc = p.op("scalar", lambda e, it2=it2: e.activation(
                    rb[it2], rrow, AF.Exp, scale=-1.0), inc=self.sA)
                recip_c[oi] = rc
                p.wait("vector", self.sA, rc)
                p.wait("vector", self.sSt, self.sSt.n)
                mu = p.op("vector", lambda e, it2=it2: e.tensor_tensor(
                    ctev[it2], pctx[it2][:, :], rb[it2], ALU.mult), inc=self.sV)
                ctx_mul[oi] = mu
                kq = hl // 4
                rowb = ((hl % 4) * 8 + s) * 2 * 128
                piece = dt[f"ctxp{kq}"]
                p.dma("scalar", piece[rowb:rowb + 128, :],
                      ctev[it2][:, 0:256], self.sSt, wait=[(self.sV, mu)])
                p.dma("scalar", piece[rowb + 128:rowb + 256, :],
                      ctev[it2][:, 256:512], self.sSt)
                oi += 1
            if hl % 4 == 3:
                kq = hl // 4
                p.wait("gpsimd", self.sSt, self.sSt.n)
                cc = p.op("gpsimd", lambda e, kq=kq: e.collective_compute(
                    "AllGather", ALU.bypass, replica_groups=[list(range(NCORES))],
                    ins=[dt[f"ctxp{kq}"].ap().opt()],
                    outs=[dt[f"CTXALLp{kq}"].ap().opt()]), inc=self.sCC)
                p.wait("sync", self.sCC, cc)
                G4 = 4 * NSTRIP
                def dyncopy(e, kq=kq, G4=G4):
                    srcv = dt[f"CTXALLp{kq}"].ap().rearrange(
                        "(r g h p) c -> r g h p c", r=NCORES, g=G4, h=2, p=128)
                    dstv = dt[f"CTXSELFp{kq}"].ap().rearrange(
                        "(r g p) c -> r g p c", r=2, g=G4, p=128)
                    return e.dma_start(
                        out=dstv, in_=srcv[bass.ds(self._pb, 2), :,
                                           bass.ds(self._aa, 1), :, :])
                pr = p.op("sync", dyncopy, inc=self.sLd, k=16)
                self.ctx_ready[hl // 4] = pr
            if cast_pend:
                k0 = (len(cast_pend) * hl) // 16
                k1 = (len(cast_pend) * (hl + 1)) // 16
                if k1 > k0:
                    p.wait("gpsimd", self.sV, self.sV.n)
                    for chk in cast_pend[k0:k1]:
                        self.emit_cast_chunk(chk)
        self.psum_close()

    def exchange_ctx(self):
        self.fence()
        p, dt = self.p, self.dt
        cc = p.op("gpsimd", lambda e: e.collective_compute(
            "AllGather", ALU.bypass, replica_groups=[list(range(NCORES))],
            ins=[dt["ctxm"].ap().opt()], outs=[dt["CTXALL"].ap().opt()]),
            inc=self.sCC)
        p.wait("sync", self.sCC, cc)
        G = NSTRIP * 16
        def dyncopy(e):
            rv = e.partition_id()
            pb = e.snap(e.scalar_reg_alu(ALU.bitwise_and, rv, 0xFFFFFFFE),
                        min_val=0, max_val=6)
            aa = e.snap(e.scalar_reg_alu(ALU.bitwise_and, rv, 1),
                        min_val=0, max_val=1)
            srcv = dt["CTXALL"].ap().rearrange(
                "(r g h p) c -> r g h p c", r=NCORES, g=G, h=2, p=128)
            dstv = dt["CTXSELF"].ap().rearrange(
                "(r g p) c -> r g p c", r=2, g=G, p=128)
            return e.dma_start(out=dstv,
                               in_=srcv[bass.ds(pb, 2), :, bass.ds(aa, 1), :, :])
        p.op("sync", dyncopy, inc=self.sLd, k=16)
        return self.sLd.n

    def oproj(self, layer, wo_b, x_old, x_new, wname=None):
        self._o_mm = {}
        self.fence()
        for k in range(4):
            self.p.wait("sync", self.sLd, self.ctx_ready[k])
        p, dt = self.p, self.dt
        if wname:
            self.wait_wcast("scalar", wname)
        if layer == 1:
            self.wait_wcast("scalar", "bo_b")
        ab, af = self.aB, self.aF
        wo_sb = [ab[:, 16384 * i: 16384 * (i + 1)] for i in range(2)]
        ctx_sb = [ab[:, 32768 + 8192 * i: 32768 + 8192 * (i + 1)] for i in range(2)]
        xo = [af[:, 512 * i: 512 * (i + 1)] for i in range(2)]
        xn = [af[:, 1024 + 512 * i: 1024 + 512 * (i + 1)] for i in range(2)]
        po = self.psum_open([("po0", [128, 512], F32), ("po1", [128, 512], F32)])
        G = NSTRIP * 16
        boT = [ab[0:1, 49152 + 512 * i: 49152 + 512 * (i + 1)] for i in range(2)]
        self._bod = 0
        ei = 0
        for op2 in range(4):
            p.wait("scalar", self.sPE, self.sPE.n)
            for osl in range(2):
                os = 2 * op2 + osl
                for dc in range(32):
                    p.dma("scalar", wo_sb[osl][:, 512 * dc: 512 * (dc + 1)],
                          wo_b[128 * dc: 128 * (dc + 1), 512 * os: 512 * (os + 1)],
                          self.sLd2)
                if layer == 1:
                    p.dma("scalar", boT[osl],
                          self.wb["bo_b"][0:1, 512 * os: 512 * (os + 1)], self.sLd2)
            wod = self.sLd2.n
            self._bod = wod
            for s in range(NSTRIP):
                ci = s % 2
                if s >= 2:
                    p.wait("sync", self.sPE, self._o_mm.get((op2, s - 2), 0))
                elif op2 > 0:
                    p.wait("sync", self.sPE, self._o_mm.get((op2 - 1, NSTRIP - 2), 0))
                for rank in range(2):
                    for hl in range(16):
                        kq, hq = hl // 4, hl % 4
                        rowb = (rank * 32 + hq * 8 + s) * 128
                        p.dma("sync",
                              ctx_sb[ci][:, (rank * 16 + hl) * 256:
                                         (rank * 16 + hl) * 256 + 256],
                              dt[f"CTXSELFp{kq}"][rowb:rowb + 128, :], self.sLd)
                ctxd = self.sLd.n
                for qsub in range(2):
                    rb = 2 * s + qsub
                    for osl in range(2):
                        os = 2 * op2 + osl
                        bank = po[ei % 2]
                        p.wait("tensor", self.sLd, ctxd)
                        p.wait("tensor", self.sLd2, wod)
                        p.wait("tensor", self.sV, self.sV.n)
                        for dc in range(32):
                            last = (dc == 31 and layer == 0)
                            p.op("tensor", lambda e, dc=dc, qsub=qsub, osl=osl, s=s,
                                 bank=bank, last=last: e.matmul(
                                     bank[:, :],
                                     ctx_sb[s % 2][:, 256 * dc + 128 * qsub:
                                                   256 * dc + 128 * (qsub + 1)],
                                     wo_sb[osl][:, 512 * dc: 512 * (dc + 1)],
                                     start=(dc == 0), stop=last),
                                 inc=self.sPE if last else None)
                        if layer == 1:
                            p.wait("tensor", self.sLd2, self._bod)
                            p.op("tensor", lambda e, bank=bank, osl=osl: e.matmul(
                                bank[:, :], self.ones_row_b, boT[osl],
                                start=False, stop=True, skip_group_check=True),
                                inc=self.sPE)
                        mmv = self.sPE.n
                        xi = ei % 2
                        p.dma("sync", xo[xi],
                              x_old[128 * rb: 128 * (rb + 1), 512 * os: 512 * (os + 1)],
                              self.sLd, wait=[(self.sV, self.sV.n)])
                        xod = self.sLd.n
                        p.wait("vector", self.sPE, mmv)
                        p.wait("vector", self.sLd, xod)
                        p.wait("vector", self.sSt, self.sSt.n)
                        mv = p.op("vector", lambda e, bank=bank, xi=xi: e.tensor_tensor(
                            xn[xi], bank[:, :], xo[xi], ALU.add), inc=self.sV)
                        p.dma("sync", x_new[128 * rb: 128 * (rb + 1),
                                            512 * os: 512 * (os + 1)],
                              xn[xi], self.sSt, wait=[(self.sV, mv)])
                        ei += 1
                self._o_mm[(op2, s)] = self.sPE.n
        self.psum_close()

    def ffn0(self, x_old, x_new):
        import os as _os
        LVL = int(_os.environ.get("FFN0_LVL", "7"))
        self.fence()
        p, dt = self.p, self.dt
        self.wait_wcast("scalar", "awT", "bwT")
        self.wait_wcast("gpsimd", "expmat")
        ab, af = self.aB, self.aF
        staging = ab[:, 0:16384]
        aw_sb = ab[:, 16384:32768]
        bw_sb = ab[:, 32768:49152]
        z_bf = [ab[:, 49152 + 512 * i: 49152 + 512 * (i + 1)] for i in range(4)]
        zg_bf = [ab[:, 51200 + 512 * i: 51200 + 512 * (i + 1)] for i in range(4)]
        emat_sb = self.cB[0:64, 4737:5249]
        gate_fm_t = self.cB[0:64, 5249:5761]
        xf = [af[:, 0:4096], af[:, 4096:8192]]
        xo = [af[:, 512 * i: 512 * (i + 1)] for i in range(2)]
        xn = [af[:, 1024 + 512 * i: 1536 + 512 * i] for i in range(2)]
        lnf = af[:, 8192:12288]
        stats2 = [af[:, 12288:12296], af[:, 13864:13872]]
        fm32 = af[:, 12296:12808]
        qr_sb = af[0:16, 12808:13320]
        s0 = af[:, 13320:13576]
        scr = af[:, 13576:13832]
        mrow = af[:, 13832:13864]
        junk = lnf
        ps = self.psum_open([(f"pf{i}", [128, 512], F32) for i in range(8)])
        wq_sb = self.cF[:, 532:1044]
        keys_sb = self.cF[0:16, 1044:1108]
        for ch in range(32):
            p.dma("sync", wq_sb[:, 16 * ch: 16 * (ch + 1)],
                  dt["wqwT"][128 * ch: 128 * (ch + 1), :], self.sLd)
        p.dma("sync", keys_sb, dt["keysT"][:, :], self.sLd)
        p.dma("gpsimd", emat_sb, dt["expmat"][:, :], self.sCast)
        for ch in range(32):
            p.dma("scalar", aw_sb[:, 512 * ch: 512 * (ch + 1)],
                  self.wb["awT"][128 * ch: 128 * (ch + 1), :], self.sLd2)
        for nc4 in range(4):
            p.dma("scalar", bw_sb[:, 4096 * nc4: 4096 * (nc4 + 1)],
                  self.wb["bwT"][128 * nc4: 128 * (nc4 + 1), :], self.sLd2)
        wd = (self.sLd.n, self.sLd2.n, self.sCast.n)
        blk = lambda t, b: t[:, 64 * b:64 * (b + 1)]
        for g in range(4):
            qr_bank, sc_bank, tr_bank = ps[0], ps[1], ps[2]
            for tt in range(4):
                i = 4 * g + tt
                pi = i % 2
                p.dma("sync", xf[pi][:, :], x_old[128 * i:128 * (i + 1), :],
                      self.sLd, wait=[(self.sPE, self.sPE.n), (self.sV, self.sV.n),
                                      (self.sA, self.sA.n)])
                ld = self.sLd.n
                p.wait("vector", self.sLd, ld)
                p.wait("scalar", self.sLd, ld)
                av = self.ln_tile(xf[pi], stats2[pi], xf[1 - pi], lnf)
                p.wait("tensor", self.sV, av)
                NOTR = int(_os.environ.get("FFN0_NOTR", "0"))
                for grp in range(8):
                    if NOTR >= 2:
                        break
                    pe_l = None
                    for j in range(4):
                        ch = grp * 4 + j
                        pe_l = p.op("tensor", lambda e, ch=ch, j=j: e.transpose(
                            tr_bank[:, 128 * j:128 * (j + 1)],
                            lnf[:, 128 * ch:128 * (ch + 1)], self.idF[:, :]),
                            inc=self.sPE)
                    p.wait("vector", self.sPE, pe_l)
                    p.wait("vector", self.sA, self.sA.n)
                    for j in range(4):
                        p.op("vector", lambda e, j=j: e.tensor_copy(
                            fm32[:, 128 * j:128 * (j + 1)],
                            tr_bank[:, 128 * j:128 * (j + 1)]), inc=self.sV)
                    vv = self.sV.n
                    p.wait("scalar", self.sV, vv)
                    for j in range(4):
                        ch = grp * 4 + j
                        p.op("scalar", lambda e, ch=ch, j=j, tt=tt: e.activation(
                            staging[:, 512 * ch + 128 * tt: 512 * ch + 128 * (tt + 1)],
                            fm32[:, 128 * j:128 * (j + 1)], AF.Copy), inc=self.sA)
                    p.wait("tensor", self.sV, vv)
                    p.wait("tensor", self.sLd, wd[0])
                    for j in range(4):
                        if NOTR >= 1:
                            break
                        ch = grp * 4 + j
                        p.op("tensor", lambda e, ch=ch, j=j, tt=tt, qr_bank=qr_bank:
                             e.matmul(qr_bank[0:16, 128 * tt:128 * (tt + 1)],
                                      wq_sb[:, 16 * ch:16 * (ch + 1)],
                                      fm32[:, 128 * j:128 * (j + 1)],
                                      start=(ch == 0), stop=(ch == 31)),
                             inc=self.sPE if ch == 31 else None)
                    p.wait("tensor", self.sA, self.sA.n)
                    p.wait("tensor", self.sV, self.sV.n)
            if LVL <= 2:
                continue
            qv = self.sPE.n
            p.wait("vector", self.sPE, qv)
            ev = p.op("vector", lambda e, qr_bank=qr_bank: e.tensor_copy(
                qr_sb, qr_bank[0:16, :]), inc=self.sV)
            p.wait("tensor", self.sV, ev)
            for tt in range(4):
                p.op("tensor", lambda e, tt=tt, sc_bank=sc_bank: e.matmul(
                    sc_bank[:, 64 * tt: 64 * (tt + 1)],
                    qr_sb[:, 128 * tt:128 * (tt + 1)], keys_sb,
                    start=True, stop=True), inc=self.sPE)
            sv = self.sPE.n
            p.wait("vector", self.sPE, sv)
            sev = p.op("vector", lambda e, sc_bank=sc_bank: e.tensor_copy(
                s0, sc_bank[:, 0:256]), inc=self.sV)
            p.wait("vector", self.sV, sev)
            cp = p.op("vector", lambda e: e.tensor_copy(scr, s0), inc=self.sV)
            p.wait("vector", self.sV, cp)
            if LVL <= 3:
                continue
            for it in range(TOPK):
                for b in range(4):
                    p.op("vector", lambda e, b=b: e.tensor_reduce(
                        mrow[:, 8 + b:9 + b], blk(scr, b), AX.X, ALU.max), inc=self.sV)
                p.wait("vector", self.sV, self.sV.n)
                if it == 0:
                    mx = p.op("vector", lambda e: e.tensor_copy(
                        mrow[:, 0:4], mrow[:, 8:12]), inc=self.sV)
                    p.wait("vector", self.sV, mx)
                if it == TOPK - 1:
                    tv = p.op("vector", lambda e: e.tensor_copy(
                        mrow[:, 4:8], mrow[:, 8:12]), inc=self.sV)
                    p.wait("vector", self.sV, tv)
                    break
                for b in range(4):
                    p.op("vector", lambda e, b=b: e.tensor_scalar(
                        blk(junk, b), blk(scr, b), mrow[:, 8 + b:9 + b], NEG,
                        ALU.is_equal, ALU.mult), inc=self.sV)
                p.wait("vector", self.sV, self.sV.n)
                for b in range(4):
                    p.op("vector", lambda e, b=b: e.tensor_tensor(
                        blk(scr, b), blk(scr, b), blk(junk, b), ALU.add), inc=self.sV)
                p.wait("vector", self.sV, self.sV.n)
            if LVL <= 4:
                continue
            ng = p.op("vector", lambda e: e.tensor_scalar(
                mrow[:, 12:16], mrow[:, 0:4], -1.0, None, ALU.mult), inc=self.sV)
            p.wait("scalar", self.sV, ng)
            gate_f = scr
            for b in range(4):
                p.op("scalar", lambda e, b=b: e.activation(
                    blk(gate_f, b), blk(s0, b), AF.Exp,
                    bias=mrow[:, 12 + b:13 + b]), inc=self.sA)
            p.wait("vector", self.sA, self.sA.n)
            for b in range(4):
                p.op("vector", lambda e, b=b: e.tensor_scalar(
                    blk(junk, b), blk(s0, b), mrow[:, 4 + b:5 + b], None,
                    ALU.is_ge), inc=self.sV)
            p.wait("vector", self.sV, self.sV.n)
            for b in range(4):
                p.op("vector", lambda e, b=b: e.tensor_tensor(
                    blk(gate_f, b), blk(gate_f, b), blk(junk, b), ALU.mult),
                    inc=self.sV)
            p.wait("vector", self.sV, self.sV.n)
            for b in range(4):
                p.op("vector", lambda e, b=b: e.tensor_reduce(
                    mrow[:, 16 + b:17 + b], blk(gate_f, b), AX.X, ALU.add), inc=self.sV)
            p.wait("vector", self.sV, self.sV.n)
            rv = p.op("vector", lambda e: e.reciprocal(mrow[:, 20:24], mrow[:, 16:20]),
                      inc=self.sV)
            p.wait("vector", self.sV, rv)
            for b in range(4):
                p.op("vector", lambda e, b=b: e.tensor_scalar(
                    blk(junk, b), blk(gate_f, b), mrow[:, 20 + b:21 + b],
                    MOE_SCALING, ALU.mult, ALU.mult), inc=self.sV)
            gv = self.sV.n
            p.wait("tensor", self.sV, gv)
            for tt in range(4):
                p.op("tensor", lambda e, tt=tt: e.transpose(
                    tr_bank[0:64, 128 * tt:128 * (tt + 1)],
                    junk[:, 64 * tt: 64 * (tt + 1)], self.idF[:, :]),
                    inc=self.sPE)
            gtv = self.sPE.n
            p.wait("vector", self.sPE, gtv)
            gfm = p.op("vector", lambda e: e.tensor_copy(
                gate_fm_t, tr_bank[0:64, 0:512]), inc=self.sV)
            if LVL <= 5:
                continue
            p.wait("tensor", self.sV, gfm)
            p.wait("tensor", self.sLd2, wd[1])
            p.wait("tensor", self.sCast, wd[2])
            for nc4 in range(4):
                for ch in range(32):
                    p.op("tensor", lambda e, nc4=nc4, ch=ch: e.matmul(
                        ps[4 + nc4][:, :],
                        aw_sb[:, 512 * ch + 128 * nc4: 512 * ch + 128 * (nc4 + 1)],
                        staging[:, 512 * ch: 512 * (ch + 1)],
                        start=(ch == 0), stop=(ch == 31)),
                        inc=self.sPE if ch == 31 else None)
                p.op("tensor", lambda e, nc4=nc4: e.matmul(
                    ps[nc4][:, :], emat_sb[:, 128 * nc4:128 * (nc4 + 1)],
                    gate_fm_t, start=True, stop=True), inc=self.sPE)
            zv = self.sPE.n
            p.wait("scalar", self.sPE, zv)
            p.wait("scalar", self.sV, self.sV.n)
            for nc4 in range(4):
                p.op("scalar", lambda e, nc4=nc4: e.activation(
                    z_bf[nc4], ps[4 + nc4][:, :], AF.Copy), inc=self.sA)
            p.wait("vector", self.sA, self.sA.n)
            for nc4 in range(4):
                p.op("vector", lambda e, nc4=nc4: e.tensor_tensor(
                    zg_bf[nc4], z_bf[nc4], ps[nc4][:, :], ALU.mult), inc=self.sV)
            zgv = self.sV.n
            if LVL <= 6:
                continue
            for ts in range(4):
                rb = 4 * g + ts
                for os in range(8):
                    bank = ps[4 + (os % 2)]
                    p.wait("tensor", self.sV, max(zgv, self.sV.n))
                    for nc4 in range(4):
                        p.op("tensor", lambda e, nc4=nc4, ts=ts, os=os, bank=bank:
                             e.matmul(bank[:, :],
                                      zg_bf[nc4][:, 128 * ts: 128 * (ts + 1)],
                                      bw_sb[:, 4096 * nc4 + 512 * os:
                                            4096 * nc4 + 512 * (os + 1)],
                                      start=(nc4 == 0), stop=(nc4 == 3)),
                             inc=self.sPE if nc4 == 3 else None)
                    mmv = self.sPE.n
                    xi = os % 2
                    p.dma("sync", xo[xi],
                          x_old[128 * rb: 128 * (rb + 1), 512 * os: 512 * (os + 1)],
                          self.sLd, wait=[(self.sV, self.sV.n)])
                    xod = self.sLd.n
                    p.wait("vector", self.sPE, mmv)
                    p.wait("vector", self.sLd, xod)
                    p.wait("vector", self.sSt, self.sSt.n)
                    mv = p.op("vector", lambda e, bank=bank, xi=xi: e.tensor_tensor(
                        xn[xi], bank[:, :], xo[xi], ALU.add), inc=self.sV)
                    p.dma("sync", x_new[128 * rb: 128 * (rb + 1),
                                        512 * os: 512 * (os + 1)],
                          xn[xi], self.sSt, wait=[(self.sV, mv)])
        self.psum_close()

    def ffn1(self, x_old, x_new):
        self.fence()
        p, dt = self.p, self.dt
        self.wait_wcast("scalar", "w1ppT", "w3ppT", "b1ppT", "b3ppT", "w2ppT", "b2T")
        self.wait_wcast("sync", "a13sb", "a2sb")
        ab, af = self.aB, self.aF
        staging = ab[:, 0:16384]
        xfold = ab[:, 16384:20480]
        hfold = ab[:, 20480:31744]
        h4 = [ab[:, 31744 + 512 * i: 31744 + 512 * (i + 1)] for i in range(4)]
        w1t = [ab[:, 33792 + 4096 * i: 33792 + 4096 * (i + 1)] for i in range(2)]
        w3t = [ab[:, 41984 + 4096 * i: 41984 + 4096 * (i + 1)] for i in range(2)]
        a13buf = ab[:, 50176:50304]
        a2sbuf = ab[:, 50304:50480]
        zrow = ab[0:2, 50480:50992]
        zrow3 = ab[0:2, 50992:51504]
        z2row = ab[0:2, 51504:52016]
        b1t = [ab[0:2, 52016 + 512 * i: 52016 + 512 * (i + 1)] for i in range(2)]
        b3t = [ab[0:2, 53040 + 512 * i: 53040 + 512 * (i + 1)] for i in range(2)]
        b2t = [ab[0:2, 54064 + 512 * i: 54064 + 512 * (i + 1)] for i in range(2)]
        w2tiles = [staging[:, 0:11264], ab[:, 33792:45056]]
        xf = [af[:, 0:4096], af[:, 4096:8192]]
        xo = [af[:, 512 * i: 512 * (i + 1)] for i in range(2)]
        xn = [af[:, 1024 + 512 * i: 1536 + 512 * i] for i in range(2)]
        lnf = af[:, 8192:12288]
        stats2 = [af[:, 12288:12296], af[:, 13832:13840]]
        gsc = [af[:, 12296:12808], af[:, 12808:13320]]
        usc = [af[:, 13320:13832], af[:, 8192:8704]]
        ps = self.psum_open([(f"pq{i}", [128, 512], F32) for i in range(6)]
                            + [("pzz", [128, 1024], F32)])
        pz1 = ps[6]
        p.dma("sync", a2sbuf, self.wb["a2sb"][:, :], self.sLd)
        p.dma("sync", a13buf, self.wb["a13sb"][:, :], self.sLd)
        a2ld = self.sLd.n
        for g in range(4):
            for tt in range(4):
                i = 4 * g + tt
                pi = i % 2
                p.dma("sync", xf[pi][:, :], x_old[128 * i:128 * (i + 1), :],
                      self.sLd, wait=[(self.sPE, self.sPE.n), (self.sV, self.sV.n),
                                      (self.sA, self.sA.n)])
                ld = self.sLd.n
                p.wait("vector", self.sLd, ld)
                p.wait("scalar", self.sLd, ld)
                av = self.ln_tile(xf[pi], stats2[pi], xf[1 - pi], lnf)
                p.wait("tensor", self.sV, av)
                a_done = {}
                for grp in range(8):
                    trb = ps[4 + grp % 2]
                    if grp >= 2:
                        p.wait("tensor", self.sA, a_done[grp - 2])
                    else:
                        p.wait("tensor", self.sA, self.sA.n)
                    pe_l = None
                    for j in range(4):
                        ch = grp * 4 + j
                        pe_l = p.op("tensor", lambda e, ch=ch, j=j, trb=trb:
                                    e.transpose(trb[:, 128 * j:128 * (j + 1)],
                                                lnf[:, 128 * ch:128 * (ch + 1)],
                                                self.idF[:, :]), inc=self.sPE)
                    p.wait("scalar", self.sPE, pe_l)
                    for j in range(4):
                        ch = grp * 4 + j
                        p.op("scalar", lambda e, ch=ch, j=j, tt=tt, trb=trb:
                             e.activation(
                                 staging[:, 512 * ch + 128 * tt:
                                         512 * ch + 128 * (tt + 1)],
                                 trb[:, 128 * j:128 * (j + 1)], AF.Copy),
                             inc=self.sA)
                    a_done[grp] = self.sA.n
            stv = self.sA.n
            p.wait("vector", self.sA, stv)
            for step in range(3):
                for cf in range(8):
                    src_blk = staging[:, 512 * (cf + 8 * (step + 1)):
                                      512 * (cf + 8 * (step + 1) + 1)]
                    if step == 0:
                        p.op("vector", lambda e, cf=cf, src_blk=src_blk:
                             e.tensor_tensor(xfold[:, 512 * cf: 512 * (cf + 1)],
                                             staging[:, 512 * cf: 512 * (cf + 1)],
                                             src_blk, ALU.add), inc=self.sV)
                    else:
                        p.op("vector", lambda e, cf=cf, src_blk=src_blk:
                             e.tensor_tensor(xfold[:, 512 * cf: 512 * (cf + 1)],
                                             xfold[:, 512 * cf: 512 * (cf + 1)],
                                             src_blk, ALU.add), inc=self.sV)
                p.wait("vector", self.sV, self.sV.n)
            foldv = self.sV.n
            p.wait("tensor", self.sLd, a2ld)
            p.wait("tensor", self.sA, stv)
            for ch in range(32):
                p.op("tensor", lambda e, ch=ch: e.matmul(
                    pz1[0:2, 0:512], a13buf[:, 4 * ch:4 * ch + 2],
                    staging[:, 512 * ch: 512 * (ch + 1)],
                    start=(ch == 0), stop=(ch == 31), skip_group_check=True),
                    inc=None)
                p.op("tensor", lambda e, ch=ch: e.matmul(
                    pz1[0:2, 512:1024], a13buf[:, 4 * ch + 2:4 * ch + 4],
                    staging[:, 512 * ch: 512 * (ch + 1)],
                    start=(ch == 0), stop=(ch == 31), skip_group_check=True),
                    inc=self.sPE if ch == 31 else None)
            z1v = self.sPE.n
            p.wait("scalar", self.sPE, z1v)
            p.op("scalar", lambda e: e.activation(zrow, pz1[0:2, 0:512], AF.Copy,
                                                  scale=LORA_SC), inc=self.sA)
            p.op("scalar", lambda e: e.activation(zrow3, pz1[0:2, 512:1024], AF.Copy,
                                                  scale=LORA_SC), inc=self.sA)
            zev = self.sA.n
            p.wait("tensor", self.sA, zev)
            first_z2 = True
            pend = None
            hv_at = {}
            guv_at = {}
            aev_at = {}
            for jc in range(22):
                wp = jc % 2
                p.wait("scalar", self.sPE, guv_at.get(jc - 2, 0))
                for kc in range(8):
                    p.dma("scalar", w1t[wp][:, 512 * kc:512 * (kc + 1)],
                          self.wb["w1ppT"][128 * kc:128 * (kc + 1),
                                           512 * jc:512 * (jc + 1)], self.sLd2)
                    p.dma("scalar", w3t[wp][:, 512 * kc:512 * (kc + 1)],
                          self.wb["w3ppT"][128 * kc:128 * (kc + 1),
                                           512 * jc:512 * (jc + 1)], self.sLd2)
                p.dma("scalar", b1t[wp], self.wb["b1ppT"][:, 512 * jc:512 * (jc + 1)],
                      self.sLd2)
                p.dma("scalar", b3t[wp], self.wb["b3ppT"][:, 512 * jc:512 * (jc + 1)],
                      self.sLd2)
                wld = self.sLd2.n
                p.wait("tensor", self.sLd2, wld)
                p.wait("tensor", self.sV, foldv)
                for r4 in range(4):
                    fch = jc * 4 + r4
                    bank_g, bank_u = ps[0 + (fch % 2)], ps[2 + (fch % 2)]
                    p.wait("tensor", self.sA, aev_at.get(fch - 2, 0))
                    for kc in range(8):
                        p.op("tensor", lambda e, kc=kc, r4=r4, wp=wp, bank_g=bank_g:
                             e.matmul(bank_g[:, :],
                                      w1t[wp][:, 512 * kc + 128 * r4:
                                              512 * kc + 128 * (r4 + 1)],
                                      xfold[:, 512 * kc:512 * (kc + 1)],
                                      start=(kc == 0), stop=False,
                                      skip_group_check=True), inc=None)
                    p.op("tensor", lambda e, r4=r4, wp=wp, bank_g=bank_g: e.matmul(
                        bank_g[:, :], b1t[wp][:, 128 * r4:128 * (r4 + 1)], zrow,
                        start=False, stop=True, skip_group_check=True),
                        inc=self.sPE)
                    if pend is not None:
                        pr4, phv, pfch = pend
                        p.wait("tensor", self.sV, phv)
                        p.op("tensor", lambda e, first=first_z2, pr4=pr4, pfch=pfch:
                             e.matmul(pz1[0:2, 0:512],
                                      a2sbuf[:, 2 * pfch:2 * (pfch + 1)], h4[pr4],
                                      start=first, stop=False,
                                      skip_group_check=True), inc=None)
                        first_z2 = False
                        pend = None
                    for kc in range(8):
                        p.op("tensor", lambda e, kc=kc, r4=r4, wp=wp, bank_u=bank_u:
                             e.matmul(bank_u[:, :],
                                      w3t[wp][:, 512 * kc + 128 * r4:
                                              512 * kc + 128 * (r4 + 1)],
                                      xfold[:, 512 * kc:512 * (kc + 1)],
                                      start=(kc == 0), stop=False,
                                      skip_group_check=True), inc=None)
                    p.op("tensor", lambda e, r4=r4, wp=wp, bank_u=bank_u: e.matmul(
                        bank_u[:, :], b3t[wp][:, 128 * r4:128 * (r4 + 1)], zrow3,
                        start=False, stop=True, skip_group_check=True),
                        inc=self.sPE)
                    guv = self.sPE.n
                    p.wait("scalar", self.sPE, guv)
                    gp = fch % 2
                    p.wait("scalar", self.sV, hv_at.get(fch - 2, 0))
                    sg = p.op("scalar", lambda e, bank_g=bank_g, gp=gp: e.activation(
                        gsc[gp], bank_g[:, :], AF.Silu), inc=self.sA)
                    su = p.op("scalar", lambda e, bank_u=bank_u, gp=gp: e.activation(
                        usc[gp], bank_u[:, :], AF.Copy), inc=self.sA)
                    aev_at[fch] = su
                    p.wait("vector", self.sA, su)
                    hv = p.op("vector", lambda e, r4=r4, gp=gp: e.tensor_tensor(
                        h4[r4], gsc[gp], usc[gp], ALU.mult), inc=self.sV)
                    hv_at[fch] = hv
                    pend = (r4, hv, fch)
                guv_at[jc] = self.sPE.n
                f1 = p.op("vector", lambda e, jc=jc: e.tensor_tensor(
                    hfold[:, 512 * jc:512 * (jc + 1)], h4[0], h4[1], ALU.add),
                    inc=self.sV)
                p.wait("vector", self.sV, f1)
                f2 = p.op("vector", lambda e, jc=jc: e.tensor_tensor(
                    hfold[:, 512 * jc:512 * (jc + 1)],
                    hfold[:, 512 * jc:512 * (jc + 1)], h4[2], ALU.add), inc=self.sV)
                p.wait("vector", self.sV, f2)
                p.op("vector", lambda e, jc=jc: e.tensor_tensor(
                    hfold[:, 512 * jc:512 * (jc + 1)],
                    hfold[:, 512 * jc:512 * (jc + 1)], h4[3], ALU.add), inc=self.sV)
            pr4, phv, pfch = pend
            p.wait("tensor", self.sV, phv)
            p.op("tensor", lambda e, pr4=pr4, pfch=pfch: e.matmul(
                pz1[0:2, 0:512], a2sbuf[:, 2 * pfch:2 * (pfch + 1)], h4[pr4],
                start=False, stop=True, skip_group_check=True), inc=self.sPE)
            pend = None
            hforv = self.sV.n
            z2v = self.sPE.n
            p.wait("scalar", self.sPE, z2v)
            z2e = p.op("scalar", lambda e: e.activation(z2row, pz1[0:2, 0:512],
                                                        AF.Copy, scale=LORA_SC),
                       inc=self.sA)
            p.wait("vector", self.sA, z2e)
            os_mm = {}
            for os in range(8):
                wpp = os % 2
                p.wait("scalar", self.sPE, os_mm.get(os - 2, self.sPE.n if os < 2 else 0))
                for kc in range(22):
                    p.dma("scalar", w2tiles[wpp][:, 512 * kc:512 * (kc + 1)],
                          self.wb["w2ppT"][128 * kc:128 * (kc + 1),
                                           512 * os:512 * (os + 1)], self.sLd2)
                p.dma("scalar", b2t[wpp],
                      self.wb["b2T"][:, 512 * os:512 * (os + 1)], self.sLd2)
                w2ld = self.sLd2.n
                for ts in range(4):
                    rb = 4 * g + ts
                    bank = ps[4 + (ts % 2)]
                    p.wait("tensor", self.sLd2, w2ld)
                    if os == 0 and ts == 0:
                        p.wait("tensor", self.sV, hforv)
                    p.wait("tensor", self.sV, self.sV.n)
                    for kc in range(22):
                        p.op("tensor", lambda e, kc=kc, ts=ts, os=os, bank=bank,
                             wpp=wpp: e.matmul(
                                 bank[:, :],
                                 hfold[:, 512 * kc + 128 * ts:
                                       512 * kc + 128 * (ts + 1)],
                                 w2tiles[wpp][:, 512 * kc:512 * (kc + 1)],
                                 start=(kc == 0), stop=False,
                                 skip_group_check=True), inc=None)
                    p.op("tensor", lambda e, bank=bank, ts=ts, wpp=wpp: e.matmul(
                        bank[:, :], z2row[:, 128 * ts:128 * (ts + 1)],
                        b2t[wpp], start=False, stop=True, skip_group_check=True),
                        inc=self.sPE)
                    mmv = self.sPE.n
                    xi = ts % 2
                    p.dma("sync", xo[xi],
                          x_old[128 * rb: 128 * (rb + 1), 512 * os: 512 * (os + 1)],
                          self.sLd, wait=[(self.sV, self.sV.n)])
                    xod = self.sLd.n
                    p.wait("vector", self.sPE, mmv)
                    p.wait("vector", self.sLd, xod)
                    p.wait("vector", self.sSt, self.sSt.n)
                    mv = p.op("vector", lambda e, bank=bank, xi=xi: e.tensor_tensor(
                        xn[xi], bank[:, :], xo[xi], ALU.add), inc=self.sV)
                    p.dma("sync", x_new[128 * rb: 128 * (rb + 1),
                                        512 * os: 512 * (os + 1)],
                          xn[xi], self.sSt, wait=[(self.sV, mv)])
                os_mm[os] = self.sPE.n
        self.psum_close()

    def finish(self, taps=None):
        p, dt = self.p, self.dt
        self.fence()
        if taps:
            for name, h in taps:
                o = self.nc.dram_tensor(name, list(h.shape), h.dtype,
                                        kind="ExternalOutput")
                p.dma("sync", o[tuple(slice(None) for _ in h.shape)],
                      h[tuple(slice(None) for _ in h.shape)], self.sSt)
        p.wait("sync", self.sSt)
        p.wait("sync", self.sLd)
        p.wait("sync", self.sLd2)
        p.wait("sync", self.sCast)

    def build(self):
        nc = self.nc
        with self.es:
            sem_cms = []
            for s in self.p.sems:
                cm = nc.semaphore(s.name)
                s.h = cm.__enter__()
                sem_cms.append(cm)
            with nc.Block() as block:
                self.p.build_into(block)
            for cm in reversed(sem_cms):
                cm.__exit__(None, None, None)
        return nc


PHASES_ALL = ("ln1", "qkv0", "attn0", "oproj0", "ffn0",
              "ln3", "qkv1", "attn1", "oproj1", "ffn1")


def build_program(phases=PHASES_ALL, cut=0, taps=()):
    nc = bass.Bass("TRN2", target_bir_lowering=False)
    b = Builder(nc, cut=cut)
    b.prologue()
    if cut in (0, 1):
        b.emit_casts(["w0q", "w0kv"])
    if cut == 2:
        b.emit_casts(b._order_l1)
    dt = b.dt
    for ph in phases:
        if ph == "ln1":
            b.ln_phase("x0")
        elif ph == "ex1" or ph == "ex3":
            b.exchange_lnx()
        elif ph == "qkv0":
            b.qproj(b.wb["w0q"], dt["qtm"], wname="w0q")
            b.kvproj(b.wb["w0kv"], [dt[f"LNXPAIRp{k}"] for k in range(4)],
                     [0, 1], dt["kvK0"], dt["kvV0"], wname="w0kv")
        elif ph == "attn0":
            b.p.wait("gpsimd", b.sLd, b.sLd.n)
            b.attention(0, cast_pend=b.cast_chunks(b._order_l0 + b._order_l1))
        elif ph == "exc0" or ph == "exc1":
            b.exchange_ctx()
        elif ph == "oproj0":
            b.oproj(0, b.wb["w0o"], dt["x0"], dt["xA"], wname="w0o")
        elif ph == "ffn0":
            b.ffn0(dt["xA"], dt["xB"] if cut == 0 else dt["xB_out"])
        elif ph == "ln3":
            b.ln_phase("xB")
        elif ph == "qkv1":
            b.qproj(b.wb["w1q_attn"], dt["qtm"], wname="w1q_attn")
            for bi in range(4):
                b.kvproj(b.wb["w1kv"], [dt[f"LNXALLp{k}"] for k in range(4)],
                         [2 * bi, 2 * bi + 1],
                         dt["kv1K"][128 * bi:128 * (bi + 1), :],
                         dt["kv1V"][128 * bi:128 * (bi + 1), :], wname="w1kv")
        elif ph == "attn1":
            b.attention(1)
        elif ph == "oproj1":
            b.oproj(1, b.wb["w1o"], dt["xB"], dt["xC"], wname="w1o")
        elif ph == "ffn1":
            b.ffn1(dt["xC"], dt["x_out"])
        else:
            raise ValueError(ph)
    b.finish(taps=[(n + "_tap", dt[n]) for n in taps])
    return b.build()


def kernel(**inputs):
    nc = build_program()
    in_maps = [host_prep(inputs, c) for c in range(NCORES)]
    res = run_bass_kernel_spmd(nc, in_maps, core_ids=list(range(NCORES)))
    return assemble_out(res.results)


# version 15 (final)
# speedup vs baseline: 1.1087x; 1.1087x over previous
import contextlib
import math

import numpy as np

import concourse.bass as bass
import concourse.mybir as mybir
from concourse.bass_utils import run_bass_kernel_spmd

F32 = mybir.dt.float32
BF16 = mybir.dt.bfloat16
AF = mybir.ActivationFunctionType
ALU = mybir.AluOpType
AX = mybir.AxisListType

NCORES = 8
B, S, D = 4, 4096, 4096
NH, HD = 32, 128
T = 2048
NSTRIP = 8
QSCALE = 1.0 / math.sqrt(HD)
MSHIFT = 20.0
NEG = -1e30

NEXP, TOPK, RDIM, RANK = 64, 16, 16, 8
MOE_SCALING = 2.0
HFF = 11008
HPAD = 11264
JFOLD = 2752
JPAD = 2816
LORA_SC = 0.5


def _alibi_slopes(n: int) -> np.ndarray:
    cp2 = 2 ** math.floor(math.log2(n))
    base = 2.0 ** (-(2.0 ** (-(math.log2(cp2) - 3))))
    slopes = base ** np.arange(1, cp2 + 1, dtype=np.float64)
    if cp2 != n:
        extra_base = 2.0 ** (-(2.0 ** (-(math.log2(2 * cp2) - 3))))
        rem = min(cp2, n - cp2)
        extra = extra_base ** np.arange(1, 1 + 2 * rem, 2, dtype=np.float64)
        slopes = np.concatenate([slopes, extra])
    return slopes.astype(np.float32)



def _own_rows(c: int) -> np.ndarray:
    b, a = c // 2, c % 2
    idx = []
    for s in range(NSTRIP):
        st = 512 * s + 256 * a
        idx.extend(range(st, st + 256))
    return np.asarray(idx)


def _l1_perm_pad_rows(w: np.ndarray) -> np.ndarray:
    out = np.zeros((HPAD,) + w.shape[1:], dtype=w.dtype)
    for jc in range(22):
        npj = 128 if jc < 21 else 64
        for r in range(4):
            slot0 = (jc * 4 + r) * 128
            src0 = JFOLD * r + 128 * jc
            out[slot0:slot0 + npj] = w[src0:src0 + npj]
    return out


def _w2_pad_rows(w2qT: np.ndarray) -> np.ndarray:
    out = np.zeros((JPAD,) + w2qT.shape[1:], dtype=w2qT.dtype)
    out[:JFOLD] = w2qT
    return out


def host_prep(inputs: dict, c: int) -> dict:
    b, a = c // 2, c % 2
    f32 = lambda x: np.ascontiguousarray(np.asarray(x), dtype=np.float32)
    d = {}
    x0 = f32(inputs["L0_attn_hidden_states"])
    d["x0"] = np.ascontiguousarray(x0[b][_own_rows(c)])
    wqkv = f32(inputs["L0_attn_Wqkv"])
    d["w0q"] = np.ascontiguousarray(wqkv[:, 2048 * a:2048 * a + 2048])
    d["w0kv"] = np.ascontiguousarray(wqkv[:, 4096:4352])
    d["w0o"] = f32(inputs["L0_attn_Wo"])
    d["awT"] = np.ascontiguousarray(f32(inputs["L0_ffn_A_w"]).T)
    d["bwT"] = np.ascontiguousarray(f32(inputs["L0_ffn_B_w"]).T)
    d["wqwT"] = np.ascontiguousarray(f32(inputs["L0_ffn_Wq_w"]).T)
    d["keysT"] = np.ascontiguousarray(f32(inputs["L0_ffn_keys"]).T)
    d["w1q_attn"] = np.ascontiguousarray(f32(inputs["L1_attn_Wq"])[:, 2048 * a:2048 * a + 2048])
    d["w1kv"] = np.ascontiguousarray(f32(inputs["L1_attn_Wkv"])[:, 256 * c:256 * c + 256])
    d["w1o"] = f32(inputs["L1_attn_Wo"])
    d["bo_b"] = np.ascontiguousarray(
        np.broadcast_to(f32(inputs["L1_attn_bo"])[None, :], (1, D)))
    w1pp = _l1_perm_pad_rows(f32(inputs["L1_ffn_w1q"]))
    w3pp = _l1_perm_pad_rows(f32(inputs["L1_ffn_w3q"]))
    b1pp = _l1_perm_pad_rows(f32(inputs["L1_ffn_b1"]))
    b3pp = _l1_perm_pad_rows(f32(inputs["L1_ffn_b3"]))
    d["w1ppT"] = np.ascontiguousarray(w1pp.T)
    d["w3ppT"] = np.ascontiguousarray(w3pp.T)
    d["b1ppT"] = np.ascontiguousarray(b1pp.T)
    d["b3ppT"] = np.ascontiguousarray(b3pp.T)
    a1T = np.ascontiguousarray(f32(inputs["L1_ffn_a1"]).T)
    a3T = np.ascontiguousarray(f32(inputs["L1_ffn_a3"]).T)
    a13 = np.zeros((128, 128), np.float32)
    for ch in range(32):
        a13[:, 4 * ch:4 * ch + 2] = a1T[128 * ch:128 * (ch + 1)]
        a13[:, 4 * ch + 2:4 * ch + 4] = a3T[128 * ch:128 * (ch + 1)]
    d["a13sb"] = np.ascontiguousarray(a13)
    a2pp = _l1_perm_pad_rows(np.ascontiguousarray(f32(inputs["L1_ffn_a2"]).T))
    d["a2sb"] = np.ascontiguousarray(
        a2pp.reshape(88, 128, 2).transpose(1, 0, 2).reshape(128, 176))
    d["w2ppT"] = _w2_pad_rows(np.ascontiguousarray(f32(inputs["L1_ffn_w2q"]).T))
    d["b2T"] = np.ascontiguousarray(f32(inputs["L1_ffn_b2"]).T)

    slopes = _alibi_slopes(NH)[16 * a:16 * a + 16]
    d["slopes"] = np.ascontiguousarray(
        np.broadcast_to(slopes[None, :], (128, 16)).astype(np.float32))
    hh, ss, kk = np.meshgrid(np.arange(16), np.arange(8), np.arange(32), indexing="ij")
    ctab = (128 * kk - 512 * ss + 127).astype(np.float32)
    btab = slopes[hh.astype(np.int64)] * ctab - MSHIFT
    d["biastab"] = np.ascontiguousarray(
        np.broadcast_to(btab.reshape(1, -1), (128, 16 * 8 * 32)).astype(np.float32))
    kp = np.arange(128)[None, :, None]
    qf = np.arange(512)[None, None, :]
    rr = np.arange(4)[:, None, None]
    d["diagmask"] = np.ascontiguousarray(
        (kp <= qf - 128 * rr).astype(np.float32))
    d["identf"] = np.eye(128, dtype=np.float32)
    cvals = np.array([-MSHIFT, 1e-5, 1.0 / D, 1.0], dtype=np.float32)
    d["consts"] = np.ascontiguousarray(np.broadcast_to(cvals[None, :], (128, 4)))
    ee = np.arange(NEXP)[:, None]
    nn = np.arange(NEXP * RANK)[None, :]
    d["expmat"] = np.ascontiguousarray((nn // RANK == ee).astype(np.float32))
    return d


def assemble_out(results) -> np.ndarray:
    out = np.empty((B, S, D), dtype=np.float32)
    for c in range(NCORES):
        out[c // 2].reshape(S, D)[_own_rows(c)] = results[c]["x_out"]
    return out



class Sem:
    __slots__ = ("name", "n", "h")

    def __init__(self, name):
        self.name = name
        self.n = 0
        self.h = None

    def add(self, k=1):
        self.n += k
        return self.n


class Prog:

    ENGS = ("sync", "gpsimd", "tensor", "vector", "scalar")

    def __init__(self, nc):
        self.nc = nc
        self.q = {e: [] for e in self.ENGS}
        self.sems = []

    def sem(self, name):
        s = Sem(name)
        self.sems.append(s)
        return s

    def raw(self, eng, fn):
        self.q[eng].append(fn)

    def wait(self, eng, sem, val=None):
        v = sem.n if val is None else val
        if v <= 0:
            return
        self.q[eng].append(lambda e, s=sem, v=v: e.wait_ge(s.h, v))

    def op(self, eng, fn, inc=None, k=1):
        if inc is None:
            self.q[eng].append(fn)
            return None
        v = inc.add(k)
        self.q[eng].append(lambda e, f=fn, s=inc, kk=k: f(e).then_inc(s.h, kk))
        return v

    def dma(self, eng, out, in_, inc, wait=None, **kw):
        if wait:
            for s, v in wait:
                self.wait(eng, s, v)
        v = inc.add(16)
        self.q[eng].append(
            lambda e, o=out, i=in_, s=inc, kw2=kw: e.dma_start(out=o, in_=i, **kw2).then_inc(s.h, 16))
        return v

    def build_into(self, block):
        for eng in self.ENGS:
            ops = self.q[eng]
            if not ops:
                continue
            def make(ops_):
                def body(e):
                    for f in ops_:
                        f(e)
                return body
            getattr(block, eng)(make(ops))


def ceil_div(a, b):
    return (a + b - 1) // b


class Builder:
    def __init__(self, nc, cut=0):
        self.nc = nc
        self.p = Prog(nc)
        self.cut = cut
        self.es = contextlib.ExitStack()
        self._mk_dram()
        self._mk_sbuf()
        self._mk_sems()

    def _mk_dram(self):
        nc = self.nc
        dt = {}
        def ein(name, shape, dtype=F32):
            dt[name] = nc.dram_tensor(name, shape, dtype, kind="ExternalInput")
        def internal(name, shape, dtype):
            dt[name] = nc.dram_tensor(name, shape, dtype)

        want_l0 = self.cut in (0, 1)
        want_l1 = self.cut in (0, 2)

        if want_l0:
            ein("x0", [T, D])
            ein("w0q", [D, 2048]); ein("w0kv", [D, 256]); ein("w0o", [D, D])
            ein("awT", [D, 512]); ein("bwT", [512, D])
            ein("wqwT", [D, RDIM]); ein("keysT", [RDIM, NEXP])
            ein("expmat", [NEXP, NEXP * RANK])
            ein("biastab", [128, 16 * 8 * 32])
        if want_l1:
            ein("w1q_attn", [D, 2048]); ein("w1kv", [D, 256]); ein("w1o", [D, D])
            ein("bo_b", [1, D])
            ein("w1ppT", [1024, HPAD]); ein("w3ppT", [1024, HPAD])
            ein("b1ppT", [2, HPAD]); ein("b3ppT", [2, HPAD])
            ein("a13sb", [128, 128])
            ein("a2sb", [128, 176]); ein("w2ppT", [JPAD, D]); ein("b2T", [2, D])
        ein("slopes", [128, 16])
        ein("diagmask", [4, 128, 512])
        ein("identf", [128, 128])
        ein("consts", [128, 4])
        if self.cut == 2:
            ein("xB", [T, D])

        if self.cut == 1:
            dt["xB_out"] = nc.dram_tensor("xB_out", [T, D], F32, kind="ExternalOutput")
        if want_l1:
            dt["x_out"] = nc.dram_tensor("x_out", [T, D], F32, kind="ExternalOutput")

        for k in range(4):
            internal(f"lnxp{k}", [D, 512], BF16)
            dt[f"LNXALLp{k}"] = nc.dram_tensor(f"LNXALLp{k}", [NCORES * D, 512],
                                               BF16, addr_space="Shared")
            internal(f"LNXPAIRp{k}", [2 * D, 512], BF16)
            internal(f"ctxp{k}", [4 * NSTRIP * 2 * 128, 256], BF16)
            dt[f"CTXALLp{k}"] = nc.dram_tensor(
                f"CTXALLp{k}", [NCORES * 4 * NSTRIP * 2 * 128, 256], BF16,
                addr_space="Shared")
            internal(f"CTXSELFp{k}", [2 * 4 * NSTRIP * 128, 256], BF16)
        internal("dbgstats", [16 * 128, 8], F32)
        internal("dbgp", [32 * 128, 512], BF16)
        internal("qtm", [S, 2048], BF16)
        internal("kvK0", [128, S], BF16); internal("kvV0", [128, S], BF16)
        if want_l0:
            internal("xA", [T, D], F32)
            if self.cut == 0:
                internal("xB", [T, D], F32)
        if want_l1:
            internal("kv1K", [4 * 128, S], BF16)
            internal("kv1V", [4 * 128, S], BF16)
            internal("xC", [T, D], F32)
        self.wb = {}
        def wcast(name, shape):
            self.wb[name] = nc.dram_tensor(name + "_b", shape, BF16)
        if want_l0:
            wcast("w0q", [D, 2048]); wcast("w0kv", [D, 256]); wcast("w0o", [D, D])
            wcast("awT", [D, 512]); wcast("bwT", [512, D])
            wcast("expmat", [NEXP, NEXP * RANK])
        if want_l1:
            wcast("w1q_attn", [D, 2048]); wcast("w1kv", [D, 256]); wcast("w1o", [D, D])
            wcast("w1ppT", [1024, HPAD]); wcast("w3ppT", [1024, HPAD])
            wcast("b1ppT", [2, HPAD]); wcast("b3ppT", [2, HPAD])
            wcast("a13sb", [128, 128])
            wcast("a2sb", [128, 176]); wcast("w2ppT", [JPAD, D]); wcast("b2T", [2, D])
            wcast("bo_b", [1, D])
        self.dt = dt

    def _mk_sbuf(self):
        nc, es = self.nc, self.es
        self.aF = es.enter_context(nc.sbuf_tensor("arenaF", [128, 14336], F32))
        self.aB = es.enter_context(nc.sbuf_tensor("arenaB", [128, 55296], BF16))
        self.cF = es.enter_context(nc.sbuf_tensor("constF", [128, 640 + 16 + 4 + 4096], F32))
        self.cB = es.enter_context(nc.sbuf_tensor("constB", [128, 6016], BF16))
        self.idF = es.enter_context(nc.sbuf_tensor("idF", [128, 128], F32))
        self.idB = es.enter_context(nc.sbuf_tensor("idB", [128, 128], BF16))

    def _mk_sems(self):
        p = self.p
        self.sLd = p.sem("sLd")
        self.sLd2 = p.sem("sLd2")
        self.sCast = p.sem("sCast")
        self.sSt = p.sem("sSt")
        self.sPE = p.sem("sPE")
        self.sV = p.sem("sV")
        self.sA = p.sem("sA")
        self.sCC = p.sem("sCC")
        self.sG = p.sem("sG")
        self.sFree = p.sem("sFree")
        self.sMisc = p.sem("sMisc")

    @property
    def dm(self):
        return self.cF[:, 0:512]
    @property
    def slopes_sb(self):
        return self.cF[:, 512:528]
    @property
    def consts_sb(self):
        return self.cF[:, 528:532]
    @property
    def biastab_sb(self):
        return self.cF[:, 532:532 + 4096]
    def btile(self, i):
        return self.cB[:, 512 * i:512 * (i + 1)]
    @property
    def ones_col_b(self):
        return self.cB[:, 2560:2561]
    @property
    def ones_row_f(self):
        return self.cF[0:1, 4628:4756]
    @property
    def ones_full_b(self):
        return self.cB[:, 5888:6016]
    @property
    def ones_row_b(self):
        return self.cB[0:1, 2561:2689]
    def diagmask_b(self, r):
        return self.cB[:, 2689 + 512 * r: 2689 + 512 * (r + 1)]

    def prologue(self):
        p, dt = self.p, self.dt
        p.dma("sync", self.idF[:, :], dt["identf"][:, :], self.sLd)
        p.dma("sync", self.cF[:, 512:528], dt["slopes"][:, :], self.sLd)
        p.dma("sync", self.cF[:, 528:532], dt["consts"][:, :], self.sLd)
        p.dma("gpsimd", self.idB[:, :], dt["identf"][:, :], self.sCast)
        for r in range(4):
            p.dma("gpsimd", self.diagmask_b(r), dt["diagmask"][r, :, :], self.sCast)
        def mkregs(e):
            rv = e.partition_id()
            self._pb = e.snap(e.scalar_reg_alu(ALU.bitwise_and, rv, 0xFFFFFFFE),
                              min_val=0, max_val=6)
            self._aa = e.snap(e.scalar_reg_alu(ALU.bitwise_and, rv, 1),
                              min_val=0, max_val=1)
            self._pbD = e.snap(e.scalar_reg_alu(ALU.mult, self._pb, D),
                               min_val=0, max_val=6 * D)
            return e.nop()
        p.raw("sync", mkregs)
        p.op("gpsimd", lambda e: e.iota(self.dm, [[-1, 512]], base=-127,
                                        channel_multiplier=1,
                                        allow_small_or_imprecise_dtypes=True),
             inc=self.sG)
        p.op("vector", lambda e: e.memset(self.ones_col_b, 1.0), inc=self.sV)
        p.op("vector", lambda e: e.memset(self.cB[0:1, 2561:2689], 1.0), inc=self.sV)
        p.op("vector", lambda e: e.memset(self.cF[0:1, 4628:4756], 1.0), inc=self.sV)
        p.op("vector", lambda e: e.memset(self.cB[:, 5888:6016], 1.0), inc=self.sV)
        self.cast_consts = self.sCast.n
        self.wcast_done = {}
        order_l0 = ["w0q", "w0kv", "w0o", "awT", "bwT", "expmat"]
        order_l1 = ["w1q_attn", "w1kv", "w1o", "bo_b",
                    "w1ppT", "w3ppT", "b1ppT", "b3ppT", "a13sb",
                    "a2sb", "w2ppT", "b2T"]
        order = []
        if self.cut in (0, 1):
            order += order_l0
        if self.cut in (0, 2):
            order += order_l1
        self._order_l0, self._order_l1 = order_l0, order_l1
        import os
        skip = os.environ.get("SKIP_WCAST", "") == "1"
        self._wcast_order = [] if skip else order
        self._wcast_emitted = set()

    def cast_chunks(self, names):
        dt = self.dt
        out = []
        for name in names:
            if name not in self.wb or name in self._wcast_emitted \
                    or name not in self._wcast_order:
                continue
            self._wcast_emitted.add(name)
            srct = dt[name]
            rows = srct.shape[0]
            step = max(128, min(rows, (1 << 22) // max(1, srct.shape[1])))
            spans = [(r, min(rows, r + step)) for r in range(0, rows, step)]
            for idx, span in enumerate(spans):
                out.append((name, idx == len(spans) - 1, span))
        return out

    def emit_cast_chunk(self, chunk):
        name, is_last, (r, r2) = chunk
        v = self.p.dma("gpsimd", self.wb[name][r:r2, :],
                       self.dt[name][r:r2, :], self.sCast)
        if is_last:
            self.wcast_done[name] = v

    def emit_casts(self, names):
        for chunk in self.cast_chunks(names):
            self.emit_cast_chunk(chunk)

    def fence(self, include_cast=False):
        p = self.p
        sems = [self.sLd, self.sLd2, self.sSt, self.sPE, self.sV,
                self.sA, self.sCC, self.sG, self.sMisc]
        if include_cast:
            sems.append(self.sCast)
        for eng in Prog.ENGS:
            for s in sems:
                p.wait(eng, s)

    def wait_wcast(self, eng, *names):
        vals = [self.wcast_done.get(n, 0) for n in names]
        v = max(vals) if vals else 0
        if v:
            self.p.wait(eng, self.sCast, v)

    def ln_tile_A(self, xf, stats, junk):
        p = self.p
        v0 = p.op("vector", lambda e: e.tensor_scalar(
            junk, xf, 1.0, None, ALU.mult, ALU.add, accum_out=stats[:, 0:1]),
            inc=self.sV)
        a0 = p.op("scalar", lambda e: e.activation(
            junk, xf, AF.Square, accum_out=stats[:, 1:2]), inc=self.sA)
        return (v0, a0)

    def ln_tile_B(self, xf, stats, out_bf, va, out_f32=None):
        p = self.p
        sV, sA = self.sV, self.sA
        v0, a0 = va
        p.wait("vector", sA, a0)
        p.wait("vector", sV, v0)
        v1 = p.op("vector", lambda e: e.tensor_scalar(
            stats[:, 2:3], stats[:, 0:1], 1.0 / D, None, ALU.mult), inc=sV)
        p.wait("vector", sV, v1)
        v2 = p.op("vector", lambda e: e.tensor_scalar(
            stats[:, 3:4], stats[:, 1:2], 1.0 / D, None, ALU.mult), inc=sV)
        p.wait("vector", sV, v2)
        v3 = p.op("vector", lambda e: e.tensor_tensor(
            stats[:, 4:5], stats[:, 2:3], stats[:, 2:3], ALU.mult), inc=sV)
        p.wait("vector", sV, v3)
        v4 = p.op("vector", lambda e: e.tensor_tensor(
            stats[:, 5:6], stats[:, 3:4], stats[:, 4:5], ALU.subtract), inc=sV)
        p.wait("vector", sV, v4)
        v5 = p.op("vector", lambda e: e.tensor_scalar(
            stats[:, 5:6], stats[:, 5:6], 1e-5, None, ALU.add), inc=sV)
        p.wait("scalar", sV, v5)
        a1 = p.op("scalar", lambda e: e.activation(
            stats[:, 6:7], stats[:, 5:6], AF.Sqrt), inc=sA)
        p.wait("vector", sA, a1)
        v6 = p.op("vector", lambda e: e.reciprocal(stats[:, 7:8], stats[:, 6:7]),
                  inc=sV)
        p.wait("vector", sV, v6)
        last = p.op("vector", lambda e: e.tensor_scalar(
            out_bf, xf, stats[:, 2:3], stats[:, 7:8], ALU.subtract, ALU.mult), inc=sV)
        if out_f32 is not None:
            last = p.op("vector", lambda e: e.tensor_scalar(
                out_f32, xf, stats[:, 2:3], stats[:, 7:8], ALU.subtract, ALU.mult),
                inc=sV)
        return last

    def ln_tile(self, xf, stats, junk, out_bf, out_f32=None):
        va = self.ln_tile_A(xf, stats, junk)
        return self.ln_tile_B(xf, stats, out_bf, va, out_f32=out_f32)

    def transpose_tile_to_staging(self, src_bf, staging, tt, wait_v):
        p = self.p
        p.wait("tensor", self.sV, wait_v)
        p.wait("tensor", self.sA, getattr(self, "_tts_prevA", 0))
        ev_done = {}
        for grp in range(4):
            pT = self.pT2[grp % 2]
            if grp >= 2:
                s2, c2 = ev_done[grp - 2]
                p.wait("tensor", s2, c2)
            pe_last = None
            for j in range(8):
                ch = grp * 8 + j
                pe_last = p.op("tensor", lambda e, ch=ch, j=j, pT=pT: e.transpose(
                    pT[:, 128 * j:128 * (j + 1)],
                    src_bf[:, 128 * ch:128 * (ch + 1)], self.idB[:, :]), inc=self.sPE)
            if grp % 2 == 0:
                p.wait("vector", self.sPE, pe_last)
                for j in range(8):
                    ch = grp * 8 + j
                    p.op("vector", lambda e, ch=ch, j=j, pT=pT: e.tensor_copy(
                        staging[:, 512 * ch + 128 * tt: 512 * ch + 128 * (tt + 1)],
                        pT[:, 128 * j:128 * (j + 1)]), inc=self.sV)
                ev_done[grp] = (self.sV, self.sV.n)
            else:
                p.wait("scalar", self.sPE, pe_last)
                for j in range(8):
                    ch = grp * 8 + j
                    p.op("scalar", lambda e, ch=ch, j=j, pT=pT: e.activation(
                        staging[:, 512 * ch + 128 * tt: 512 * ch + 128 * (tt + 1)],
                        pT[:, 128 * j:128 * (j + 1)], AF.Copy), inc=self.sA)
                ev_done[grp] = (self.sA, self.sA.n)
        self._tts_prevA = self.sA.n
        return (self.sV.n, self.sA.n)

    def ln_phase(self, src_name):
        self.fence()
        self.lnx_pair_ready = {}
        self.lnx_all_ready = {}
        self.pT2 = self.psum_open([("pT", [128, 1024], BF16), ("pT_b", [128, 1024], BF16)])
        p, dt = self.p, self.dt
        p.wait("tensor", self.sCast, self.cast_consts)
        src = dt[src_name]
        af, ab = self.aF, self.aB
        xf = [af[:, 0:4096], af[:, 4096:8192]]
        junk = af[:, 8192:12288]
        stats2 = [af[:, 12288:12296], af[:, 12304:12312]]
        lnb = [ab[:, 0:4096], ab[:, 4096:8192]]
        staging = [ab[:, 8192 + 16384 * i:8192 + 16384 * (i + 1)] for i in range(2)]
        tr_done = {}
        st_done = {}
        va_at = {}
        def emit_B(i):
            pi = i % 2
            s, tt = i // 4, i % 4
            sp = s % 2
            av = self.ln_tile_B(xf[pi], stats2[pi], lnb[pi], va_at[i])
            p.wait("tensor", self.sV, av)
            if tt == 0 and s >= 2:
                p.wait("tensor", self.sSt, st_done[s - 2])
            self.transpose_tile_to_staging(lnb[pi], staging[sp], tt, av)
            tr_done[i] = (self.sV.n, self.sA.n)

        def emit_strip_out(s):
            sp = s % 2
            vstore, astore = self.sV.n, self.sA.n
            piece = dt[f"lnxp{s}"]
            for ch in range(32):
                p.dma("scalar", piece[128 * ch:128 * (ch + 1), :],
                      staging[sp][:, 512 * ch:512 * (ch + 1)], self.sSt,
                      wait=[(self.sV, vstore), (self.sA, astore)] if ch == 0 else None)
            st_done[s] = self.sSt.n
            p.wait("gpsimd", self.sSt, st_done[s])
            cc = p.op("gpsimd", lambda e, s=s: e.collective_compute(
                "AllGather", ALU.bypass, replica_groups=[list(range(NCORES))],
                ins=[dt[f"lnxp{s}"].ap().opt()],
                outs=[dt[f"LNXALLp{s}"].ap().opt()]), inc=self.sCC)
            p.wait("sync", self.sCC, cc)
            def dyncopy(e, s=s):
                return e.dma_start(out=dt[f"LNXPAIRp{s}"][:, :],
                                   in_=dt[f"LNXALLp{s}"][bass.ds(self._pbD, 2 * D), :])
            pr = p.op("sync", dyncopy, inc=self.sLd, k=16)
            self.lnx_pair_ready[s] = pr
            self.lnx_all_ready[s] = cc

        for i in range(16):
            pi = i % 2
            wl = []
            if i >= 2:
                wl.append((self.sV, tr_done[i - 2][0]))
                wl.append((self.sA, tr_done[i - 2][1]))
            p.dma("sync", xf[pi][:, :], src[128 * i:128 * (i + 1), :],
                  self.sLd, wait=wl or None)
            ld = self.sLd.n
            if i >= 1:
                emit_B(i - 1)
                if (i - 1) % 4 == 3:
                    emit_strip_out((i - 1) // 4)
            p.wait("vector", self.sLd, ld)
            p.wait("scalar", self.sLd, ld)
            va_at[i] = self.ln_tile_A(xf[pi], stats2[pi], junk)
        emit_B(15)
        emit_strip_out(3)
        self.psum_close()
        return self.sSt.n

    def exchange_lnx(self):
        self.fence()
        p, dt = self.p, self.dt
        cc = p.op("gpsimd", lambda e: e.collective_compute(
            "AllGather", ALU.bypass, replica_groups=[list(range(NCORES))],
            ins=[dt["lnxfm"].ap().opt()], outs=[dt["LNXALL"].ap().opt()]),
            inc=self.sCC)
        p.wait("sync", self.sCC, cc)
        def dyncopy(e):
            rv = e.partition_id()
            pairbase = e.scalar_reg_alu(ALU.bitwise_and, rv, 0xFFFFFFFE)
            off = e.scalar_reg_alu(ALU.mult, pairbase, D)
            return e.dma_start(out=dt["LNXPAIR"][:, :],
                               in_=dt["LNXALL"][bass.ds(off, 2 * D), :])
        p.op("sync", dyncopy, inc=self.sLd, k=16)
        return self.sLd.n

    _psum_uid = 0

    def psum_open(self, specs):
        self._psum_cms = []
        out = []
        for name, shape, dtype in specs:
            Builder._psum_uid += 1
            cm = self.nc.psum_tensor(f"{name}_u{Builder._psum_uid}", shape, dtype)
            out.append(cm.__enter__())
            self._psum_cms.append(cm)
        return out

    def psum_close(self):
        for cm in reversed(self._psum_cms):
            cm.__exit__(None, None, None)
        self._psum_cms = []

    def qproj(self, wq_b, qtm, wname=None):
        self.fence()
        p, dt = self.p, self.dt
        if wname:
            self.wait_wcast("scalar", wname)
        ab = self.aB
        stg = ab[:, 0:16384]
        staging = [stg, stg]
        wt = [ab[:, 16384 + 16384 * i: 16384 + 16384 * (i + 1)] for i in range(2)]
        qev = [ab[:, 49152 + 512 * i: 49152 + 512 * (i + 1)] for i in range(2)]
        (pq0, pq1) = self.psum_open([("pq0", [128, 512], F32), ("pq1", [128, 512], F32)])
        mm_s = {}
        mm_fg = {}
        stage_at = {}
        for s in range(NSTRIP):
            sp = 0
            if s >= 1:
                p.wait("sync", self.sPE, mm_s.get(s - 1, 0))
            pair = dt[f"LNXPAIRp{s // 2}"]
            coff = 256 * (s % 2)
            p.wait("sync", self.sLd, self.lnx_pair_ready[s // 2])
            for ch in range(32):
                for r in range(2):
                    p.dma("sync", staging[sp][:, 512 * ch + 256 * r:
                                              512 * ch + 256 * (r + 1)],
                          pair[4096 * r + 128 * ch: 4096 * r + 128 * (ch + 1),
                               coff: coff + 256], self.sLd)
            stage_at[s] = self.sLd.n
            for fg in range(4):
                gk = 4 * s + fg
                p.wait("scalar", self.sPE, mm_fg.get(gk - 2, 0))
                for ch in range(32):
                    p.dma("scalar", wt[fg % 2][:, 512 * ch: 512 * (ch + 1)],
                          wq_b[128 * ch: 128 * (ch + 1), 512 * fg: 512 * (fg + 1)],
                          self.sLd2)
                wdone = self.sLd2.n
                p.wait("tensor", self.sLd, stage_at[s])
                p.wait("tensor", self.sLd2, wdone)
                for tt in range(4):
                    pq = (pq0, pq1)[tt % 2]
                    p.wait("tensor", self.sA, self.sA.n)
                    for ch in range(32):
                        p.op("tensor", lambda e, ch=ch, tt=tt, pq=pq, fg=fg, sp=sp:
                             e.matmul(
                                 pq[:, :],
                                 staging[sp][:, 512 * ch + 128 * tt:
                                             512 * ch + 128 * (tt + 1)],
                                 wt[fg % 2][:, 512 * ch: 512 * (ch + 1)],
                                 start=(ch == 0), stop=(ch == 31),
                                 skip_group_check=True),
                             inc=self.sPE if ch == 31 else None)
                    mmv = self.sPE.n
                    p.wait("scalar", self.sPE, mmv)
                    p.wait("scalar", self.sSt, self.sSt.n)
                    ev = p.op("scalar", lambda e, tt=tt, pq=pq: e.activation(
                        qev[tt % 2], pq[:, :], AF.Copy, scale=QSCALE), inc=self.sA)
                    p.dma("sync", qtm[512 * s + 128 * tt: 512 * s + 128 * (tt + 1),
                                      512 * fg: 512 * (fg + 1)],
                          qev[tt % 2], self.sSt, wait=[(self.sA, ev)])
                mm_fg[gk] = self.sPE.n
            mm_s[s] = self.sPE.n
        self.psum_close()

    def kvproj(self, wkv_b, pieces, ranks, outK, outV, wname=None):
        self.fence()
        p = self.p
        if wname:
            self.wait_wcast("scalar", wname)
        for k in range(4):
            p.wait("sync", self.sCC, self.lnx_all_ready[k])
            p.wait("sync", self.sLd, self.lnx_pair_ready[k])
        ab = self.aB
        page = [ab[:, 0:2048], ab[:, 2048:4096]]
        wkv = ab[:, 4096:4096 + 32 * 256]
        kev = [ab[:, 12288 + 512 * i:12288 + 512 * (i + 1)] for i in range(2)]
        vev = [ab[:, 13312 + 512 * i:13312 + 512 * (i + 1)] for i in range(2)]
        ps = self.psum_open([(f"pkv{i}", [128, 512], F32) for i in range(8)])
        for ch in range(32):
            p.dma("scalar", wkv[:, 256 * ch: 256 * (ch + 1)],
                  wkv_b[128 * ch: 128 * (ch + 1), :], self.sLd2)
        wv = self.sLd2.n
        p.wait("tensor", self.sLd2, wv)
        for ri, r in enumerate(ranks):
            mm_at = {}
            for ch in range(32):
                pi = ch % 2
                wl = []
                if ch >= 2:
                    wl.append((self.sPE, mm_at[ch - 2]))
                elif ri > 0:
                    wl.append((self.sPE, self.sPE.n))
                for k in range(4):
                    p.dma("sync", page[pi][:, 512 * k:512 * (k + 1)],
                          pieces[k][4096 * r + 128 * ch: 4096 * r + 128 * (ch + 1), :],
                          self.sLd, wait=(wl or None) if k == 0 else None)
                p.wait("tensor", self.sLd, self.sLd.n)
                for k in range(4):
                    for fc in range(2):
                        p.op("tensor", lambda e, ch=ch, k=k, fc=fc, pi=pi: e.matmul(
                            ps[fc * 4 + k][:, :],
                            wkv[:, 256 * ch + 128 * fc: 256 * ch + 128 * (fc + 1)],
                            page[pi][:, 512 * k: 512 * (k + 1)],
                            start=(ch == 0), stop=(ch == 31)),
                            inc=self.sPE if (k == 3 and fc == 1) else None)
                mm_at[ch] = self.sPE.n
            mmv = self.sPE.n
            p.wait("scalar", self.sPE, mmv)
            a = r % 2
            for k in range(4):
                for fc in range(2):
                    ev = (kev, vev)[fc][k % 2]
                    dst = (outK, outV)[fc]
                    p.wait("scalar", self.sSt, self.sSt.n)
                    e1 = p.op("scalar", lambda e, k=k, fc=fc, ev=ev: e.activation(
                        ev, ps[fc * 4 + k][:, :], AF.Copy), inc=self.sA)
                    for h2 in range(2):
                        col = 1024 * k + 512 * h2 + 256 * a
                        p.dma("sync", dst[:, col:col + 256],
                              ev[:, 256 * h2: 256 * (h2 + 1)], self.sSt,
                              wait=[(self.sA, e1)] if h2 == 0 else None)
        self.psum_close()

    def attention(self, layer, cast_pend=None):
        self.fence()
        self.ctx_ready = {}
        p, dt = self.p, self.dt
        p.wait("tensor", self.sCast, self.cast_consts)
        p.wait("vector", self.sCast, self.cast_consts)
        ab, af = self.aB, self.aF
        qt = [ab[:, 512 * i:512 * (i + 1)] for i in range(3)]
        ptile = [ab[:, 1536 + 512 * i:1536 + 512 * (i + 1)] for i in range(4)]
        bset = [[ab[:, 3584 + 512 * (5 * pp + k): 3584 + 512 * (5 * pp + k + 1)]
                 for k in range(5)] for pp in range(2)]
        ksb = [ab[:, 8704 + 4096 * i: 8704 + 4096 * (i + 1)] for i in range(2)]
        vsb = [ab[:, 16896 + 4096 * i: 16896 + 4096 * (i + 1)] for i in range(2)]
        ctev = [ab[:, 25088 + 512 * i: 25088 + 512 * (i + 1)] for i in range(2)]
        bscr = [af[:, 512 * i:512 * (i + 1)] for i in range(2)]
        rrow = af[:, 1024:1536]
        rb = [af[:, 1536 + 512 * i:1536 + 512 * (i + 1)] for i in range(2)]
        pq0, pq1, pq2, pq3, pc0, pc1, psb0, psb1 = self.psum_open(
            [("pqA", [128, 512], F32), ("pqB", [128, 512], F32),
             ("pqC", [128, 512], F32), ("pqD", [128, 512], F32),
             ("pcA", [128, 512], F32), ("pcB", [128, 512], F32),
             ("psA", [128, 512], F32), ("psB", [128, 512], F32)])
        pqk = (pq0, pq1, pq2, pq3)
        pctx = (pc0, pc1)
        psumb = (psb0, psb1)
        kvready = 0
        if layer == 0:
            p.dma("sync", self.cF[:, 532:532 + 4096], dt["biastab"][:, :], self.sLd)
            p.dma("sync", ksb[0][:, :], dt["kvK0"][:, :], self.sLd)
            for kt in range(32):
                p.op("sync", lambda e, kt=kt: e.dma_start(
                    out=vsb[0][:, 128 * kt: 128 * (kt + 1)],
                    in_=dt["kvV0"][:, 128 * kt: 128 * (kt + 1)], transpose=True),
                    inc=self.sLd, k=16)
            kvready = self.sLd.n

        qk_last = {}
        ctx_mul = {}
        recip_c = {}
        kv_pi = 0
        oi = 0
        LEAD = 3
        for hl in range(16):
            pp = hl % 2
            if layer == 0:
                bs = bset[pp]
                p.wait("vector", self.sA, self.sA.n)
                bv = p.op("vector", lambda e, hl=hl, pp=pp: e.tensor_scalar(
                    bscr[pp], self.dm, self.slopes_sb[:, hl:hl + 1], None, ALU.mult),
                    inc=self.sV)
                p.wait("scalar", self.sV, bv)
                av = p.op("scalar", lambda e, bs=bs, pp=pp: e.activation(
                    bs[0], bscr[pp], AF.Exp), inc=self.sA)
                for r in range(4):
                    p.wait("vector", self.sA, av)
                    bv = p.op("vector", lambda e, r=r, hl=hl, pp=pp: e.tensor_scalar(
                        bscr[pp], self.dm, float(127 + 128 * r),
                        self.slopes_sb[:, hl:hl + 1], ALU.add, ALU.mult), inc=self.sV)
                    p.wait("vector", self.sV, bv)
                    bv = p.op("vector", lambda e, pp=pp: e.tensor_scalar(
                        bscr[pp], bscr[pp], 0.0, None, ALU.min), inc=self.sV)
                    p.wait("scalar", self.sV, bv)
                    av = p.op("scalar", lambda e, r=r, bs=bs, pp=pp: e.activation(
                        bs[1 + r], bscr[pp], AF.Exp), inc=self.sA)
                    p.wait("vector", self.sA, av)
                    p.op("vector", lambda e, r=r, bs=bs: e.tensor_tensor(
                        bs[1 + r], bs[1 + r], self.diagmask_b(r), ALU.mult),
                        inc=self.sV)
                bready = (self.sV.n, self.sA.n)
            if layer == 1 and hl % 4 == 0:
                bi = hl // 4
                kv_pi = bi % 2
                wl = [(self.sPE, self.sPE.n)] if bi >= 2 else None
                p.dma("sync", ksb[kv_pi][:, :],
                      dt["kv1K"][128 * bi:128 * (bi + 1), :], self.sLd, wait=wl)
                for kt in range(32):
                    p.op("sync", lambda e, kt=kt, bi=bi, pi=kv_pi: e.dma_start(
                        out=vsb[pi][:, 128 * kt: 128 * (kt + 1)],
                        in_=dt["kv1V"][128 * bi: 128 * bi + 128,
                                       128 * kt: 128 * (kt + 1)], transpose=True),
                        inc=self.sLd, k=16)
                kvready = self.sLd.n
            for s in range(NSTRIP):
                E = 4 * (s + 1)
                qi = oi % 3
                it2 = oi % 2
                if oi >= 3:
                    p.wait("sync", self.sPE, qk_last[oi - 3])
                p.op("sync", lambda e, s=s, hl=hl, qi=qi: e.dma_start(
                    out=qt[qi][:, :],
                    in_=dt["qtm"][512 * s: 512 * (s + 1), 128 * hl: 128 * (hl + 1)],
                    transpose=True), inc=self.sLd, k=16)
                qready = self.sLd.n
                p.wait("tensor", self.sLd, max(qready, kvready))
                if layer == 0:
                    p.wait("tensor", self.sV, bready[0])
                exp_done = {}
                sum_done = {}
                pready = {}
                war_ctx = ctx_mul.get(oi - 2, 0)
                war_sum = recip_c.get(oi - 2, 0)

                def emit_pv(kt):
                    bank = kt % 4
                    prs, prv = pready[kt]
                    p.wait("tensor", prs, prv)
                    if kt == 0:
                        p.wait("tensor", self.sV, war_ctx)
                        p.wait("tensor", self.sA, war_sum)
                    p.op("tensor", lambda e, kt=kt, bank=bank, pi=kv_pi, E=E,
                         it2=it2: e.matmul(
                             pctx[it2][:, :], vsb[pi][:, 128 * kt: 128 * (kt + 1)],
                             ptile[bank][:, :], start=(kt == 0), stop=(kt == E - 1),
                             skip_group_check=True),
                         inc=self.sPE if kt == E - 1 else None)
                    sm = p.op("tensor", lambda e, kt=kt, bank=bank, E=E, it2=it2:
                              e.matmul(psumb[it2][:, :], self.ones_full_b,
                                       ptile[bank][:, :], start=(kt == 0),
                                       stop=(kt == E - 1), skip_group_check=True),
                              inc=self.sPE)
                    sum_done[kt] = sm

                for kt in range(E):
                    bank = kt % 4
                    if kt >= 4:
                        p.wait("tensor", self.sA, exp_done[kt - 4])
                    qkv = p.op("tensor", lambda e, kt=kt, bank=bank, qi=qi,
                               pi=kv_pi: e.matmul(
                                   pqk[bank][:, :],
                                   ksb[pi][:, 128 * kt: 128 * (kt + 1)],
                                   qt[qi][:, :], start=True, stop=True,
                                   skip_group_check=True), inc=self.sPE)
                    p.wait("scalar", self.sPE, qkv)
                    if kt >= 4:
                        p.wait("scalar", self.sPE, sum_done.get(kt - 4, 0))
                    r = kt - 4 * s
                    if layer == 0 and r < 0:
                        bias_ap = self.biastab_sb[:, hl * 256 + s * 32 + kt:
                                                  hl * 256 + s * 32 + kt + 1]
                    else:
                        bias_ap = self.consts_sb[:, 0:1]
                    ev = p.op("scalar", lambda e, bank=bank, bias_ap=bias_ap:
                              e.activation(ptile[bank], pqk[bank][:, :], AF.Exp,
                                           bias=bias_ap), inc=self.sA)
                    exp_done[kt] = ev
                    if layer == 0:
                        btile = bset[pp][1 + r] if r >= 0 else bset[pp][0]
                        p.wait("vector", self.sA, ev)
                        mv = p.op("vector", lambda e, bank=bank, bt=btile:
                                  e.tensor_tensor(ptile[bank], ptile[bank], bt,
                                                  ALU.mult), inc=self.sV)
                        pready[kt] = (self.sV, mv)
                    elif r >= 0:
                        p.wait("vector", self.sA, ev)
                        mv = p.op("vector", lambda e, bank=bank, r=r:
                                  e.tensor_tensor(ptile[bank], ptile[bank],
                                                  self.diagmask_b(r), ALU.mult),
                                  inc=self.sV)
                        pready[kt] = (self.sV, mv)
                    else:
                        pready[kt] = (self.sA, ev)
                    if kt >= LEAD:
                        emit_pv(kt - LEAD)
                for kt in range(max(0, E - LEAD), E):
                    emit_pv(kt)
                sm = sum_done[E - 1]
                qk_last[oi] = sm
                p.wait("scalar", self.sPE, sm)
                p.wait("scalar", self.sV, ctx_mul.get(oi - 2, 0))
                l1 = p.op("scalar", lambda e, it2=it2: e.activation(
                    rrow, psumb[it2][:, :], AF.Ln), inc=self.sA)
                p.wait("scalar", self.sA, l1)
                rc = p.op("scalar", lambda e, it2=it2: e.activation(
                    rb[it2], rrow, AF.Exp, scale=-1.0), inc=self.sA)
                recip_c[oi] = rc
                p.wait("vector", self.sA, rc)
                p.wait("vector", self.sSt, self.sSt.n)
                mu = p.op("vector", lambda e, it2=it2: e.tensor_tensor(
                    ctev[it2], pctx[it2][:, :], rb[it2], ALU.mult), inc=self.sV)
                ctx_mul[oi] = mu
                kq = hl // 4
                rowb = ((hl % 4) * 8 + s) * 2 * 128
                piece = dt[f"ctxp{kq}"]
                p.dma("gpsimd", piece[rowb:rowb + 128, :],
                      ctev[it2][:, 0:256], self.sSt, wait=[(self.sV, mu)])
                p.dma("gpsimd", piece[rowb + 128:rowb + 256, :],
                      ctev[it2][:, 256:512], self.sSt)
                oi += 1
            if hl % 4 == 3:
                kq = hl // 4
                p.wait("gpsimd", self.sSt, self.sSt.n)
                cc = p.op("gpsimd", lambda e, kq=kq: e.collective_compute(
                    "AllGather", ALU.bypass, replica_groups=[list(range(NCORES))],
                    ins=[dt[f"ctxp{kq}"].ap().opt()],
                    outs=[dt[f"CTXALLp{kq}"].ap().opt()]), inc=self.sCC)
                p.wait("sync", self.sCC, cc)
                G4 = 4 * NSTRIP
                def dyncopy(e, kq=kq, G4=G4):
                    srcv = dt[f"CTXALLp{kq}"].ap().rearrange(
                        "(r g h p) c -> r g h p c", r=NCORES, g=G4, h=2, p=128)
                    dstv = dt[f"CTXSELFp{kq}"].ap().rearrange(
                        "(r g p) c -> r g p c", r=2, g=G4, p=128)
                    return e.dma_start(
                        out=dstv, in_=srcv[bass.ds(self._pb, 2), :,
                                           bass.ds(self._aa, 1), :, :])
                pr = p.op("sync", dyncopy, inc=self.sLd, k=16)
                self.ctx_ready[hl // 4] = pr
            if cast_pend:
                k0 = (len(cast_pend) * hl) // 16
                k1 = (len(cast_pend) * (hl + 1)) // 16
                if k1 > k0:
                    p.wait("gpsimd", self.sV, self.sV.n)
                    for chk in cast_pend[k0:k1]:
                        self.emit_cast_chunk(chk)
        self.psum_close()

    def exchange_ctx(self):
        self.fence()
        p, dt = self.p, self.dt
        cc = p.op("gpsimd", lambda e: e.collective_compute(
            "AllGather", ALU.bypass, replica_groups=[list(range(NCORES))],
            ins=[dt["ctxm"].ap().opt()], outs=[dt["CTXALL"].ap().opt()]),
            inc=self.sCC)
        p.wait("sync", self.sCC, cc)
        G = NSTRIP * 16
        def dyncopy(e):
            rv = e.partition_id()
            pb = e.snap(e.scalar_reg_alu(ALU.bitwise_and, rv, 0xFFFFFFFE),
                        min_val=0, max_val=6)
            aa = e.snap(e.scalar_reg_alu(ALU.bitwise_and, rv, 1),
                        min_val=0, max_val=1)
            srcv = dt["CTXALL"].ap().rearrange(
                "(r g h p) c -> r g h p c", r=NCORES, g=G, h=2, p=128)
            dstv = dt["CTXSELF"].ap().rearrange(
                "(r g p) c -> r g p c", r=2, g=G, p=128)
            return e.dma_start(out=dstv,
                               in_=srcv[bass.ds(pb, 2), :, bass.ds(aa, 1), :, :])
        p.op("sync", dyncopy, inc=self.sLd, k=16)
        return self.sLd.n

    def oproj(self, layer, wo_b, x_old, x_new, wname=None):
        self._o_mm = {}
        self.fence()
        for k in range(4):
            self.p.wait("sync", self.sLd, self.ctx_ready[k])
        p, dt = self.p, self.dt
        if wname:
            self.wait_wcast("scalar", wname)
        if layer == 1:
            self.wait_wcast("scalar", "bo_b")
        ab, af = self.aB, self.aF
        wo_sb = [ab[:, 16384 * i: 16384 * (i + 1)] for i in range(2)]
        ctx_sb = [ab[:, 32768 + 8192 * i: 32768 + 8192 * (i + 1)] for i in range(2)]
        xo = [af[:, 512 * i: 512 * (i + 1)] for i in range(2)]
        xn = [af[:, 1024 + 512 * i: 1024 + 512 * (i + 1)] for i in range(2)]
        po = self.psum_open([("po0", [128, 512], F32), ("po1", [128, 512], F32)])
        G = NSTRIP * 16
        boT = [ab[0:1, 49152 + 512 * i: 49152 + 512 * (i + 1)] for i in range(2)]
        self._bod = 0
        ei = 0
        for op2 in range(4):
            p.wait("scalar", self.sPE, self.sPE.n)
            for osl in range(2):
                os = 2 * op2 + osl
                for dc in range(32):
                    p.dma("scalar", wo_sb[osl][:, 512 * dc: 512 * (dc + 1)],
                          wo_b[128 * dc: 128 * (dc + 1), 512 * os: 512 * (os + 1)],
                          self.sLd2)
                if layer == 1:
                    p.dma("scalar", boT[osl],
                          self.wb["bo_b"][0:1, 512 * os: 512 * (os + 1)], self.sLd2)
            wod = self.sLd2.n
            self._bod = wod
            for s in range(NSTRIP):
                ci = s % 2
                if s >= 2:
                    p.wait("sync", self.sPE, self._o_mm.get((op2, s - 2), 0))
                elif op2 > 0:
                    p.wait("sync", self.sPE, self._o_mm.get((op2 - 1, NSTRIP - 2), 0))
                for rank in range(2):
                    for hl in range(16):
                        kq, hq = hl // 4, hl % 4
                        rowb = (rank * 32 + hq * 8 + s) * 128
                        p.dma("sync",
                              ctx_sb[ci][:, (rank * 16 + hl) * 256:
                                         (rank * 16 + hl) * 256 + 256],
                              dt[f"CTXSELFp{kq}"][rowb:rowb + 128, :], self.sLd)
                ctxd = self.sLd.n
                for qsub in range(2):
                    rb = 2 * s + qsub
                    for osl in range(2):
                        os = 2 * op2 + osl
                        bank = po[ei % 2]
                        p.wait("tensor", self.sLd, ctxd)
                        p.wait("tensor", self.sLd2, wod)
                        p.wait("tensor", self.sV, self.sV.n)
                        for dc in range(32):
                            last = (dc == 31 and layer == 0)
                            p.op("tensor", lambda e, dc=dc, qsub=qsub, osl=osl, s=s,
                                 bank=bank, last=last: e.matmul(
                                     bank[:, :],
                                     ctx_sb[s % 2][:, 256 * dc + 128 * qsub:
                                                   256 * dc + 128 * (qsub + 1)],
                                     wo_sb[osl][:, 512 * dc: 512 * (dc + 1)],
                                     start=(dc == 0), stop=last),
                                 inc=self.sPE if last else None)
                        if layer == 1:
                            p.wait("tensor", self.sLd2, self._bod)
                            p.op("tensor", lambda e, bank=bank, osl=osl: e.matmul(
                                bank[:, :], self.ones_row_b, boT[osl],
                                start=False, stop=True, skip_group_check=True),
                                inc=self.sPE)
                        mmv = self.sPE.n
                        xi = ei % 2
                        p.dma("sync", xo[xi],
                              x_old[128 * rb: 128 * (rb + 1), 512 * os: 512 * (os + 1)],
                              self.sLd, wait=[(self.sV, self.sV.n)])
                        xod = self.sLd.n
                        p.wait("vector", self.sPE, mmv)
                        p.wait("vector", self.sLd, xod)
                        p.wait("vector", self.sSt, self.sSt.n)
                        mv = p.op("vector", lambda e, bank=bank, xi=xi: e.tensor_tensor(
                            xn[xi], bank[:, :], xo[xi], ALU.add), inc=self.sV)
                        p.dma("sync", x_new[128 * rb: 128 * (rb + 1),
                                            512 * os: 512 * (os + 1)],
                              xn[xi], self.sSt, wait=[(self.sV, mv)])
                        ei += 1
                self._o_mm[(op2, s)] = self.sPE.n
        self.psum_close()

    def ffn0(self, x_old, x_new):
        import os as _os
        LVL = int(_os.environ.get("FFN0_LVL", "7"))
        self.fence()
        p, dt = self.p, self.dt
        self.wait_wcast("scalar", "awT", "bwT")
        self.wait_wcast("gpsimd", "expmat")
        ab, af = self.aB, self.aF
        staging = ab[:, 0:16384]
        aw_sb = ab[:, 16384:32768]
        bw_sb = ab[:, 32768:49152]
        z_bf = [ab[:, 49152 + 512 * i: 49152 + 512 * (i + 1)] for i in range(4)]
        zg_bf = [ab[:, 51200 + 512 * i: 51200 + 512 * (i + 1)] for i in range(4)]
        emat_sb = self.cB[0:64, 4737:5249]
        gate_fm_t = self.cB[0:64, 5249:5761]
        xf = [af[:, 0:4096], af[:, 4096:8192]]
        xo = [af[:, 512 * i: 512 * (i + 1)] for i in range(2)]
        xn = [af[:, 1024 + 512 * i: 1536 + 512 * i] for i in range(2)]
        lnf = af[:, 8192:12288]
        stats2 = [af[:, 12288:12296], af[:, 13864:13872]]
        fm32 = af[:, 12296:12808]
        qr_sb = af[0:16, 12808:13320]
        s0 = af[:, 13320:13576]
        scr = af[:, 13576:13832]
        mrow = af[:, 13832:13864]
        junk = lnf
        ps = self.psum_open([(f"pf{i}", [128, 512], F32) for i in range(8)])
        wq_sb = self.cF[:, 532:1044]
        keys_sb = self.cF[0:16, 1044:1108]
        for ch in range(32):
            p.dma("sync", wq_sb[:, 16 * ch: 16 * (ch + 1)],
                  dt["wqwT"][128 * ch: 128 * (ch + 1), :], self.sLd)
        p.dma("sync", keys_sb, dt["keysT"][:, :], self.sLd)
        p.dma("gpsimd", emat_sb, dt["expmat"][:, :], self.sCast)
        for ch in range(32):
            p.dma("scalar", aw_sb[:, 512 * ch: 512 * (ch + 1)],
                  self.wb["awT"][128 * ch: 128 * (ch + 1), :], self.sLd2)
        for nc4 in range(4):
            p.dma("scalar", bw_sb[:, 4096 * nc4: 4096 * (nc4 + 1)],
                  self.wb["bwT"][128 * nc4: 128 * (nc4 + 1), :], self.sLd2)
        wd = (self.sLd.n, self.sLd2.n, self.sCast.n)
        blk = lambda t, b: t[:, 64 * b:64 * (b + 1)]
        for g in range(4):
            qr_bank, sc_bank, tr_bank = ps[0], ps[1], ps[2]
            for tt in range(4):
                i = 4 * g + tt
                pi = i % 2
                p.dma("sync", xf[pi][:, :], x_old[128 * i:128 * (i + 1), :],
                      self.sLd, wait=[(self.sPE, self.sPE.n), (self.sV, self.sV.n),
                                      (self.sA, self.sA.n)])
                ld = self.sLd.n
                p.wait("vector", self.sLd, ld)
                p.wait("scalar", self.sLd, ld)
                av = self.ln_tile(xf[pi], stats2[pi], xf[1 - pi], lnf)
                p.wait("tensor", self.sV, av)
                NOTR = int(_os.environ.get("FFN0_NOTR", "0"))
                for grp in range(8):
                    if NOTR >= 2:
                        break
                    pe_l = None
                    for j in range(4):
                        ch = grp * 4 + j
                        pe_l = p.op("tensor", lambda e, ch=ch, j=j: e.transpose(
                            tr_bank[:, 128 * j:128 * (j + 1)],
                            lnf[:, 128 * ch:128 * (ch + 1)], self.idF[:, :]),
                            inc=self.sPE)
                    p.wait("vector", self.sPE, pe_l)
                    p.wait("vector", self.sA, self.sA.n)
                    for j in range(4):
                        p.op("vector", lambda e, j=j: e.tensor_copy(
                            fm32[:, 128 * j:128 * (j + 1)],
                            tr_bank[:, 128 * j:128 * (j + 1)]), inc=self.sV)
                    vv = self.sV.n
                    p.wait("scalar", self.sV, vv)
                    for j in range(4):
                        ch = grp * 4 + j
                        p.op("scalar", lambda e, ch=ch, j=j, tt=tt: e.activation(
                            staging[:, 512 * ch + 128 * tt: 512 * ch + 128 * (tt + 1)],
                            fm32[:, 128 * j:128 * (j + 1)], AF.Copy), inc=self.sA)
                    p.wait("tensor", self.sV, vv)
                    p.wait("tensor", self.sLd, wd[0])
                    for j in range(4):
                        if NOTR >= 1:
                            break
                        ch = grp * 4 + j
                        p.op("tensor", lambda e, ch=ch, j=j, tt=tt, qr_bank=qr_bank:
                             e.matmul(qr_bank[0:16, 128 * tt:128 * (tt + 1)],
                                      wq_sb[:, 16 * ch:16 * (ch + 1)],
                                      fm32[:, 128 * j:128 * (j + 1)],
                                      start=(ch == 0), stop=(ch == 31)),
                             inc=self.sPE if ch == 31 else None)
                    p.wait("tensor", self.sA, self.sA.n)
                    p.wait("tensor", self.sV, self.sV.n)
            if LVL <= 2:
                continue
            qv = self.sPE.n
            p.wait("vector", self.sPE, qv)
            ev = p.op("vector", lambda e, qr_bank=qr_bank: e.tensor_copy(
                qr_sb, qr_bank[0:16, :]), inc=self.sV)
            p.wait("tensor", self.sV, ev)
            for tt in range(4):
                p.op("tensor", lambda e, tt=tt, sc_bank=sc_bank: e.matmul(
                    sc_bank[:, 64 * tt: 64 * (tt + 1)],
                    qr_sb[:, 128 * tt:128 * (tt + 1)], keys_sb,
                    start=True, stop=True), inc=self.sPE)
            sv = self.sPE.n
            p.wait("vector", self.sPE, sv)
            sev = p.op("vector", lambda e, sc_bank=sc_bank: e.tensor_copy(
                s0, sc_bank[:, 0:256]), inc=self.sV)
            p.wait("vector", self.sV, sev)
            cp = p.op("vector", lambda e: e.tensor_copy(scr, s0), inc=self.sV)
            p.wait("vector", self.sV, cp)
            if LVL <= 3:
                continue
            for it in range(TOPK):
                for b in range(4):
                    p.op("vector", lambda e, b=b: e.tensor_reduce(
                        mrow[:, 8 + b:9 + b], blk(scr, b), AX.X, ALU.max), inc=self.sV)
                p.wait("vector", self.sV, self.sV.n)
                if it == 0:
                    mx = p.op("vector", lambda e: e.tensor_copy(
                        mrow[:, 0:4], mrow[:, 8:12]), inc=self.sV)
                    p.wait("vector", self.sV, mx)
                if it == TOPK - 1:
                    tv = p.op("vector", lambda e: e.tensor_copy(
                        mrow[:, 4:8], mrow[:, 8:12]), inc=self.sV)
                    p.wait("vector", self.sV, tv)
                    break
                for b in range(4):
                    p.op("vector", lambda e, b=b: e.tensor_scalar(
                        blk(junk, b), blk(scr, b), mrow[:, 8 + b:9 + b], NEG,
                        ALU.is_equal, ALU.mult), inc=self.sV)
                p.wait("vector", self.sV, self.sV.n)
                for b in range(4):
                    p.op("vector", lambda e, b=b: e.tensor_tensor(
                        blk(scr, b), blk(scr, b), blk(junk, b), ALU.add), inc=self.sV)
                p.wait("vector", self.sV, self.sV.n)
            if LVL <= 4:
                continue
            ng = p.op("vector", lambda e: e.tensor_scalar(
                mrow[:, 12:16], mrow[:, 0:4], -1.0, None, ALU.mult), inc=self.sV)
            p.wait("scalar", self.sV, ng)
            gate_f = scr
            for b in range(4):
                p.op("scalar", lambda e, b=b: e.activation(
                    blk(gate_f, b), blk(s0, b), AF.Exp,
                    bias=mrow[:, 12 + b:13 + b]), inc=self.sA)
            p.wait("vector", self.sA, self.sA.n)
            for b in range(4):
                p.op("vector", lambda e, b=b: e.tensor_scalar(
                    blk(junk, b), blk(s0, b), mrow[:, 4 + b:5 + b], None,
                    ALU.is_ge), inc=self.sV)
            p.wait("vector", self.sV, self.sV.n)
            for b in range(4):
                p.op("vector", lambda e, b=b: e.tensor_tensor(
                    blk(gate_f, b), blk(gate_f, b), blk(junk, b), ALU.mult),
                    inc=self.sV)
            p.wait("vector", self.sV, self.sV.n)
            for b in range(4):
                p.op("vector", lambda e, b=b: e.tensor_reduce(
                    mrow[:, 16 + b:17 + b], blk(gate_f, b), AX.X, ALU.add), inc=self.sV)
            p.wait("vector", self.sV, self.sV.n)
            rv = p.op("vector", lambda e: e.reciprocal(mrow[:, 20:24], mrow[:, 16:20]),
                      inc=self.sV)
            p.wait("vector", self.sV, rv)
            for b in range(4):
                p.op("vector", lambda e, b=b: e.tensor_scalar(
                    blk(junk, b), blk(gate_f, b), mrow[:, 20 + b:21 + b],
                    MOE_SCALING, ALU.mult, ALU.mult), inc=self.sV)
            gv = self.sV.n
            p.wait("tensor", self.sV, gv)
            for tt in range(4):
                p.op("tensor", lambda e, tt=tt: e.transpose(
                    tr_bank[0:64, 128 * tt:128 * (tt + 1)],
                    junk[:, 64 * tt: 64 * (tt + 1)], self.idF[:, :]),
                    inc=self.sPE)
            gtv = self.sPE.n
            p.wait("vector", self.sPE, gtv)
            gfm = p.op("vector", lambda e: e.tensor_copy(
                gate_fm_t, tr_bank[0:64, 0:512]), inc=self.sV)
            if LVL <= 5:
                continue
            p.wait("tensor", self.sV, gfm)
            p.wait("tensor", self.sLd2, wd[1])
            p.wait("tensor", self.sCast, wd[2])
            for nc4 in range(4):
                for ch in range(32):
                    p.op("tensor", lambda e, nc4=nc4, ch=ch: e.matmul(
                        ps[4 + nc4][:, :],
                        aw_sb[:, 512 * ch + 128 * nc4: 512 * ch + 128 * (nc4 + 1)],
                        staging[:, 512 * ch: 512 * (ch + 1)],
                        start=(ch == 0), stop=(ch == 31)),
                        inc=self.sPE if ch == 31 else None)
                p.op("tensor", lambda e, nc4=nc4: e.matmul(
                    ps[nc4][:, :], emat_sb[:, 128 * nc4:128 * (nc4 + 1)],
                    gate_fm_t, start=True, stop=True), inc=self.sPE)
            zv = self.sPE.n
            p.wait("scalar", self.sPE, zv)
            p.wait("scalar", self.sV, self.sV.n)
            for nc4 in range(4):
                p.op("scalar", lambda e, nc4=nc4: e.activation(
                    z_bf[nc4], ps[4 + nc4][:, :], AF.Copy), inc=self.sA)
            p.wait("vector", self.sA, self.sA.n)
            for nc4 in range(4):
                p.op("vector", lambda e, nc4=nc4: e.tensor_tensor(
                    zg_bf[nc4], z_bf[nc4], ps[nc4][:, :], ALU.mult), inc=self.sV)
            zgv = self.sV.n
            if LVL <= 6:
                continue
            for ts in range(4):
                rb = 4 * g + ts
                for os in range(8):
                    bank = ps[4 + (os % 2)]
                    p.wait("tensor", self.sV, max(zgv, self.sV.n))
                    for nc4 in range(4):
                        p.op("tensor", lambda e, nc4=nc4, ts=ts, os=os, bank=bank:
                             e.matmul(bank[:, :],
                                      zg_bf[nc4][:, 128 * ts: 128 * (ts + 1)],
                                      bw_sb[:, 4096 * nc4 + 512 * os:
                                            4096 * nc4 + 512 * (os + 1)],
                                      start=(nc4 == 0), stop=(nc4 == 3)),
                             inc=self.sPE if nc4 == 3 else None)
                    mmv = self.sPE.n
                    xi = os % 2
                    p.dma("sync", xo[xi],
                          x_old[128 * rb: 128 * (rb + 1), 512 * os: 512 * (os + 1)],
                          self.sLd, wait=[(self.sV, self.sV.n)])
                    xod = self.sLd.n
                    p.wait("vector", self.sPE, mmv)
                    p.wait("vector", self.sLd, xod)
                    p.wait("vector", self.sSt, self.sSt.n)
                    mv = p.op("vector", lambda e, bank=bank, xi=xi: e.tensor_tensor(
                        xn[xi], bank[:, :], xo[xi], ALU.add), inc=self.sV)
                    p.dma("sync", x_new[128 * rb: 128 * (rb + 1),
                                        512 * os: 512 * (os + 1)],
                          xn[xi], self.sSt, wait=[(self.sV, mv)])
        self.psum_close()

    def ffn1(self, x_old, x_new):
        self.fence()
        p, dt = self.p, self.dt
        self.wait_wcast("scalar", "w1ppT", "w3ppT", "b1ppT", "b3ppT", "w2ppT", "b2T")
        self.wait_wcast("sync", "a13sb", "a2sb")
        ab, af = self.aB, self.aF
        staging = ab[:, 0:16384]
        xfold = ab[:, 16384:20480]
        hfold = ab[:, 20480:31744]
        h4 = [ab[:, 31744 + 512 * i: 31744 + 512 * (i + 1)] for i in range(4)]
        w1t = [ab[:, 33792 + 4096 * i: 33792 + 4096 * (i + 1)] for i in range(2)]
        w3t = [ab[:, 41984 + 4096 * i: 41984 + 4096 * (i + 1)] for i in range(2)]
        a13buf = ab[:, 50176:50304]
        a2sbuf = ab[:, 50304:50480]
        zrow = ab[0:2, 50480:50992]
        zrow3 = ab[0:2, 50992:51504]
        z2row = ab[0:2, 51504:52016]
        b1t = [ab[0:2, 52016 + 512 * i: 52016 + 512 * (i + 1)] for i in range(2)]
        b3t = [ab[0:2, 53040 + 512 * i: 53040 + 512 * (i + 1)] for i in range(2)]
        b2t = [ab[0:2, 54064 + 512 * i: 54064 + 512 * (i + 1)] for i in range(2)]
        w2tiles = [staging[:, 0:11264], ab[:, 33792:45056]]
        xf = [af[:, 0:4096], af[:, 4096:8192]]
        xo = [af[:, 512 * i: 512 * (i + 1)] for i in range(2)]
        xn = [af[:, 1024 + 512 * i: 1536 + 512 * i] for i in range(2)]
        lnf = af[:, 8192:12288]
        stats2 = [af[:, 12288:12296], af[:, 13832:13840]]
        gsc = [af[:, 12296:12808], af[:, 12808:13320]]
        usc = [af[:, 13320:13832], af[:, 8192:8704]]
        ps = self.psum_open([(f"pq{i}", [128, 512], F32) for i in range(6)]
                            + [("pzz", [128, 1024], F32)])
        pz1 = ps[6]
        p.dma("sync", a2sbuf, self.wb["a2sb"][:, :], self.sLd)
        p.dma("sync", a13buf, self.wb["a13sb"][:, :], self.sLd)
        a2ld = self.sLd.n
        for g in range(4):
            for tt in range(4):
                i = 4 * g + tt
                pi = i % 2
                p.dma("sync", xf[pi][:, :], x_old[128 * i:128 * (i + 1), :],
                      self.sLd, wait=[(self.sPE, self.sPE.n), (self.sV, self.sV.n),
                                      (self.sA, self.sA.n)])
                ld = self.sLd.n
                p.wait("vector", self.sLd, ld)
                p.wait("scalar", self.sLd, ld)
                av = self.ln_tile(xf[pi], stats2[pi], xf[1 - pi], lnf)
                p.wait("tensor", self.sV, av)
                a_done = {}
                for grp in range(8):
                    trb = ps[4 + grp % 2]
                    if grp >= 2:
                        p.wait("tensor", self.sA, a_done[grp - 2])
                    else:
                        p.wait("tensor", self.sA, self.sA.n)
                    pe_l = None
                    for j in range(4):
                        ch = grp * 4 + j
                        pe_l = p.op("tensor", lambda e, ch=ch, j=j, trb=trb:
                                    e.transpose(trb[:, 128 * j:128 * (j + 1)],
                                                lnf[:, 128 * ch:128 * (ch + 1)],
                                                self.idF[:, :]), inc=self.sPE)
                    p.wait("scalar", self.sPE, pe_l)
                    for j in range(4):
                        ch = grp * 4 + j
                        p.op("scalar", lambda e, ch=ch, j=j, tt=tt, trb=trb:
                             e.activation(
                                 staging[:, 512 * ch + 128 * tt:
                                         512 * ch + 128 * (tt + 1)],
                                 trb[:, 128 * j:128 * (j + 1)], AF.Copy),
                             inc=self.sA)
                    a_done[grp] = self.sA.n
            stv = self.sA.n
            p.wait("vector", self.sA, stv)
            for step in range(3):
                for cf in range(8):
                    src_blk = staging[:, 512 * (cf + 8 * (step + 1)):
                                      512 * (cf + 8 * (step + 1) + 1)]
                    if step == 0:
                        p.op("vector", lambda e, cf=cf, src_blk=src_blk:
                             e.tensor_tensor(xfold[:, 512 * cf: 512 * (cf + 1)],
                                             staging[:, 512 * cf: 512 * (cf + 1)],
                                             src_blk, ALU.add), inc=self.sV)
                    else:
                        p.op("vector", lambda e, cf=cf, src_blk=src_blk:
                             e.tensor_tensor(xfold[:, 512 * cf: 512 * (cf + 1)],
                                             xfold[:, 512 * cf: 512 * (cf + 1)],
                                             src_blk, ALU.add), inc=self.sV)
                p.wait("vector", self.sV, self.sV.n)
            foldv = self.sV.n
            p.wait("tensor", self.sLd, a2ld)
            p.wait("tensor", self.sA, stv)
            for ch in range(32):
                p.op("tensor", lambda e, ch=ch: e.matmul(
                    pz1[0:2, 0:512], a13buf[:, 4 * ch:4 * ch + 2],
                    staging[:, 512 * ch: 512 * (ch + 1)],
                    start=(ch == 0), stop=(ch == 31), skip_group_check=True),
                    inc=None)
                p.op("tensor", lambda e, ch=ch: e.matmul(
                    pz1[0:2, 512:1024], a13buf[:, 4 * ch + 2:4 * ch + 4],
                    staging[:, 512 * ch: 512 * (ch + 1)],
                    start=(ch == 0), stop=(ch == 31), skip_group_check=True),
                    inc=self.sPE if ch == 31 else None)
            z1v = self.sPE.n
            p.wait("scalar", self.sPE, z1v)
            p.op("scalar", lambda e: e.activation(zrow, pz1[0:2, 0:512], AF.Copy,
                                                  scale=LORA_SC), inc=self.sA)
            p.op("scalar", lambda e: e.activation(zrow3, pz1[0:2, 512:1024], AF.Copy,
                                                  scale=LORA_SC), inc=self.sA)
            zev = self.sA.n
            p.wait("tensor", self.sA, zev)
            first_z2 = True
            pend = None
            hv_at = {}
            guv_at = {}
            aev_at = {}
            for jc in range(22):
                wp = jc % 2
                p.wait("scalar", self.sPE, guv_at.get(jc - 2, 0))
                for kc in range(8):
                    p.dma("scalar", w1t[wp][:, 512 * kc:512 * (kc + 1)],
                          self.wb["w1ppT"][128 * kc:128 * (kc + 1),
                                           512 * jc:512 * (jc + 1)], self.sLd2)
                    p.dma("scalar", w3t[wp][:, 512 * kc:512 * (kc + 1)],
                          self.wb["w3ppT"][128 * kc:128 * (kc + 1),
                                           512 * jc:512 * (jc + 1)], self.sLd2)
                p.dma("scalar", b1t[wp], self.wb["b1ppT"][:, 512 * jc:512 * (jc + 1)],
                      self.sLd2)
                p.dma("scalar", b3t[wp], self.wb["b3ppT"][:, 512 * jc:512 * (jc + 1)],
                      self.sLd2)
                wld = self.sLd2.n
                p.wait("tensor", self.sLd2, wld)
                p.wait("tensor", self.sV, foldv)
                for r4 in range(4):
                    fch = jc * 4 + r4
                    bank_g, bank_u = ps[0 + (fch % 2)], ps[2 + (fch % 2)]
                    p.wait("tensor", self.sA, aev_at.get(fch - 2, 0))
                    for kc in range(8):
                        p.op("tensor", lambda e, kc=kc, r4=r4, wp=wp, bank_g=bank_g:
                             e.matmul(bank_g[:, :],
                                      w1t[wp][:, 512 * kc + 128 * r4:
                                              512 * kc + 128 * (r4 + 1)],
                                      xfold[:, 512 * kc:512 * (kc + 1)],
                                      start=(kc == 0), stop=False,
                                      skip_group_check=True), inc=None)
                    p.op("tensor", lambda e, r4=r4, wp=wp, bank_g=bank_g: e.matmul(
                        bank_g[:, :], b1t[wp][:, 128 * r4:128 * (r4 + 1)], zrow,
                        start=False, stop=True, skip_group_check=True),
                        inc=self.sPE)
                    if pend is not None:
                        pr4, phv, pfch = pend
                        p.wait("tensor", self.sV, phv)
                        p.op("tensor", lambda e, first=first_z2, pr4=pr4, pfch=pfch:
                             e.matmul(pz1[0:2, 0:512],
                                      a2sbuf[:, 2 * pfch:2 * (pfch + 1)], h4[pr4],
                                      start=first, stop=False,
                                      skip_group_check=True), inc=None)
                        first_z2 = False
                        pend = None
                    for kc in range(8):
                        p.op("tensor", lambda e, kc=kc, r4=r4, wp=wp, bank_u=bank_u:
                             e.matmul(bank_u[:, :],
                                      w3t[wp][:, 512 * kc + 128 * r4:
                                              512 * kc + 128 * (r4 + 1)],
                                      xfold[:, 512 * kc:512 * (kc + 1)],
                                      start=(kc == 0), stop=False,
                                      skip_group_check=True), inc=None)
                    p.op("tensor", lambda e, r4=r4, wp=wp, bank_u=bank_u: e.matmul(
                        bank_u[:, :], b3t[wp][:, 128 * r4:128 * (r4 + 1)], zrow3,
                        start=False, stop=True, skip_group_check=True),
                        inc=self.sPE)
                    guv = self.sPE.n
                    p.wait("scalar", self.sPE, guv)
                    gp = fch % 2
                    p.wait("scalar", self.sV, hv_at.get(fch - 2, 0))
                    sg = p.op("scalar", lambda e, bank_g=bank_g, gp=gp: e.activation(
                        gsc[gp], bank_g[:, :], AF.Silu), inc=self.sA)
                    su = p.op("scalar", lambda e, bank_u=bank_u, gp=gp: e.activation(
                        usc[gp], bank_u[:, :], AF.Copy), inc=self.sA)
                    aev_at[fch] = su
                    p.wait("vector", self.sA, su)
                    hv = p.op("vector", lambda e, r4=r4, gp=gp: e.tensor_tensor(
                        h4[r4], gsc[gp], usc[gp], ALU.mult), inc=self.sV)
                    hv_at[fch] = hv
                    pend = (r4, hv, fch)
                guv_at[jc] = self.sPE.n
                f1 = p.op("vector", lambda e, jc=jc: e.tensor_tensor(
                    hfold[:, 512 * jc:512 * (jc + 1)], h4[0], h4[1], ALU.add),
                    inc=self.sV)
                p.wait("vector", self.sV, f1)
                f2 = p.op("vector", lambda e, jc=jc: e.tensor_tensor(
                    hfold[:, 512 * jc:512 * (jc + 1)],
                    hfold[:, 512 * jc:512 * (jc + 1)], h4[2], ALU.add), inc=self.sV)
                p.wait("vector", self.sV, f2)
                p.op("vector", lambda e, jc=jc: e.tensor_tensor(
                    hfold[:, 512 * jc:512 * (jc + 1)],
                    hfold[:, 512 * jc:512 * (jc + 1)], h4[3], ALU.add), inc=self.sV)
            pr4, phv, pfch = pend
            p.wait("tensor", self.sV, phv)
            p.op("tensor", lambda e, pr4=pr4, pfch=pfch: e.matmul(
                pz1[0:2, 0:512], a2sbuf[:, 2 * pfch:2 * (pfch + 1)], h4[pr4],
                start=False, stop=True, skip_group_check=True), inc=self.sPE)
            pend = None
            hforv = self.sV.n
            z2v = self.sPE.n
            p.wait("scalar", self.sPE, z2v)
            z2e = p.op("scalar", lambda e: e.activation(z2row, pz1[0:2, 0:512],
                                                        AF.Copy, scale=LORA_SC),
                       inc=self.sA)
            p.wait("vector", self.sA, z2e)
            os_mm = {}
            for os in range(8):
                wpp = os % 2
                p.wait("scalar", self.sPE, os_mm.get(os - 2, self.sPE.n if os < 2 else 0))
                for kc in range(22):
                    p.dma("scalar", w2tiles[wpp][:, 512 * kc:512 * (kc + 1)],
                          self.wb["w2ppT"][128 * kc:128 * (kc + 1),
                                           512 * os:512 * (os + 1)], self.sLd2)
                p.dma("scalar", b2t[wpp],
                      self.wb["b2T"][:, 512 * os:512 * (os + 1)], self.sLd2)
                w2ld = self.sLd2.n
                for ts in range(4):
                    rb = 4 * g + ts
                    bank = ps[4 + (ts % 2)]
                    p.wait("tensor", self.sLd2, w2ld)
                    if os == 0 and ts == 0:
                        p.wait("tensor", self.sV, hforv)
                    p.wait("tensor", self.sV, self.sV.n)
                    for kc in range(22):
                        p.op("tensor", lambda e, kc=kc, ts=ts, os=os, bank=bank,
                             wpp=wpp: e.matmul(
                                 bank[:, :],
                                 hfold[:, 512 * kc + 128 * ts:
                                       512 * kc + 128 * (ts + 1)],
                                 w2tiles[wpp][:, 512 * kc:512 * (kc + 1)],
                                 start=(kc == 0), stop=False,
                                 skip_group_check=True), inc=None)
                    p.op("tensor", lambda e, bank=bank, ts=ts, wpp=wpp: e.matmul(
                        bank[:, :], z2row[:, 128 * ts:128 * (ts + 1)],
                        b2t[wpp], start=False, stop=True, skip_group_check=True),
                        inc=self.sPE)
                    mmv = self.sPE.n
                    xi = ts % 2
                    p.dma("sync", xo[xi],
                          x_old[128 * rb: 128 * (rb + 1), 512 * os: 512 * (os + 1)],
                          self.sLd, wait=[(self.sV, self.sV.n)])
                    xod = self.sLd.n
                    p.wait("vector", self.sPE, mmv)
                    p.wait("vector", self.sLd, xod)
                    p.wait("vector", self.sSt, self.sSt.n)
                    mv = p.op("vector", lambda e, bank=bank, xi=xi: e.tensor_tensor(
                        xn[xi], bank[:, :], xo[xi], ALU.add), inc=self.sV)
                    p.dma("sync", x_new[128 * rb: 128 * (rb + 1),
                                        512 * os: 512 * (os + 1)],
                          xn[xi], self.sSt, wait=[(self.sV, mv)])
                os_mm[os] = self.sPE.n
        self.psum_close()

    def finish(self, taps=None):
        p, dt = self.p, self.dt
        self.fence()
        if taps:
            for name, h in taps:
                o = self.nc.dram_tensor(name, list(h.shape), h.dtype,
                                        kind="ExternalOutput")
                p.dma("sync", o[tuple(slice(None) for _ in h.shape)],
                      h[tuple(slice(None) for _ in h.shape)], self.sSt)
        p.wait("sync", self.sSt)
        p.wait("sync", self.sLd)
        p.wait("sync", self.sLd2)
        p.wait("sync", self.sCast)

    def build(self):
        nc = self.nc
        with self.es:
            sem_cms = []
            for s in self.p.sems:
                cm = nc.semaphore(s.name)
                s.h = cm.__enter__()
                sem_cms.append(cm)
            with nc.Block() as block:
                self.p.build_into(block)
            for cm in reversed(sem_cms):
                cm.__exit__(None, None, None)
        return nc


PHASES_ALL = ("ln1", "qkv0", "attn0", "oproj0", "ffn0",
              "ln3", "qkv1", "attn1", "oproj1", "ffn1")


def build_program(phases=PHASES_ALL, cut=0, taps=()):
    nc = bass.Bass("TRN2", target_bir_lowering=False)
    b = Builder(nc, cut=cut)
    b.prologue()
    if cut in (0, 1):
        b.emit_casts(["w0q", "w0kv"])
    if cut == 2:
        b.emit_casts(b._order_l1)
    dt = b.dt
    for ph in phases:
        if ph == "ln1":
            b.ln_phase("x0")
        elif ph == "ex1" or ph == "ex3":
            b.exchange_lnx()
        elif ph == "qkv0":
            b.qproj(b.wb["w0q"], dt["qtm"], wname="w0q")
            b.kvproj(b.wb["w0kv"], [dt[f"LNXPAIRp{k}"] for k in range(4)],
                     [0, 1], dt["kvK0"], dt["kvV0"], wname="w0kv")
        elif ph == "attn0":
            b.p.wait("gpsimd", b.sLd, b.sLd.n)
            b.attention(0, cast_pend=b.cast_chunks(b._order_l0 + b._order_l1))
        elif ph == "exc0" or ph == "exc1":
            b.exchange_ctx()
        elif ph == "oproj0":
            b.oproj(0, b.wb["w0o"], dt["x0"], dt["xA"], wname="w0o")
        elif ph == "ffn0":
            b.ffn0(dt["xA"], dt["xB"] if cut == 0 else dt["xB_out"])
        elif ph == "ln3":
            b.ln_phase("xB")
        elif ph == "qkv1":
            b.qproj(b.wb["w1q_attn"], dt["qtm"], wname="w1q_attn")
            for bi in range(4):
                b.kvproj(b.wb["w1kv"], [dt[f"LNXALLp{k}"] for k in range(4)],
                         [2 * bi, 2 * bi + 1],
                         dt["kv1K"][128 * bi:128 * (bi + 1), :],
                         dt["kv1V"][128 * bi:128 * (bi + 1), :], wname="w1kv")
        elif ph == "attn1":
            b.attention(1)
        elif ph == "oproj1":
            b.oproj(1, b.wb["w1o"], dt["xB"], dt["xC"], wname="w1o")
        elif ph == "ffn1":
            b.ffn1(dt["xC"], dt["x_out"])
        else:
            raise ValueError(ph)
    b.finish(taps=[(n + "_tap", dt[n]) for n in taps])
    return b.build()


def kernel(**inputs):
    nc = build_program()
    in_maps = [host_prep(inputs, c) for c in range(NCORES)]
    res = run_bass_kernel_spmd(nc, in_maps, core_ids=list(range(NCORES)))
    return assemble_out(res.results)


# version 19
# speedup vs baseline: 1.1324x; 1.0214x over previous
import contextlib
import math

import numpy as np

import concourse.bass as bass
import concourse.mybir as mybir
from concourse.bass_utils import run_bass_kernel_spmd

F32 = mybir.dt.float32
BF16 = mybir.dt.bfloat16
AF = mybir.ActivationFunctionType
ALU = mybir.AluOpType
AX = mybir.AxisListType

NCORES = 8
B, S, D = 4, 4096, 4096
NH, HD = 32, 128
T = 2048
NSTRIP = 8
QSCALE = 1.0 / math.sqrt(HD)
MSHIFT = 20.0
NEG = -1e30

NEXP, TOPK, RDIM, RANK = 64, 16, 16, 8
MOE_SCALING = 2.0
HFF = 11008
HPAD = 11264
JFOLD = 2752
JPAD = 2816
LORA_SC = 0.5


def _alibi_slopes(n: int) -> np.ndarray:
    cp2 = 2 ** math.floor(math.log2(n))
    base = 2.0 ** (-(2.0 ** (-(math.log2(cp2) - 3))))
    slopes = base ** np.arange(1, cp2 + 1, dtype=np.float64)
    if cp2 != n:
        extra_base = 2.0 ** (-(2.0 ** (-(math.log2(2 * cp2) - 3))))
        rem = min(cp2, n - cp2)
        extra = extra_base ** np.arange(1, 1 + 2 * rem, 2, dtype=np.float64)
        slopes = np.concatenate([slopes, extra])
    return slopes.astype(np.float32)



def _own_rows(c: int) -> np.ndarray:
    b, a = c // 2, c % 2
    idx = []
    for s in range(NSTRIP):
        st = 512 * s + 256 * a
        idx.extend(range(st, st + 256))
    return np.asarray(idx)


def _l1_perm_pad_rows(w: np.ndarray) -> np.ndarray:
    out = np.zeros((HPAD,) + w.shape[1:], dtype=w.dtype)
    for jc in range(22):
        npj = 128 if jc < 21 else 64
        for r in range(4):
            slot0 = (jc * 4 + r) * 128
            src0 = JFOLD * r + 128 * jc
            out[slot0:slot0 + npj] = w[src0:src0 + npj]
    return out


def _w2_pad_rows(w2qT: np.ndarray) -> np.ndarray:
    out = np.zeros((JPAD,) + w2qT.shape[1:], dtype=w2qT.dtype)
    out[:JFOLD] = w2qT
    return out


def host_prep(inputs: dict, c: int) -> dict:
    b, a = c // 2, c % 2
    f32 = lambda x: np.ascontiguousarray(np.asarray(x), dtype=np.float32)
    d = {}
    x0 = f32(inputs["L0_attn_hidden_states"])
    d["x0"] = np.ascontiguousarray(x0[b][_own_rows(c)])
    wqkv = f32(inputs["L0_attn_Wqkv"])
    d["w0q"] = np.ascontiguousarray(wqkv[:, 2048 * a:2048 * a + 2048])
    d["w0kv"] = np.ascontiguousarray(wqkv[:, 4096:4352])
    d["w0o"] = f32(inputs["L0_attn_Wo"])
    d["awT"] = np.ascontiguousarray(f32(inputs["L0_ffn_A_w"]).T)
    d["bwT"] = np.ascontiguousarray(f32(inputs["L0_ffn_B_w"]).T)
    d["wqwT"] = np.ascontiguousarray(f32(inputs["L0_ffn_Wq_w"]).T)
    d["keysT"] = np.ascontiguousarray(f32(inputs["L0_ffn_keys"]).T)
    d["w1q_attn"] = np.ascontiguousarray(f32(inputs["L1_attn_Wq"])[:, 2048 * a:2048 * a + 2048])
    d["w1kv"] = np.ascontiguousarray(f32(inputs["L1_attn_Wkv"])[:, 256 * c:256 * c + 256])
    d["w1o"] = f32(inputs["L1_attn_Wo"])
    d["bo_b"] = np.ascontiguousarray(
        np.broadcast_to(f32(inputs["L1_attn_bo"])[None, :], (1, D)))
    w1pp = _l1_perm_pad_rows(f32(inputs["L1_ffn_w1q"]))
    w3pp = _l1_perm_pad_rows(f32(inputs["L1_ffn_w3q"]))
    b1pp = _l1_perm_pad_rows(f32(inputs["L1_ffn_b1"]))
    b3pp = _l1_perm_pad_rows(f32(inputs["L1_ffn_b3"]))
    d["w1ppT"] = np.ascontiguousarray(w1pp.T)
    d["w3ppT"] = np.ascontiguousarray(w3pp.T)
    d["b1ppT"] = np.ascontiguousarray(b1pp.T)
    d["b3ppT"] = np.ascontiguousarray(b3pp.T)
    a1T = np.ascontiguousarray(f32(inputs["L1_ffn_a1"]).T)
    a3T = np.ascontiguousarray(f32(inputs["L1_ffn_a3"]).T)
    a13 = np.zeros((128, 128), np.float32)
    for ch in range(32):
        a13[:, 4 * ch:4 * ch + 2] = a1T[128 * ch:128 * (ch + 1)]
        a13[:, 4 * ch + 2:4 * ch + 4] = a3T[128 * ch:128 * (ch + 1)]
    d["a13sb"] = np.ascontiguousarray(a13)
    a2pp = _l1_perm_pad_rows(np.ascontiguousarray(f32(inputs["L1_ffn_a2"]).T))
    d["a2sb"] = np.ascontiguousarray(
        a2pp.reshape(88, 128, 2).transpose(1, 0, 2).reshape(128, 176))
    d["w2ppT"] = _w2_pad_rows(np.ascontiguousarray(f32(inputs["L1_ffn_w2q"]).T))
    d["b2T"] = np.ascontiguousarray(f32(inputs["L1_ffn_b2"]).T)

    slopes = _alibi_slopes(NH)[16 * a:16 * a + 16]
    d["slopes"] = np.ascontiguousarray(
        np.broadcast_to(slopes[None, :], (128, 16)).astype(np.float32))
    hh, ss, kk = np.meshgrid(np.arange(16), np.arange(8), np.arange(32), indexing="ij")
    ctab = (128 * kk - 512 * ss + 127).astype(np.float32)
    btab = slopes[hh.astype(np.int64)] * ctab - MSHIFT
    d["biastab"] = np.ascontiguousarray(
        np.broadcast_to(btab.reshape(1, -1), (128, 16 * 8 * 32)).astype(np.float32))
    kp = np.arange(128)[None, :, None]
    qf = np.arange(512)[None, None, :]
    rr = np.arange(4)[:, None, None]
    d["diagmask"] = np.ascontiguousarray(
        (kp <= qf - 128 * rr).astype(np.float32))
    d["identf"] = np.eye(128, dtype=np.float32)
    cvals = np.array([-MSHIFT, 1e-5, 1.0 / D, 1.0], dtype=np.float32)
    d["consts"] = np.ascontiguousarray(np.broadcast_to(cvals[None, :], (128, 4)))
    ee = np.arange(NEXP)[:, None]
    nn = np.arange(NEXP * RANK)[None, :]
    d["expmat"] = np.ascontiguousarray((nn // RANK == ee).astype(np.float32))
    return d


def assemble_out(results) -> np.ndarray:
    out = np.empty((B, S, D), dtype=np.float32)
    for c in range(NCORES):
        out[c // 2].reshape(S, D)[_own_rows(c)] = results[c]["x_out"]
    return out



class Sem:
    __slots__ = ("name", "n", "h")

    def __init__(self, name):
        self.name = name
        self.n = 0
        self.h = None

    def add(self, k=1):
        self.n += k
        return self.n


class Prog:

    ENGS = ("sync", "gpsimd", "tensor", "vector", "scalar")

    def __init__(self, nc):
        self.nc = nc
        self.q = {e: [] for e in self.ENGS}
        self.sems = []

    def sem(self, name):
        s = Sem(name)
        self.sems.append(s)
        return s

    def raw(self, eng, fn):
        self.q[eng].append(fn)

    def wait(self, eng, sem, val=None):
        v = sem.n if val is None else val
        if v <= 0:
            return
        self.q[eng].append(lambda e, s=sem, v=v: e.wait_ge(s.h, v))

    def op(self, eng, fn, inc=None, k=1):
        if inc is None:
            self.q[eng].append(fn)
            return None
        v = inc.add(k)
        self.q[eng].append(lambda e, f=fn, s=inc, kk=k: f(e).then_inc(s.h, kk))
        return v

    def dma(self, eng, out, in_, inc, wait=None, **kw):
        if wait:
            for s, v in wait:
                self.wait(eng, s, v)
        v = inc.add(16)
        self.q[eng].append(
            lambda e, o=out, i=in_, s=inc, kw2=kw: e.dma_start(out=o, in_=i, **kw2).then_inc(s.h, 16))
        return v

    def build_into(self, block):
        for eng in self.ENGS:
            ops = self.q[eng]
            if not ops:
                continue
            def make(ops_):
                def body(e):
                    for f in ops_:
                        f(e)
                return body
            getattr(block, eng)(make(ops))


def ceil_div(a, b):
    return (a + b - 1) // b


class Builder:
    def __init__(self, nc, cut=0):
        self.nc = nc
        self.p = Prog(nc)
        self.cut = cut
        self.es = contextlib.ExitStack()
        self._mk_dram()
        self._mk_sbuf()
        self._mk_sems()

    def _mk_dram(self):
        nc = self.nc
        dt = {}
        def ein(name, shape, dtype=F32):
            dt[name] = nc.dram_tensor(name, shape, dtype, kind="ExternalInput")
        def internal(name, shape, dtype):
            dt[name] = nc.dram_tensor(name, shape, dtype)

        want_l0 = self.cut in (0, 1)
        want_l1 = self.cut in (0, 2)

        if want_l0:
            ein("x0", [T, D])
            ein("w0q", [D, 2048]); ein("w0kv", [D, 256]); ein("w0o", [D, D])
            ein("awT", [D, 512]); ein("bwT", [512, D])
            ein("wqwT", [D, RDIM]); ein("keysT", [RDIM, NEXP])
            ein("expmat", [NEXP, NEXP * RANK])
            ein("biastab", [128, 16 * 8 * 32])
        if want_l1:
            ein("w1q_attn", [D, 2048]); ein("w1kv", [D, 256]); ein("w1o", [D, D])
            ein("bo_b", [1, D])
            ein("w1ppT", [1024, HPAD]); ein("w3ppT", [1024, HPAD])
            ein("b1ppT", [2, HPAD]); ein("b3ppT", [2, HPAD])
            ein("a13sb", [128, 128])
            ein("a2sb", [128, 176]); ein("w2ppT", [JPAD, D]); ein("b2T", [2, D])
        ein("slopes", [128, 16])
        ein("diagmask", [4, 128, 512])
        ein("identf", [128, 128])
        ein("consts", [128, 4])
        if self.cut == 2:
            ein("xB", [T, D])

        if self.cut == 1:
            dt["xB_out"] = nc.dram_tensor("xB_out", [T, D], F32, kind="ExternalOutput")
        if want_l1:
            dt["x_out"] = nc.dram_tensor("x_out", [T, D], F32, kind="ExternalOutput")

        for k in range(4):
            internal(f"lnxp{k}", [D, 512], BF16)
            dt[f"LNXALLp{k}"] = nc.dram_tensor(f"LNXALLp{k}", [NCORES * D, 512],
                                               BF16, addr_space="Shared")
            internal(f"LNXPAIRp{k}", [2 * D, 512], BF16)
            internal(f"ctxp{k}", [4 * NSTRIP * 2 * 128, 256], BF16)
            dt[f"CTXALLp{k}"] = nc.dram_tensor(
                f"CTXALLp{k}", [NCORES * 4 * NSTRIP * 2 * 128, 256], BF16,
                addr_space="Shared")
            internal(f"CTXSELFp{k}", [2 * 4 * NSTRIP * 128, 256], BF16)
        internal("dbgstats", [16 * 128, 8], F32)
        internal("dbgp", [32 * 128, 512], BF16)
        internal("qtm", [S, 2048], BF16)
        internal("kvK0", [128, S], BF16); internal("kvV0", [128, S], BF16)
        if want_l0:
            internal("xA", [T, D], F32)
            if self.cut == 0:
                internal("xB", [T, D], F32)
        if want_l1:
            internal("kv1K", [4 * 128, S], BF16)
            internal("kv1V", [4 * 128, S], BF16)
            internal("xC", [T, D], F32)
        self.wb = {}
        def wcast(name, shape):
            self.wb[name] = nc.dram_tensor(name + "_b", shape, BF16)
        if want_l0:
            wcast("w0q", [D, 2048]); wcast("w0kv", [D, 256]); wcast("w0o", [D, D])
            wcast("awT", [D, 512]); wcast("bwT", [512, D])
            wcast("expmat", [NEXP, NEXP * RANK])
        if want_l1:
            wcast("w1q_attn", [D, 2048]); wcast("w1kv", [D, 256]); wcast("w1o", [D, D])
            wcast("w1ppT", [1024, HPAD]); wcast("w3ppT", [1024, HPAD])
            wcast("b1ppT", [2, HPAD]); wcast("b3ppT", [2, HPAD])
            wcast("a13sb", [128, 128])
            wcast("a2sb", [128, 176]); wcast("w2ppT", [JPAD, D]); wcast("b2T", [2, D])
            wcast("bo_b", [1, D])
        self.dt = dt

    def _mk_sbuf(self):
        nc, es = self.nc, self.es
        self.aF = es.enter_context(nc.sbuf_tensor("arenaF", [128, 14336], F32))
        self.aB = es.enter_context(nc.sbuf_tensor("arenaB", [128, 55296], BF16))
        self.cF = es.enter_context(nc.sbuf_tensor("constF", [128, 640 + 16 + 4 + 4096], F32))
        self.cB = es.enter_context(nc.sbuf_tensor("constB", [128, 6016], BF16))
        self.idF = es.enter_context(nc.sbuf_tensor("idF", [128, 128], F32))
        self.idB = es.enter_context(nc.sbuf_tensor("idB", [128, 128], BF16))

    def _mk_sems(self):
        p = self.p
        self.sLd = p.sem("sLd")
        self.sLd2 = p.sem("sLd2")
        self.sCast = p.sem("sCast")
        self.sSt = p.sem("sSt")
        self.sPE = p.sem("sPE")
        self.sV = p.sem("sV")
        self.sA = p.sem("sA")
        self.sCC = p.sem("sCC")
        self.sG = p.sem("sG")
        self.sFree = p.sem("sFree")
        self.sMisc = p.sem("sMisc")

    @property
    def dm(self):
        return self.cF[:, 0:512]
    @property
    def slopes_sb(self):
        return self.cF[:, 512:528]
    @property
    def consts_sb(self):
        return self.cF[:, 528:532]
    @property
    def biastab_sb(self):
        return self.cF[:, 532:532 + 4096]
    def btile(self, i):
        return self.cB[:, 512 * i:512 * (i + 1)]
    @property
    def ones_col_b(self):
        return self.cB[:, 2560:2561]
    @property
    def ones_row_f(self):
        return self.cF[0:1, 4628:4756]
    @property
    def ones_full_b(self):
        return self.cB[:, 5888:6016]
    @property
    def ones_row_b(self):
        return self.cB[0:1, 2561:2689]
    def diagmask_b(self, r):
        return self.cB[:, 2689 + 512 * r: 2689 + 512 * (r + 1)]

    def prologue(self):
        p, dt = self.p, self.dt
        p.dma("sync", self.idF[:, :], dt["identf"][:, :], self.sLd)
        p.dma("sync", self.cF[:, 512:528], dt["slopes"][:, :], self.sLd)
        p.dma("sync", self.cF[:, 528:532], dt["consts"][:, :], self.sLd)
        p.dma("gpsimd", self.idB[:, :], dt["identf"][:, :], self.sCast)
        for r in range(4):
            p.dma("gpsimd", self.diagmask_b(r), dt["diagmask"][r, :, :], self.sCast)
        def mkregs(e):
            rv = e.partition_id()
            self._pb = e.snap(e.scalar_reg_alu(ALU.bitwise_and, rv, 0xFFFFFFFE),
                              min_val=0, max_val=6)
            self._aa = e.snap(e.scalar_reg_alu(ALU.bitwise_and, rv, 1),
                              min_val=0, max_val=1)
            self._pbD = e.snap(e.scalar_reg_alu(ALU.mult, self._pb, D),
                               min_val=0, max_val=6 * D)
            return e.nop()
        p.raw("sync", mkregs)
        p.op("gpsimd", lambda e: e.iota(self.dm, [[-1, 512]], base=-127,
                                        channel_multiplier=1,
                                        allow_small_or_imprecise_dtypes=True),
             inc=self.sG)
        p.op("vector", lambda e: e.memset(self.ones_col_b, 1.0), inc=self.sV)
        p.op("vector", lambda e: e.memset(self.cB[0:1, 2561:2689], 1.0), inc=self.sV)
        p.op("vector", lambda e: e.memset(self.cF[0:1, 4628:4756], 1.0), inc=self.sV)
        p.op("vector", lambda e: e.memset(self.cB[:, 5888:6016], 1.0), inc=self.sV)
        self.cast_consts = self.sCast.n
        self.wcast_done = {}
        order_l0 = ["w0q", "w0kv", "w0o", "awT", "bwT", "expmat"]
        order_l1 = ["w1q_attn", "w1kv", "w1o", "bo_b",
                    "w1ppT", "w3ppT", "b1ppT", "b3ppT", "a13sb",
                    "a2sb", "w2ppT", "b2T"]
        order = []
        if self.cut in (0, 1):
            order += order_l0
        if self.cut in (0, 2):
            order += order_l1
        self._order_l0, self._order_l1 = order_l0, order_l1
        import os
        skip = os.environ.get("SKIP_WCAST", "") == "1"
        self._wcast_order = [] if skip else order
        self._wcast_emitted = set()

    def cast_chunks(self, names):
        dt = self.dt
        out = []
        for name in names:
            if name not in self.wb or name in self._wcast_emitted \
                    or name not in self._wcast_order:
                continue
            self._wcast_emitted.add(name)
            srct = dt[name]
            rows = srct.shape[0]
            step = max(128, min(rows, (1 << 22) // max(1, srct.shape[1])))
            spans = [(r, min(rows, r + step)) for r in range(0, rows, step)]
            for idx, span in enumerate(spans):
                out.append((name, idx == len(spans) - 1, span))
        return out

    def emit_cast_chunk(self, chunk):
        name, is_last, (r, r2) = chunk
        v = self.p.dma("gpsimd", self.wb[name][r:r2, :],
                       self.dt[name][r:r2, :], self.sCast)
        if is_last:
            self.wcast_done[name] = v

    def emit_casts(self, names):
        for chunk in self.cast_chunks(names):
            self.emit_cast_chunk(chunk)

    def fence(self, include_cast=False):
        p = self.p
        sems = [self.sLd, self.sLd2, self.sSt, self.sPE, self.sV,
                self.sA, self.sCC, self.sG, self.sMisc]
        if include_cast:
            sems.append(self.sCast)
        for eng in Prog.ENGS:
            for s in sems:
                p.wait(eng, s)

    def wait_wcast(self, eng, *names):
        vals = [self.wcast_done.get(n, 0) for n in names]
        v = max(vals) if vals else 0
        if v:
            self.p.wait(eng, self.sCast, v)

    def ln_tile_A(self, xf, stats, junk):
        p = self.p
        v0 = p.op("vector", lambda e: e.tensor_scalar(
            junk, xf, 1.0, None, ALU.mult, ALU.add, accum_out=stats[:, 0:1]),
            inc=self.sV)
        a0 = p.op("scalar", lambda e: e.activation(
            junk, xf, AF.Square, accum_out=stats[:, 1:2]), inc=self.sA)
        return (v0, a0)

    def ln_tile_B(self, xf, stats, out_bf, va, out_f32=None):
        p = self.p
        sV, sA = self.sV, self.sA
        v0, a0 = va
        p.wait("vector", sA, a0)
        p.wait("vector", sV, v0)
        v1 = p.op("vector", lambda e: e.tensor_scalar(
            stats[:, 2:3], stats[:, 0:1], 1.0 / D, None, ALU.mult), inc=sV)
        p.wait("vector", sV, v1)
        v2 = p.op("vector", lambda e: e.tensor_scalar(
            stats[:, 3:4], stats[:, 1:2], 1.0 / D, None, ALU.mult), inc=sV)
        p.wait("vector", sV, v2)
        v3 = p.op("vector", lambda e: e.tensor_tensor(
            stats[:, 4:5], stats[:, 2:3], stats[:, 2:3], ALU.mult), inc=sV)
        p.wait("vector", sV, v3)
        v4 = p.op("vector", lambda e: e.tensor_tensor(
            stats[:, 5:6], stats[:, 3:4], stats[:, 4:5], ALU.subtract), inc=sV)
        p.wait("vector", sV, v4)
        v5 = p.op("vector", lambda e: e.tensor_scalar(
            stats[:, 5:6], stats[:, 5:6], 1e-5, None, ALU.add), inc=sV)
        p.wait("scalar", sV, v5)
        a1 = p.op("scalar", lambda e: e.activation(
            stats[:, 6:7], stats[:, 5:6], AF.Sqrt), inc=sA)
        p.wait("vector", sA, a1)
        v6 = p.op("vector", lambda e: e.reciprocal(stats[:, 7:8], stats[:, 6:7]),
                  inc=sV)
        p.wait("vector", sV, v6)
        last = p.op("vector", lambda e: e.tensor_scalar(
            out_bf, xf, stats[:, 2:3], stats[:, 7:8], ALU.subtract, ALU.mult), inc=sV)
        if out_f32 is not None:
            last = p.op("vector", lambda e: e.tensor_scalar(
                out_f32, xf, stats[:, 2:3], stats[:, 7:8], ALU.subtract, ALU.mult),
                inc=sV)
        return last

    def ln_tile(self, xf, stats, junk, out_bf, out_f32=None):
        va = self.ln_tile_A(xf, stats, junk)
        return self.ln_tile_B(xf, stats, out_bf, va, out_f32=out_f32)

    def transpose_tile_to_staging(self, src_bf, staging, tt, wait_v):
        p = self.p
        p.wait("tensor", self.sV, wait_v)
        p.wait("tensor", self.sA, getattr(self, "_tts_prevA", 0))
        ev_done = {}
        for grp in range(4):
            pT = self.pT2[grp % 2]
            if grp >= 2:
                s2, c2 = ev_done[grp - 2]
                p.wait("tensor", s2, c2)
            pe_last = None
            for j in range(8):
                ch = grp * 8 + j
                pe_last = p.op("tensor", lambda e, ch=ch, j=j, pT=pT: e.transpose(
                    pT[:, 128 * j:128 * (j + 1)],
                    src_bf[:, 128 * ch:128 * (ch + 1)], self.idB[:, :]), inc=self.sPE)
            if grp % 2 == 0:
                p.wait("vector", self.sPE, pe_last)
                for j in range(8):
                    ch = grp * 8 + j
                    p.op("vector", lambda e, ch=ch, j=j, pT=pT: e.tensor_copy(
                        staging[:, 512 * ch + 128 * tt: 512 * ch + 128 * (tt + 1)],
                        pT[:, 128 * j:128 * (j + 1)]), inc=self.sV)
                ev_done[grp] = (self.sV, self.sV.n)
            else:
                p.wait("scalar", self.sPE, pe_last)
                for j in range(8):
                    ch = grp * 8 + j
                    p.op("scalar", lambda e, ch=ch, j=j, pT=pT: e.activation(
                        staging[:, 512 * ch + 128 * tt: 512 * ch + 128 * (tt + 1)],
                        pT[:, 128 * j:128 * (j + 1)], AF.Copy), inc=self.sA)
                ev_done[grp] = (self.sA, self.sA.n)
        self._tts_prevA = self.sA.n
        return (self.sV.n, self.sA.n)

    def ln_phase(self, src_name):
        self.fence()
        self.lnx_pair_ready = {}
        self.lnx_all_ready = {}
        self.pT2 = self.psum_open([("pT", [128, 1024], BF16), ("pT_b", [128, 1024], BF16)])
        p, dt = self.p, self.dt
        p.wait("tensor", self.sCast, self.cast_consts)
        src = dt[src_name]
        af, ab = self.aF, self.aB
        xf = [af[:, 0:4096], af[:, 4096:8192]]
        junk = af[:, 8192:12288]
        stats2 = [af[:, 12288:12296], af[:, 12304:12312]]
        lnb = [ab[:, 0:4096], ab[:, 4096:8192]]
        staging = [ab[:, 8192 + 16384 * i:8192 + 16384 * (i + 1)] for i in range(2)]
        tr_done = {}
        st_done = {}
        va_at = {}
        def emit_B(i):
            pi = i % 2
            s, tt = i // 4, i % 4
            sp = s % 2
            av = self.ln_tile_B(xf[pi], stats2[pi], lnb[pi], va_at[i])
            p.wait("tensor", self.sV, av)
            if tt == 0 and s >= 2:
                p.wait("tensor", self.sSt, st_done[s - 2])
            self.transpose_tile_to_staging(lnb[pi], staging[sp], tt, av)
            tr_done[i] = (self.sV.n, self.sA.n)

        def emit_strip_out(s):
            sp = s % 2
            vstore, astore = self.sV.n, self.sA.n
            pview = dt[f"lnxp{s}"].ap().rearrange("(c p) w -> p c w",
                                                  c=32, p=128)
            sview = staging[sp].rearrange("p (c w) -> p c w", c=32, w=512)
            p.dma("scalar", pview, sview, self.sSt,
                  wait=[(self.sV, vstore), (self.sA, astore)])
            st_done[s] = self.sSt.n
            p.wait("gpsimd", self.sSt, st_done[s])
            cc = p.op("gpsimd", lambda e, s=s: e.collective_compute(
                "AllGather", ALU.bypass, replica_groups=[list(range(NCORES))],
                ins=[dt[f"lnxp{s}"].ap().opt()],
                outs=[dt[f"LNXALLp{s}"].ap().opt()]), inc=self.sCC)
            p.wait("sync", self.sCC, cc)
            def dyncopy(e, s=s):
                return e.dma_start(out=dt[f"LNXPAIRp{s}"][:, :],
                                   in_=dt[f"LNXALLp{s}"][bass.ds(self._pbD, 2 * D), :])
            pr = p.op("sync", dyncopy, inc=self.sLd, k=16)
            self.lnx_pair_ready[s] = pr
            self.lnx_all_ready[s] = cc

        for i in range(16):
            pi = i % 2
            wl = []
            if i >= 2:
                wl.append((self.sV, tr_done[i - 2][0]))
                wl.append((self.sA, tr_done[i - 2][1]))
            p.dma("sync", xf[pi][:, :], src[128 * i:128 * (i + 1), :],
                  self.sLd, wait=wl or None)
            ld = self.sLd.n
            if i >= 1:
                emit_B(i - 1)
                if (i - 1) % 4 == 3:
                    emit_strip_out((i - 1) // 4)
            p.wait("vector", self.sLd, ld)
            p.wait("scalar", self.sLd, ld)
            va_at[i] = self.ln_tile_A(xf[pi], stats2[pi], junk)
        emit_B(15)
        emit_strip_out(3)
        self.psum_close()
        return self.sSt.n

    def exchange_lnx(self):
        self.fence()
        p, dt = self.p, self.dt
        cc = p.op("gpsimd", lambda e: e.collective_compute(
            "AllGather", ALU.bypass, replica_groups=[list(range(NCORES))],
            ins=[dt["lnxfm"].ap().opt()], outs=[dt["LNXALL"].ap().opt()]),
            inc=self.sCC)
        p.wait("sync", self.sCC, cc)
        def dyncopy(e):
            rv = e.partition_id()
            pairbase = e.scalar_reg_alu(ALU.bitwise_and, rv, 0xFFFFFFFE)
            off = e.scalar_reg_alu(ALU.mult, pairbase, D)
            return e.dma_start(out=dt["LNXPAIR"][:, :],
                               in_=dt["LNXALL"][bass.ds(off, 2 * D), :])
        p.op("sync", dyncopy, inc=self.sLd, k=16)
        return self.sLd.n

    _psum_uid = 0

    def psum_open(self, specs):
        self._psum_cms = []
        out = []
        for name, shape, dtype in specs:
            Builder._psum_uid += 1
            cm = self.nc.psum_tensor(f"{name}_u{Builder._psum_uid}", shape, dtype)
            out.append(cm.__enter__())
            self._psum_cms.append(cm)
        return out

    def psum_close(self):
        for cm in reversed(self._psum_cms):
            cm.__exit__(None, None, None)
        self._psum_cms = []

    def qproj(self, wq_b, qtm, wname=None):
        self.fence()
        p, dt = self.p, self.dt
        if wname:
            self.wait_wcast("scalar", wname)
        ab = self.aB
        stg = ab[:, 0:16384]
        staging = [stg, stg]
        wt = [ab[:, 16384 + 16384 * i: 16384 + 16384 * (i + 1)] for i in range(2)]
        qev = [ab[:, 49152 + 512 * i: 49152 + 512 * (i + 1)] for i in range(2)]
        (pq0, pq1) = self.psum_open([("pq0", [128, 512], F32), ("pq1", [128, 512], F32)])
        mm_s = {}
        mm_fg = {}
        stage_at = {}
        for s in range(NSTRIP):
            sp = 0
            if s >= 1:
                p.wait("sync", self.sPE, mm_s.get(s - 1, 0))
            pair = dt[f"LNXPAIRp{s // 2}"]
            coff = 256 * (s % 2)
            p.wait("sync", self.sLd, self.lnx_pair_ready[s // 2])
            for ch in range(32):
                for r in range(2):
                    p.dma("sync", staging[sp][:, 512 * ch + 256 * r:
                                              512 * ch + 256 * (r + 1)],
                          pair[4096 * r + 128 * ch: 4096 * r + 128 * (ch + 1),
                               coff: coff + 256], self.sLd)
            stage_at[s] = self.sLd.n
            for fg in range(4):
                gk = 4 * s + fg
                p.wait("scalar", self.sPE, mm_fg.get(gk - 2, 0))
                for ch in range(32):
                    p.dma("scalar", wt[fg % 2][:, 512 * ch: 512 * (ch + 1)],
                          wq_b[128 * ch: 128 * (ch + 1), 512 * fg: 512 * (fg + 1)],
                          self.sLd2)
                wdone = self.sLd2.n
                p.wait("tensor", self.sLd, stage_at[s])
                p.wait("tensor", self.sLd2, wdone)
                for tt in range(4):
                    pq = (pq0, pq1)[tt % 2]
                    p.wait("tensor", self.sA, self.sA.n)
                    for ch in range(32):
                        p.op("tensor", lambda e, ch=ch, tt=tt, pq=pq, fg=fg, sp=sp:
                             e.matmul(
                                 pq[:, :],
                                 staging[sp][:, 512 * ch + 128 * tt:
                                             512 * ch + 128 * (tt + 1)],
                                 wt[fg % 2][:, 512 * ch: 512 * (ch + 1)],
                                 start=(ch == 0), stop=(ch == 31),
                                 skip_group_check=True),
                             inc=self.sPE if ch == 31 else None)
                    mmv = self.sPE.n
                    p.wait("scalar", self.sPE, mmv)
                    p.wait("scalar", self.sSt, self.sSt.n)
                    ev = p.op("scalar", lambda e, tt=tt, pq=pq: e.activation(
                        qev[tt % 2], pq[:, :], AF.Copy, scale=QSCALE), inc=self.sA)
                    p.dma("sync", qtm[512 * s + 128 * tt: 512 * s + 128 * (tt + 1),
                                      512 * fg: 512 * (fg + 1)],
                          qev[tt % 2], self.sSt, wait=[(self.sA, ev)])
                mm_fg[gk] = self.sPE.n
            mm_s[s] = self.sPE.n
        self.psum_close()

    def kvproj(self, wkv_b, pieces, ranks, outK, outV, wname=None):
        self.fence()
        p = self.p
        if wname:
            self.wait_wcast("scalar", wname)
        for k in range(4):
            p.wait("sync", self.sCC, self.lnx_all_ready[k])
            p.wait("sync", self.sLd, self.lnx_pair_ready[k])
        ab = self.aB
        page = [ab[:, 0:2048], ab[:, 2048:4096]]
        wkv = ab[:, 4096:4096 + 32 * 256]
        kev = [ab[:, 12288 + 512 * i:12288 + 512 * (i + 1)] for i in range(2)]
        vev = [ab[:, 13312 + 512 * i:13312 + 512 * (i + 1)] for i in range(2)]
        ps = self.psum_open([(f"pkv{i}", [128, 512], F32) for i in range(8)])
        for ch in range(32):
            p.dma("scalar", wkv[:, 256 * ch: 256 * (ch + 1)],
                  wkv_b[128 * ch: 128 * (ch + 1), :], self.sLd2)
        wv = self.sLd2.n
        p.wait("tensor", self.sLd2, wv)
        for ri, r in enumerate(ranks):
            mm_at = {}
            for ch in range(32):
                pi = ch % 2
                wl = []
                if ch >= 2:
                    wl.append((self.sPE, mm_at[ch - 2]))
                elif ri > 0:
                    wl.append((self.sPE, self.sPE.n))
                for k in range(4):
                    p.dma("sync", page[pi][:, 512 * k:512 * (k + 1)],
                          pieces[k][4096 * r + 128 * ch: 4096 * r + 128 * (ch + 1), :],
                          self.sLd, wait=(wl or None) if k == 0 else None)
                p.wait("tensor", self.sLd, self.sLd.n)
                for k in range(4):
                    for fc in range(2):
                        p.op("tensor", lambda e, ch=ch, k=k, fc=fc, pi=pi: e.matmul(
                            ps[fc * 4 + k][:, :],
                            wkv[:, 256 * ch + 128 * fc: 256 * ch + 128 * (fc + 1)],
                            page[pi][:, 512 * k: 512 * (k + 1)],
                            start=(ch == 0), stop=(ch == 31)),
                            inc=self.sPE if (k == 3 and fc == 1) else None)
                mm_at[ch] = self.sPE.n
            mmv = self.sPE.n
            p.wait("scalar", self.sPE, mmv)
            a = r % 2
            for k in range(4):
                for fc in range(2):
                    ev = (kev, vev)[fc][k % 2]
                    dst = (outK, outV)[fc]
                    p.wait("scalar", self.sSt, self.sSt.n)
                    e1 = p.op("scalar", lambda e, k=k, fc=fc, ev=ev: e.activation(
                        ev, ps[fc * 4 + k][:, :], AF.Copy), inc=self.sA)
                    for h2 in range(2):
                        col = 1024 * k + 512 * h2 + 256 * a
                        p.dma("sync", dst[:, col:col + 256],
                              ev[:, 256 * h2: 256 * (h2 + 1)], self.sSt,
                              wait=[(self.sA, e1)] if h2 == 0 else None)
        self.psum_close()

    def attention(self, layer, cast_pend=None):
        self.fence()
        self.ctx_ready = {}
        p, dt = self.p, self.dt
        p.wait("tensor", self.sCast, self.cast_consts)
        p.wait("vector", self.sCast, self.cast_consts)
        ab, af = self.aB, self.aF
        qt = [ab[:, 512 * i:512 * (i + 1)] for i in range(3)]
        ptile = [ab[:, 1536 + 512 * i:1536 + 512 * (i + 1)] for i in range(4)]
        bset = [[ab[:, 3584 + 512 * (5 * pp + k): 3584 + 512 * (5 * pp + k + 1)]
                 for k in range(5)] for pp in range(2)]
        ksb = [ab[:, 8704 + 4096 * i: 8704 + 4096 * (i + 1)] for i in range(2)]
        vsb = [ab[:, 16896 + 4096 * i: 16896 + 4096 * (i + 1)] for i in range(2)]
        ctev = [ab[:, 25088 + 512 * i: 25088 + 512 * (i + 1)] for i in range(2)]
        bscr = [af[:, 512 * i:512 * (i + 1)] for i in range(2)]
        rrow = af[:, 1024:1536]
        rb = [af[:, 1536 + 512 * i:1536 + 512 * (i + 1)] for i in range(2)]
        pq0, pq1, pq2, pq3, pc0, pc1, psb0, psb1 = self.psum_open(
            [("pqA", [128, 512], F32), ("pqB", [128, 512], F32),
             ("pqC", [128, 512], F32), ("pqD", [128, 512], F32),
             ("pcA", [128, 512], F32), ("pcB", [128, 512], F32),
             ("psA", [128, 512], F32), ("psB", [128, 512], F32)])
        pqk = (pq0, pq1, pq2, pq3)
        pctx = (pc0, pc1)
        psumb = (psb0, psb1)
        kvready = 0
        if layer == 0:
            p.dma("sync", self.cF[:, 532:532 + 4096], dt["biastab"][:, :], self.sLd)
            p.dma("sync", ksb[0][:, :], dt["kvK0"][:, :], self.sLd)
            for kt in range(32):
                p.op("sync", lambda e, kt=kt: e.dma_start(
                    out=vsb[0][:, 128 * kt: 128 * (kt + 1)],
                    in_=dt["kvV0"][:, 128 * kt: 128 * (kt + 1)], transpose=True),
                    inc=self.sLd, k=16)
            kvready = self.sLd.n

        qk_last = {}
        ctx_mul = {}
        recip_c = {}
        kv_pi = 0
        oi = 0
        LEAD = 3
        for hl in range(16):
            pp = hl % 2
            if layer == 0:
                bs = bset[pp]
                p.wait("vector", self.sA, self.sA.n)
                bv = p.op("vector", lambda e, hl=hl, pp=pp: e.tensor_scalar(
                    bscr[pp], self.dm, self.slopes_sb[:, hl:hl + 1], None, ALU.mult),
                    inc=self.sV)
                p.wait("scalar", self.sV, bv)
                av = p.op("scalar", lambda e, bs=bs, pp=pp: e.activation(
                    bs[0], bscr[pp], AF.Exp), inc=self.sA)
                for r in range(4):
                    p.wait("vector", self.sA, av)
                    bv = p.op("vector", lambda e, r=r, hl=hl, pp=pp: e.tensor_scalar(
                        bscr[pp], self.dm, float(127 + 128 * r),
                        self.slopes_sb[:, hl:hl + 1], ALU.add, ALU.mult), inc=self.sV)
                    p.wait("vector", self.sV, bv)
                    bv = p.op("vector", lambda e, pp=pp: e.tensor_scalar(
                        bscr[pp], bscr[pp], 0.0, None, ALU.min), inc=self.sV)
                    p.wait("scalar", self.sV, bv)
                    av = p.op("scalar", lambda e, r=r, bs=bs, pp=pp: e.activation(
                        bs[1 + r], bscr[pp], AF.Exp), inc=self.sA)
                    p.wait("vector", self.sA, av)
                    p.op("vector", lambda e, r=r, bs=bs: e.tensor_tensor(
                        bs[1 + r], bs[1 + r], self.diagmask_b(r), ALU.mult),
                        inc=self.sV)
                bready = (self.sV.n, self.sA.n)
            if layer == 1 and hl % 4 == 0:
                bi = hl // 4
                kv_pi = bi % 2
                wl = [(self.sPE, self.sPE.n)] if bi >= 2 else None
                p.dma("sync", ksb[kv_pi][:, :],
                      dt["kv1K"][128 * bi:128 * (bi + 1), :], self.sLd, wait=wl)
                for kt in range(32):
                    p.op("sync", lambda e, kt=kt, bi=bi, pi=kv_pi: e.dma_start(
                        out=vsb[pi][:, 128 * kt: 128 * (kt + 1)],
                        in_=dt["kv1V"][128 * bi: 128 * bi + 128,
                                       128 * kt: 128 * (kt + 1)], transpose=True),
                        inc=self.sLd, k=16)
                kvready = self.sLd.n
            for s in range(NSTRIP):
                E = 4 * (s + 1)
                qi = oi % 3
                it2 = oi % 2
                if oi >= 3:
                    p.wait("sync", self.sPE, qk_last[oi - 3])
                p.op("sync", lambda e, s=s, hl=hl, qi=qi: e.dma_start(
                    out=qt[qi][:, :],
                    in_=dt["qtm"][512 * s: 512 * (s + 1), 128 * hl: 128 * (hl + 1)],
                    transpose=True), inc=self.sLd, k=16)
                qready = self.sLd.n
                p.wait("tensor", self.sLd, max(qready, kvready))
                if layer == 0:
                    p.wait("tensor", self.sV, bready[0])
                exp_done = {}
                sum_done = {}
                pready = {}
                war_ctx = ctx_mul.get(oi - 2, 0)
                war_sum = recip_c.get(oi - 2, 0)

                def emit_pv(kt):
                    bank = kt % 4
                    prs, prv = pready[kt]
                    p.wait("tensor", prs, prv)
                    if kt == 0:
                        p.wait("tensor", self.sV, war_ctx)
                        p.wait("tensor", self.sA, war_sum)
                    p.op("tensor", lambda e, kt=kt, bank=bank, pi=kv_pi, E=E,
                         it2=it2: e.matmul(
                             pctx[it2][:, :], vsb[pi][:, 128 * kt: 128 * (kt + 1)],
                             ptile[bank][:, :], start=(kt == 0), stop=(kt == E - 1),
                             skip_group_check=True),
                         inc=self.sPE if kt == E - 1 else None)
                    sm = p.op("tensor", lambda e, kt=kt, bank=bank, E=E, it2=it2:
                              e.matmul(psumb[it2][:, :], self.ones_full_b,
                                       ptile[bank][:, :], start=(kt == 0),
                                       stop=(kt == E - 1), skip_group_check=True),
                              inc=self.sPE)
                    sum_done[kt] = sm

                for kt in range(E):
                    bank = kt % 4
                    if kt >= 4:
                        p.wait("tensor", self.sA, exp_done[kt - 4])
                    qkv = p.op("tensor", lambda e, kt=kt, bank=bank, qi=qi,
                               pi=kv_pi: e.matmul(
                                   pqk[bank][:, :],
                                   ksb[pi][:, 128 * kt: 128 * (kt + 1)],
                                   qt[qi][:, :], start=True, stop=True,
                                   skip_group_check=True), inc=self.sPE)
                    p.wait("scalar", self.sPE, qkv)
                    if kt >= 4:
                        p.wait("scalar", self.sPE, sum_done.get(kt - 4, 0))
                    r = kt - 4 * s
                    if layer == 0 and r < 0:
                        bias_ap = self.biastab_sb[:, hl * 256 + s * 32 + kt:
                                                  hl * 256 + s * 32 + kt + 1]
                    else:
                        bias_ap = self.consts_sb[:, 0:1]
                    ev = p.op("scalar", lambda e, bank=bank, bias_ap=bias_ap:
                              e.activation(ptile[bank], pqk[bank][:, :], AF.Exp,
                                           bias=bias_ap), inc=self.sA)
                    exp_done[kt] = ev
                    if layer == 0:
                        btile = bset[pp][1 + r] if r >= 0 else bset[pp][0]
                        p.wait("vector", self.sA, ev)
                        mv = p.op("vector", lambda e, bank=bank, bt=btile:
                                  e.tensor_tensor(ptile[bank], ptile[bank], bt,
                                                  ALU.mult), inc=self.sV)
                        pready[kt] = (self.sV, mv)
                    elif r >= 0:
                        p.wait("vector", self.sA, ev)
                        mv = p.op("vector", lambda e, bank=bank, r=r:
                                  e.tensor_tensor(ptile[bank], ptile[bank],
                                                  self.diagmask_b(r), ALU.mult),
                                  inc=self.sV)
                        pready[kt] = (self.sV, mv)
                    else:
                        pready[kt] = (self.sA, ev)
                    if kt >= LEAD:
                        emit_pv(kt - LEAD)
                for kt in range(max(0, E - LEAD), E):
                    emit_pv(kt)
                sm = sum_done[E - 1]
                qk_last[oi] = sm
                p.wait("scalar", self.sPE, sm)
                p.wait("scalar", self.sV, ctx_mul.get(oi - 2, 0))
                l1 = p.op("scalar", lambda e, it2=it2: e.activation(
                    rrow, psumb[it2][:, :], AF.Ln), inc=self.sA)
                p.wait("scalar", self.sA, l1)
                rc = p.op("scalar", lambda e, it2=it2: e.activation(
                    rb[it2], rrow, AF.Exp, scale=-1.0), inc=self.sA)
                recip_c[oi] = rc
                p.wait("vector", self.sA, rc)
                p.wait("vector", self.sSt, self.sSt.n)
                mu = p.op("vector", lambda e, it2=it2: e.tensor_tensor(
                    ctev[it2], pctx[it2][:, :], rb[it2], ALU.mult), inc=self.sV)
                ctx_mul[oi] = mu
                kq = hl // 4
                rowb = ((hl % 4) * 8 + s) * 2 * 128
                piece = dt[f"ctxp{kq}"]
                p.dma("gpsimd", piece[rowb:rowb + 128, :],
                      ctev[it2][:, 0:256], self.sSt, wait=[(self.sV, mu)])
                p.dma("gpsimd", piece[rowb + 128:rowb + 256, :],
                      ctev[it2][:, 256:512], self.sSt)
                oi += 1
            if hl % 4 == 3:
                kq = hl // 4
                p.wait("gpsimd", self.sSt, self.sSt.n)
                cc = p.op("gpsimd", lambda e, kq=kq: e.collective_compute(
                    "AllGather", ALU.bypass, replica_groups=[list(range(NCORES))],
                    ins=[dt[f"ctxp{kq}"].ap().opt()],
                    outs=[dt[f"CTXALLp{kq}"].ap().opt()]), inc=self.sCC)
                p.wait("sync", self.sCC, cc)
                G4 = 4 * NSTRIP
                def dyncopy(e, kq=kq, G4=G4):
                    srcv = dt[f"CTXALLp{kq}"].ap().rearrange(
                        "(r g h p) c -> r g h p c", r=NCORES, g=G4, h=2, p=128)
                    dstv = dt[f"CTXSELFp{kq}"].ap().rearrange(
                        "(r g p) c -> r g p c", r=2, g=G4, p=128)
                    return e.dma_start(
                        out=dstv, in_=srcv[bass.ds(self._pb, 2), :,
                                           bass.ds(self._aa, 1), :, :])
                pr = p.op("sync", dyncopy, inc=self.sLd, k=16)
                self.ctx_ready[hl // 4] = pr
            if cast_pend:
                k0 = (len(cast_pend) * hl) // 16
                k1 = (len(cast_pend) * (hl + 1)) // 16
                if k1 > k0:
                    p.wait("gpsimd", self.sV, self.sV.n)
                    for chk in cast_pend[k0:k1]:
                        self.emit_cast_chunk(chk)
        self.psum_close()

    def exchange_ctx(self):
        self.fence()
        p, dt = self.p, self.dt
        cc = p.op("gpsimd", lambda e: e.collective_compute(
            "AllGather", ALU.bypass, replica_groups=[list(range(NCORES))],
            ins=[dt["ctxm"].ap().opt()], outs=[dt["CTXALL"].ap().opt()]),
            inc=self.sCC)
        p.wait("sync", self.sCC, cc)
        G = NSTRIP * 16
        def dyncopy(e):
            rv = e.partition_id()
            pb = e.snap(e.scalar_reg_alu(ALU.bitwise_and, rv, 0xFFFFFFFE),
                        min_val=0, max_val=6)
            aa = e.snap(e.scalar_reg_alu(ALU.bitwise_and, rv, 1),
                        min_val=0, max_val=1)
            srcv = dt["CTXALL"].ap().rearrange(
                "(r g h p) c -> r g h p c", r=NCORES, g=G, h=2, p=128)
            dstv = dt["CTXSELF"].ap().rearrange(
                "(r g p) c -> r g p c", r=2, g=G, p=128)
            return e.dma_start(out=dstv,
                               in_=srcv[bass.ds(pb, 2), :, bass.ds(aa, 1), :, :])
        p.op("sync", dyncopy, inc=self.sLd, k=16)
        return self.sLd.n

    def oproj(self, layer, wo_b, x_old, x_new, wname=None):
        self._o_mm = {}
        self.fence()
        for k in range(4):
            self.p.wait("sync", self.sLd, self.ctx_ready[k])
        p, dt = self.p, self.dt
        if wname:
            self.wait_wcast("scalar", wname)
        if layer == 1:
            self.wait_wcast("scalar", "bo_b")
        ab, af = self.aB, self.aF
        wo_sb = [ab[:, 16384 * i: 16384 * (i + 1)] for i in range(2)]
        ctx_sb = [ab[:, 32768 + 8192 * i: 32768 + 8192 * (i + 1)] for i in range(2)]
        xo = [af[:, 512 * i: 512 * (i + 1)] for i in range(2)]
        xn = [af[:, 1024 + 512 * i: 1024 + 512 * (i + 1)] for i in range(2)]
        po = self.psum_open([("po0", [128, 512], F32), ("po1", [128, 512], F32)])
        G = NSTRIP * 16
        boT = [ab[0:1, 49152 + 512 * i: 49152 + 512 * (i + 1)] for i in range(2)]
        self._bod = 0
        ei = 0
        for op2 in range(4):
            p.wait("scalar", self.sPE, self.sPE.n)
            for osl in range(2):
                os = 2 * op2 + osl
                for dc in range(32):
                    p.dma("scalar", wo_sb[osl][:, 512 * dc: 512 * (dc + 1)],
                          wo_b[128 * dc: 128 * (dc + 1), 512 * os: 512 * (os + 1)],
                          self.sLd2)
                if layer == 1:
                    p.dma("scalar", boT[osl],
                          self.wb["bo_b"][0:1, 512 * os: 512 * (os + 1)], self.sLd2)
            wod = self.sLd2.n
            self._bod = wod
            for s in range(NSTRIP):
                ci = s % 2
                if s >= 2:
                    p.wait("sync", self.sPE, self._o_mm.get((op2, s - 2), 0))
                elif op2 > 0:
                    p.wait("sync", self.sPE, self._o_mm.get((op2 - 1, NSTRIP - 2), 0))
                for rank in range(2):
                    for hl in range(16):
                        kq, hq = hl // 4, hl % 4
                        rowb = (rank * 32 + hq * 8 + s) * 128
                        p.dma("sync",
                              ctx_sb[ci][:, (rank * 16 + hl) * 256:
                                         (rank * 16 + hl) * 256 + 256],
                              dt[f"CTXSELFp{kq}"][rowb:rowb + 128, :], self.sLd)
                ctxd = self.sLd.n
                for qsub in range(2):
                    rb = 2 * s + qsub
                    for osl in range(2):
                        os = 2 * op2 + osl
                        bank = po[ei % 2]
                        p.wait("tensor", self.sLd, ctxd)
                        p.wait("tensor", self.sLd2, wod)
                        p.wait("tensor", self.sV, self.sV.n)
                        for dc in range(32):
                            last = (dc == 31 and layer == 0)
                            p.op("tensor", lambda e, dc=dc, qsub=qsub, osl=osl, s=s,
                                 bank=bank, last=last: e.matmul(
                                     bank[:, :],
                                     ctx_sb[s % 2][:, 256 * dc + 128 * qsub:
                                                   256 * dc + 128 * (qsub + 1)],
                                     wo_sb[osl][:, 512 * dc: 512 * (dc + 1)],
                                     start=(dc == 0), stop=last),
                                 inc=self.sPE if last else None)
                        if layer == 1:
                            p.wait("tensor", self.sLd2, self._bod)
                            p.op("tensor", lambda e, bank=bank, osl=osl: e.matmul(
                                bank[:, :], self.ones_row_b, boT[osl],
                                start=False, stop=True, skip_group_check=True),
                                inc=self.sPE)
                        mmv = self.sPE.n
                        xi = ei % 2
                        p.dma("sync", xo[xi],
                              x_old[128 * rb: 128 * (rb + 1), 512 * os: 512 * (os + 1)],
                              self.sLd, wait=[(self.sV, self.sV.n)])
                        xod = self.sLd.n
                        p.wait("vector", self.sPE, mmv)
                        p.wait("vector", self.sLd, xod)
                        p.wait("vector", self.sSt, self.sSt.n)
                        mv = p.op("vector", lambda e, bank=bank, xi=xi: e.tensor_tensor(
                            xn[xi], bank[:, :], xo[xi], ALU.add), inc=self.sV)
                        p.dma("sync", x_new[128 * rb: 128 * (rb + 1),
                                            512 * os: 512 * (os + 1)],
                              xn[xi], self.sSt, wait=[(self.sV, mv)])
                        ei += 1
                self._o_mm[(op2, s)] = self.sPE.n
        self.psum_close()

    def ffn0(self, x_old, x_new):
        import os as _os
        LVL = int(_os.environ.get("FFN0_LVL", "7"))
        self.fence()
        p, dt = self.p, self.dt
        self.wait_wcast("scalar", "awT", "bwT")
        self.wait_wcast("gpsimd", "expmat")
        ab, af = self.aB, self.aF
        staging = ab[:, 0:16384]
        aw_sb = ab[:, 16384:32768]
        bw_sb = ab[:, 32768:49152]
        z_bf = [ab[:, 49152 + 512 * i: 49152 + 512 * (i + 1)] for i in range(4)]
        zg_bf = [ab[:, 51200 + 512 * i: 51200 + 512 * (i + 1)] for i in range(4)]
        emat_sb = self.cB[0:64, 4737:5249]
        gate_fm_t = self.cB[0:64, 5249:5761]
        xf = [af[:, 0:4096], af[:, 4096:8192]]
        xo = [af[:, 512 * i: 512 * (i + 1)] for i in range(2)]
        xn = [af[:, 1024 + 512 * i: 1536 + 512 * i] for i in range(2)]
        lnf = af[:, 8192:12288]
        stats2 = [af[:, 12288:12296], af[:, 13864:13872]]
        fm32 = af[:, 12296:12808]
        qr_sb = af[0:16, 12808:13320]
        s0 = af[:, 13320:13576]
        scr = af[:, 13576:13832]
        mrow = af[:, 13832:13864]
        junk = lnf
        ps = self.psum_open([(f"pf{i}", [128, 512], F32) for i in range(8)])
        wq_sb = self.cF[:, 532:1044]
        keys_sb = self.cF[0:16, 1044:1108]
        for ch in range(32):
            p.dma("sync", wq_sb[:, 16 * ch: 16 * (ch + 1)],
                  dt["wqwT"][128 * ch: 128 * (ch + 1), :], self.sLd)
        p.dma("sync", keys_sb, dt["keysT"][:, :], self.sLd)
        p.dma("gpsimd", emat_sb, dt["expmat"][:, :], self.sCast)
        for ch in range(32):
            p.dma("scalar", aw_sb[:, 512 * ch: 512 * (ch + 1)],
                  self.wb["awT"][128 * ch: 128 * (ch + 1), :], self.sLd2)
        for nc4 in range(4):
            p.dma("scalar", bw_sb[:, 4096 * nc4: 4096 * (nc4 + 1)],
                  self.wb["bwT"][128 * nc4: 128 * (nc4 + 1), :], self.sLd2)
        wd = (self.sLd.n, self.sLd2.n, self.sCast.n)
        blk = lambda t, b: t[:, 64 * b:64 * (b + 1)]
        for g in range(4):
            qr_bank, sc_bank, tr_bank = ps[0], ps[1], ps[2]
            for tt in range(4):
                i = 4 * g + tt
                pi = i % 2
                p.dma("sync", xf[pi][:, :], x_old[128 * i:128 * (i + 1), :],
                      self.sLd, wait=[(self.sPE, self.sPE.n), (self.sV, self.sV.n),
                                      (self.sA, self.sA.n)])
                ld = self.sLd.n
                p.wait("vector", self.sLd, ld)
                p.wait("scalar", self.sLd, ld)
                av = self.ln_tile(xf[pi], stats2[pi], xf[1 - pi], lnf)
                p.wait("tensor", self.sV, av)
                NOTR = int(_os.environ.get("FFN0_NOTR", "0"))
                for grp in range(8):
                    if NOTR >= 2:
                        break
                    pe_l = None
                    for j in range(4):
                        ch = grp * 4 + j
                        pe_l = p.op("tensor", lambda e, ch=ch, j=j: e.transpose(
                            tr_bank[:, 128 * j:128 * (j + 1)],
                            lnf[:, 128 * ch:128 * (ch + 1)], self.idF[:, :]),
                            inc=self.sPE)
                    p.wait("vector", self.sPE, pe_l)
                    p.wait("vector", self.sA, self.sA.n)
                    for j in range(4):
                        p.op("vector", lambda e, j=j: e.tensor_copy(
                            fm32[:, 128 * j:128 * (j + 1)],
                            tr_bank[:, 128 * j:128 * (j + 1)]), inc=self.sV)
                    vv = self.sV.n
                    p.wait("scalar", self.sV, vv)
                    for j in range(4):
                        ch = grp * 4 + j
                        p.op("scalar", lambda e, ch=ch, j=j, tt=tt: e.activation(
                            staging[:, 512 * ch + 128 * tt: 512 * ch + 128 * (tt + 1)],
                            fm32[:, 128 * j:128 * (j + 1)], AF.Copy), inc=self.sA)
                    p.wait("tensor", self.sV, vv)
                    p.wait("tensor", self.sLd, wd[0])
                    for j in range(4):
                        if NOTR >= 1:
                            break
                        ch = grp * 4 + j
                        p.op("tensor", lambda e, ch=ch, j=j, tt=tt, qr_bank=qr_bank:
                             e.matmul(qr_bank[0:16, 128 * tt:128 * (tt + 1)],
                                      wq_sb[:, 16 * ch:16 * (ch + 1)],
                                      fm32[:, 128 * j:128 * (j + 1)],
                                      start=(ch == 0), stop=(ch == 31)),
                             inc=self.sPE if ch == 31 else None)
                    p.wait("tensor", self.sA, self.sA.n)
                    p.wait("tensor", self.sV, self.sV.n)
            if LVL <= 2:
                continue
            qv = self.sPE.n
            p.wait("vector", self.sPE, qv)
            ev = p.op("vector", lambda e, qr_bank=qr_bank: e.tensor_copy(
                qr_sb, qr_bank[0:16, :]), inc=self.sV)
            p.wait("tensor", self.sV, ev)
            for tt in range(4):
                p.op("tensor", lambda e, tt=tt, sc_bank=sc_bank: e.matmul(
                    sc_bank[:, 64 * tt: 64 * (tt + 1)],
                    qr_sb[:, 128 * tt:128 * (tt + 1)], keys_sb,
                    start=True, stop=True), inc=self.sPE)
            sv = self.sPE.n
            p.wait("vector", self.sPE, sv)
            sev = p.op("vector", lambda e, sc_bank=sc_bank: e.tensor_copy(
                s0, sc_bank[:, 0:256]), inc=self.sV)
            p.wait("vector", self.sV, sev)
            cp = p.op("vector", lambda e: e.tensor_copy(scr, s0), inc=self.sV)
            p.wait("vector", self.sV, cp)
            if LVL <= 3:
                continue
            for it in range(TOPK):
                for b in range(4):
                    p.op("vector", lambda e, b=b: e.tensor_reduce(
                        mrow[:, 8 + b:9 + b], blk(scr, b), AX.X, ALU.max), inc=self.sV)
                p.wait("vector", self.sV, self.sV.n)
                if it == 0:
                    mx = p.op("vector", lambda e: e.tensor_copy(
                        mrow[:, 0:4], mrow[:, 8:12]), inc=self.sV)
                    p.wait("vector", self.sV, mx)
                if it == TOPK - 1:
                    tv = p.op("vector", lambda e: e.tensor_copy(
                        mrow[:, 4:8], mrow[:, 8:12]), inc=self.sV)
                    p.wait("vector", self.sV, tv)
                    break
                for b in range(4):
                    p.op("vector", lambda e, b=b: e.tensor_scalar(
                        blk(junk, b), blk(scr, b), mrow[:, 8 + b:9 + b], NEG,
                        ALU.is_equal, ALU.mult), inc=self.sV)
                p.wait("vector", self.sV, self.sV.n)
                for b in range(4):
                    p.op("vector", lambda e, b=b: e.tensor_tensor(
                        blk(scr, b), blk(scr, b), blk(junk, b), ALU.add), inc=self.sV)
                p.wait("vector", self.sV, self.sV.n)
            if LVL <= 4:
                continue
            ng = p.op("vector", lambda e: e.tensor_scalar(
                mrow[:, 12:16], mrow[:, 0:4], -1.0, None, ALU.mult), inc=self.sV)
            p.wait("scalar", self.sV, ng)
            gate_f = scr
            for b in range(4):
                p.op("scalar", lambda e, b=b: e.activation(
                    blk(gate_f, b), blk(s0, b), AF.Exp,
                    bias=mrow[:, 12 + b:13 + b]), inc=self.sA)
            p.wait("vector", self.sA, self.sA.n)
            for b in range(4):
                p.op("vector", lambda e, b=b: e.tensor_scalar(
                    blk(junk, b), blk(s0, b), mrow[:, 4 + b:5 + b], None,
                    ALU.is_ge), inc=self.sV)
            p.wait("vector", self.sV, self.sV.n)
            for b in range(4):
                p.op("vector", lambda e, b=b: e.tensor_tensor(
                    blk(gate_f, b), blk(gate_f, b), blk(junk, b), ALU.mult),
                    inc=self.sV)
            p.wait("vector", self.sV, self.sV.n)
            for b in range(4):
                p.op("vector", lambda e, b=b: e.tensor_reduce(
                    mrow[:, 16 + b:17 + b], blk(gate_f, b), AX.X, ALU.add), inc=self.sV)
            p.wait("vector", self.sV, self.sV.n)
            rv = p.op("vector", lambda e: e.reciprocal(mrow[:, 20:24], mrow[:, 16:20]),
                      inc=self.sV)
            p.wait("vector", self.sV, rv)
            for b in range(4):
                p.op("vector", lambda e, b=b: e.tensor_scalar(
                    blk(junk, b), blk(gate_f, b), mrow[:, 20 + b:21 + b],
                    MOE_SCALING, ALU.mult, ALU.mult), inc=self.sV)
            gv = self.sV.n
            p.wait("tensor", self.sV, gv)
            for tt in range(4):
                p.op("tensor", lambda e, tt=tt: e.transpose(
                    tr_bank[0:64, 128 * tt:128 * (tt + 1)],
                    junk[:, 64 * tt: 64 * (tt + 1)], self.idF[:, :]),
                    inc=self.sPE)
            gtv = self.sPE.n
            p.wait("vector", self.sPE, gtv)
            gfm = p.op("vector", lambda e: e.tensor_copy(
                gate_fm_t, tr_bank[0:64, 0:512]), inc=self.sV)
            if LVL <= 5:
                continue
            p.wait("tensor", self.sV, gfm)
            p.wait("tensor", self.sLd2, wd[1])
            p.wait("tensor", self.sCast, wd[2])
            for nc4 in range(4):
                for ch in range(32):
                    p.op("tensor", lambda e, nc4=nc4, ch=ch: e.matmul(
                        ps[4 + nc4][:, :],
                        aw_sb[:, 512 * ch + 128 * nc4: 512 * ch + 128 * (nc4 + 1)],
                        staging[:, 512 * ch: 512 * (ch + 1)],
                        start=(ch == 0), stop=(ch == 31)),
                        inc=self.sPE if ch == 31 else None)
                p.op("tensor", lambda e, nc4=nc4: e.matmul(
                    ps[nc4][:, :], emat_sb[:, 128 * nc4:128 * (nc4 + 1)],
                    gate_fm_t, start=True, stop=True), inc=self.sPE)
            zv = self.sPE.n
            p.wait("scalar", self.sPE, zv)
            p.wait("scalar", self.sV, self.sV.n)
            for nc4 in range(4):
                p.op("scalar", lambda e, nc4=nc4: e.activation(
                    z_bf[nc4], ps[4 + nc4][:, :], AF.Copy), inc=self.sA)
            p.wait("vector", self.sA, self.sA.n)
            for nc4 in range(4):
                p.op("vector", lambda e, nc4=nc4: e.tensor_tensor(
                    zg_bf[nc4], z_bf[nc4], ps[nc4][:, :], ALU.mult), inc=self.sV)
            zgv = self.sV.n
            if LVL <= 6:
                continue
            for ts in range(4):
                rb = 4 * g + ts
                for os in range(8):
                    bank = ps[4 + (os % 2)]
                    p.wait("tensor", self.sV, max(zgv, self.sV.n))
                    for nc4 in range(4):
                        p.op("tensor", lambda e, nc4=nc4, ts=ts, os=os, bank=bank:
                             e.matmul(bank[:, :],
                                      zg_bf[nc4][:, 128 * ts: 128 * (ts + 1)],
                                      bw_sb[:, 4096 * nc4 + 512 * os:
                                            4096 * nc4 + 512 * (os + 1)],
                                      start=(nc4 == 0), stop=(nc4 == 3)),
                             inc=self.sPE if nc4 == 3 else None)
                    mmv = self.sPE.n
                    xi = os % 2
                    p.dma("sync", xo[xi],
                          x_old[128 * rb: 128 * (rb + 1), 512 * os: 512 * (os + 1)],
                          self.sLd, wait=[(self.sV, self.sV.n)])
                    xod = self.sLd.n
                    p.wait("vector", self.sPE, mmv)
                    p.wait("vector", self.sLd, xod)
                    p.wait("vector", self.sSt, self.sSt.n)
                    mv = p.op("vector", lambda e, bank=bank, xi=xi: e.tensor_tensor(
                        xn[xi], bank[:, :], xo[xi], ALU.add), inc=self.sV)
                    p.dma("sync", x_new[128 * rb: 128 * (rb + 1),
                                        512 * os: 512 * (os + 1)],
                          xn[xi], self.sSt, wait=[(self.sV, mv)])
        self.psum_close()

    def ffn1(self, x_old, x_new):
        self.fence()
        p, dt = self.p, self.dt
        self.wait_wcast("scalar", "w1ppT", "w3ppT", "b1ppT", "b3ppT", "w2ppT", "b2T")
        self.wait_wcast("sync", "a13sb", "a2sb")
        ab, af = self.aB, self.aF
        staging = ab[:, 0:16384]
        xfold = ab[:, 16384:20480]
        hfold = ab[:, 20480:31744]
        h4 = [ab[:, 31744 + 512 * i: 31744 + 512 * (i + 1)] for i in range(4)]
        w1t = [ab[:, 33792 + 4096 * i: 33792 + 4096 * (i + 1)] for i in range(2)]
        w3t = [ab[:, 41984 + 4096 * i: 41984 + 4096 * (i + 1)] for i in range(2)]
        a13buf = ab[:, 50176:50304]
        a2sbuf = ab[:, 50304:50480]
        zrow = ab[0:2, 50480:50992]
        zrow3 = ab[0:2, 50992:51504]
        z2row = ab[0:2, 51504:52016]
        b1t = [ab[0:2, 52016 + 512 * i: 52016 + 512 * (i + 1)] for i in range(2)]
        b3t = [ab[0:2, 53040 + 512 * i: 53040 + 512 * (i + 1)] for i in range(2)]
        b2t = [ab[0:2, 54064 + 512 * i: 54064 + 512 * (i + 1)] for i in range(2)]
        w2tiles = [staging[:, 0:11264], ab[:, 33792:45056]]
        xf = [af[:, 0:4096], af[:, 4096:8192]]
        xo = [af[:, 512 * i: 512 * (i + 1)] for i in range(2)]
        xn = [af[:, 1024 + 512 * i: 1536 + 512 * i] for i in range(2)]
        lnf = af[:, 8192:12288]
        stats2 = [af[:, 12288:12296], af[:, 13832:13840]]
        gsc = [af[:, 12296:12808], af[:, 12808:13320]]
        usc = [af[:, 13320:13832], af[:, 8192:8704]]
        ps = self.psum_open([(f"pq{i}", [128, 512], F32) for i in range(6)]
                            + [("pzz", [128, 1024], F32)])
        pz1 = ps[6]
        p.dma("sync", a2sbuf, self.wb["a2sb"][:, :], self.sLd)
        p.dma("sync", a13buf, self.wb["a13sb"][:, :], self.sLd)
        a2ld = self.sLd.n
        for g in range(4):
            for tt in range(4):
                i = 4 * g + tt
                pi = i % 2
                p.dma("sync", xf[pi][:, :], x_old[128 * i:128 * (i + 1), :],
                      self.sLd, wait=[(self.sPE, self.sPE.n), (self.sV, self.sV.n),
                                      (self.sA, self.sA.n)])
                ld = self.sLd.n
                p.wait("vector", self.sLd, ld)
                p.wait("scalar", self.sLd, ld)
                av = self.ln_tile(xf[pi], stats2[pi], xf[1 - pi], lnf)
                p.wait("tensor", self.sV, av)
                a_done = {}
                for grp in range(8):
                    trb = ps[4 + grp % 2]
                    if grp >= 2:
                        p.wait("tensor", self.sA, a_done[grp - 2])
                    else:
                        p.wait("tensor", self.sA, self.sA.n)
                    pe_l = None
                    for j in range(4):
                        ch = grp * 4 + j
                        pe_l = p.op("tensor", lambda e, ch=ch, j=j, trb=trb:
                                    e.transpose(trb[:, 128 * j:128 * (j + 1)],
                                                lnf[:, 128 * ch:128 * (ch + 1)],
                                                self.idF[:, :]), inc=self.sPE)
                    p.wait("scalar", self.sPE, pe_l)
                    for j in range(4):
                        ch = grp * 4 + j
                        p.op("scalar", lambda e, ch=ch, j=j, tt=tt, trb=trb:
                             e.activation(
                                 staging[:, 512 * ch + 128 * tt:
                                         512 * ch + 128 * (tt + 1)],
                                 trb[:, 128 * j:128 * (j + 1)], AF.Copy),
                             inc=self.sA)
                    a_done[grp] = self.sA.n
            stv = self.sA.n
            p.wait("vector", self.sA, stv)
            for step in range(3):
                for cf in range(8):
                    src_blk = staging[:, 512 * (cf + 8 * (step + 1)):
                                      512 * (cf + 8 * (step + 1) + 1)]
                    if step == 0:
                        p.op("vector", lambda e, cf=cf, src_blk=src_blk:
                             e.tensor_tensor(xfold[:, 512 * cf: 512 * (cf + 1)],
                                             staging[:, 512 * cf: 512 * (cf + 1)],
                                             src_blk, ALU.add), inc=self.sV)
                    else:
                        p.op("vector", lambda e, cf=cf, src_blk=src_blk:
                             e.tensor_tensor(xfold[:, 512 * cf: 512 * (cf + 1)],
                                             xfold[:, 512 * cf: 512 * (cf + 1)],
                                             src_blk, ALU.add), inc=self.sV)
                p.wait("vector", self.sV, self.sV.n)
            foldv = self.sV.n
            p.wait("tensor", self.sLd, a2ld)
            p.wait("tensor", self.sA, stv)
            for ch in range(32):
                p.op("tensor", lambda e, ch=ch: e.matmul(
                    pz1[0:2, 0:512], a13buf[:, 4 * ch:4 * ch + 2],
                    staging[:, 512 * ch: 512 * (ch + 1)],
                    start=(ch == 0), stop=(ch == 31), skip_group_check=True),
                    inc=None)
                p.op("tensor", lambda e, ch=ch: e.matmul(
                    pz1[0:2, 512:1024], a13buf[:, 4 * ch + 2:4 * ch + 4],
                    staging[:, 512 * ch: 512 * (ch + 1)],
                    start=(ch == 0), stop=(ch == 31), skip_group_check=True),
                    inc=self.sPE if ch == 31 else None)
            z1v = self.sPE.n
            p.wait("scalar", self.sPE, z1v)
            p.op("scalar", lambda e: e.activation(zrow, pz1[0:2, 0:512], AF.Copy,
                                                  scale=LORA_SC), inc=self.sA)
            p.op("scalar", lambda e: e.activation(zrow3, pz1[0:2, 512:1024], AF.Copy,
                                                  scale=LORA_SC), inc=self.sA)
            zev = self.sA.n
            p.wait("tensor", self.sA, zev)
            first_z2 = True
            pend = None
            hv_at = {}
            guv_at = {}
            aev_at = {}
            for jc in range(22):
                wp = jc % 2
                p.wait("scalar", self.sPE, guv_at.get(jc - 2, 0))
                for kc in range(8):
                    p.dma("scalar", w1t[wp][:, 512 * kc:512 * (kc + 1)],
                          self.wb["w1ppT"][128 * kc:128 * (kc + 1),
                                           512 * jc:512 * (jc + 1)], self.sLd2)
                    p.dma("scalar", w3t[wp][:, 512 * kc:512 * (kc + 1)],
                          self.wb["w3ppT"][128 * kc:128 * (kc + 1),
                                           512 * jc:512 * (jc + 1)], self.sLd2)
                p.dma("scalar", b1t[wp], self.wb["b1ppT"][:, 512 * jc:512 * (jc + 1)],
                      self.sLd2)
                p.dma("scalar", b3t[wp], self.wb["b3ppT"][:, 512 * jc:512 * (jc + 1)],
                      self.sLd2)
                wld = self.sLd2.n
                p.wait("tensor", self.sLd2, wld)
                p.wait("tensor", self.sV, foldv)
                for r4 in range(4):
                    fch = jc * 4 + r4
                    bank_g, bank_u = ps[0 + (fch % 2)], ps[2 + (fch % 2)]
                    p.wait("tensor", self.sA, aev_at.get(fch - 2, 0))
                    for kc in range(8):
                        p.op("tensor", lambda e, kc=kc, r4=r4, wp=wp, bank_g=bank_g:
                             e.matmul(bank_g[:, :],
                                      w1t[wp][:, 512 * kc + 128 * r4:
                                              512 * kc + 128 * (r4 + 1)],
                                      xfold[:, 512 * kc:512 * (kc + 1)],
                                      start=(kc == 0), stop=False,
                                      skip_group_check=True), inc=None)
                    p.op("tensor", lambda e, r4=r4, wp=wp, bank_g=bank_g: e.matmul(
                        bank_g[:, :], b1t[wp][:, 128 * r4:128 * (r4 + 1)], zrow,
                        start=False, stop=True, skip_group_check=True),
                        inc=self.sPE)
                    if pend is not None:
                        pr4, phv, pfch = pend
                        p.wait("tensor", self.sV, phv)
                        p.op("tensor", lambda e, first=first_z2, pr4=pr4, pfch=pfch:
                             e.matmul(pz1[0:2, 0:512],
                                      a2sbuf[:, 2 * pfch:2 * (pfch + 1)], h4[pr4],
                                      start=first, stop=False,
                                      skip_group_check=True), inc=None)
                        first_z2 = False
                        pend = None
                    for kc in range(8):
                        p.op("tensor", lambda e, kc=kc, r4=r4, wp=wp, bank_u=bank_u:
                             e.matmul(bank_u[:, :],
                                      w3t[wp][:, 512 * kc + 128 * r4:
                                              512 * kc + 128 * (r4 + 1)],
                                      xfold[:, 512 * kc:512 * (kc + 1)],
                                      start=(kc == 0), stop=False,
                                      skip_group_check=True), inc=None)
                    p.op("tensor", lambda e, r4=r4, wp=wp, bank_u=bank_u: e.matmul(
                        bank_u[:, :], b3t[wp][:, 128 * r4:128 * (r4 + 1)], zrow3,
                        start=False, stop=True, skip_group_check=True),
                        inc=self.sPE)
                    guv = self.sPE.n
                    p.wait("scalar", self.sPE, guv)
                    gp = fch % 2
                    p.wait("scalar", self.sV, hv_at.get(fch - 2, 0))
                    sg = p.op("scalar", lambda e, bank_g=bank_g, gp=gp: e.activation(
                        gsc[gp], bank_g[:, :], AF.Silu), inc=self.sA)
                    su = p.op("scalar", lambda e, bank_u=bank_u, gp=gp: e.activation(
                        usc[gp], bank_u[:, :], AF.Copy), inc=self.sA)
                    aev_at[fch] = su
                    p.wait("vector", self.sA, su)
                    hv = p.op("vector", lambda e, r4=r4, gp=gp: e.tensor_tensor(
                        h4[r4], gsc[gp], usc[gp], ALU.mult), inc=self.sV)
                    hv_at[fch] = hv
                    pend = (r4, hv, fch)
                guv_at[jc] = self.sPE.n
                f1 = p.op("vector", lambda e, jc=jc: e.tensor_tensor(
                    hfold[:, 512 * jc:512 * (jc + 1)], h4[0], h4[1], ALU.add),
                    inc=self.sV)
                p.wait("vector", self.sV, f1)
                f2 = p.op("vector", lambda e, jc=jc: e.tensor_tensor(
                    hfold[:, 512 * jc:512 * (jc + 1)],
                    hfold[:, 512 * jc:512 * (jc + 1)], h4[2], ALU.add), inc=self.sV)
                p.wait("vector", self.sV, f2)
                p.op("vector", lambda e, jc=jc: e.tensor_tensor(
                    hfold[:, 512 * jc:512 * (jc + 1)],
                    hfold[:, 512 * jc:512 * (jc + 1)], h4[3], ALU.add), inc=self.sV)
            pr4, phv, pfch = pend
            p.wait("tensor", self.sV, phv)
            p.op("tensor", lambda e, pr4=pr4, pfch=pfch: e.matmul(
                pz1[0:2, 0:512], a2sbuf[:, 2 * pfch:2 * (pfch + 1)], h4[pr4],
                start=False, stop=True, skip_group_check=True), inc=self.sPE)
            pend = None
            hforv = self.sV.n
            z2v = self.sPE.n
            p.wait("scalar", self.sPE, z2v)
            z2e = p.op("scalar", lambda e: e.activation(z2row, pz1[0:2, 0:512],
                                                        AF.Copy, scale=LORA_SC),
                       inc=self.sA)
            p.wait("vector", self.sA, z2e)
            os_mm = {}
            for os in range(8):
                wpp = os % 2
                p.wait("scalar", self.sPE, os_mm.get(os - 2, self.sPE.n if os < 2 else 0))
                for kc in range(22):
                    p.dma("scalar", w2tiles[wpp][:, 512 * kc:512 * (kc + 1)],
                          self.wb["w2ppT"][128 * kc:128 * (kc + 1),
                                           512 * os:512 * (os + 1)], self.sLd2)
                p.dma("scalar", b2t[wpp],
                      self.wb["b2T"][:, 512 * os:512 * (os + 1)], self.sLd2)
                w2ld = self.sLd2.n
                for ts in range(4):
                    rb = 4 * g + ts
                    bank = ps[4 + (ts % 2)]
                    p.wait("tensor", self.sLd2, w2ld)
                    if os == 0 and ts == 0:
                        p.wait("tensor", self.sV, hforv)
                    p.wait("tensor", self.sV, self.sV.n)
                    for kc in range(22):
                        p.op("tensor", lambda e, kc=kc, ts=ts, os=os, bank=bank,
                             wpp=wpp: e.matmul(
                                 bank[:, :],
                                 hfold[:, 512 * kc + 128 * ts:
                                       512 * kc + 128 * (ts + 1)],
                                 w2tiles[wpp][:, 512 * kc:512 * (kc + 1)],
                                 start=(kc == 0), stop=False,
                                 skip_group_check=True), inc=None)
                    p.op("tensor", lambda e, bank=bank, ts=ts, wpp=wpp: e.matmul(
                        bank[:, :], z2row[:, 128 * ts:128 * (ts + 1)],
                        b2t[wpp], start=False, stop=True, skip_group_check=True),
                        inc=self.sPE)
                    mmv = self.sPE.n
                    xi = ts % 2
                    p.dma("sync", xo[xi],
                          x_old[128 * rb: 128 * (rb + 1), 512 * os: 512 * (os + 1)],
                          self.sLd, wait=[(self.sV, self.sV.n)])
                    xod = self.sLd.n
                    p.wait("vector", self.sPE, mmv)
                    p.wait("vector", self.sLd, xod)
                    p.wait("vector", self.sSt, self.sSt.n)
                    mv = p.op("vector", lambda e, bank=bank, xi=xi: e.tensor_tensor(
                        xn[xi], bank[:, :], xo[xi], ALU.add), inc=self.sV)
                    p.dma("sync", x_new[128 * rb: 128 * (rb + 1),
                                        512 * os: 512 * (os + 1)],
                          xn[xi], self.sSt, wait=[(self.sV, mv)])
                os_mm[os] = self.sPE.n
        self.psum_close()

    def finish(self, taps=None):
        p, dt = self.p, self.dt
        self.fence()
        if taps:
            for name, h in taps:
                o = self.nc.dram_tensor(name, list(h.shape), h.dtype,
                                        kind="ExternalOutput")
                p.dma("sync", o[tuple(slice(None) for _ in h.shape)],
                      h[tuple(slice(None) for _ in h.shape)], self.sSt)
        p.wait("sync", self.sSt)
        p.wait("sync", self.sLd)
        p.wait("sync", self.sLd2)
        p.wait("sync", self.sCast)

    def build(self):
        nc = self.nc
        with self.es:
            sem_cms = []
            for s in self.p.sems:
                cm = nc.semaphore(s.name)
                s.h = cm.__enter__()
                sem_cms.append(cm)
            with nc.Block() as block:
                self.p.build_into(block)
            for cm in reversed(sem_cms):
                cm.__exit__(None, None, None)
        return nc


PHASES_ALL = ("ln1", "qkv0", "attn0", "oproj0", "ffn0",
              "ln3", "qkv1", "attn1", "oproj1", "ffn1")


def build_program(phases=PHASES_ALL, cut=0, taps=()):
    nc = bass.Bass("TRN2", target_bir_lowering=False)
    b = Builder(nc, cut=cut)
    b.prologue()
    if cut in (0, 1):
        b.emit_casts(["w0q", "w0kv"])
    if cut == 2:
        b.emit_casts(b._order_l1)
    dt = b.dt
    for ph in phases:
        if ph == "ln1":
            b.ln_phase("x0")
        elif ph == "ex1" or ph == "ex3":
            b.exchange_lnx()
        elif ph == "qkv0":
            b.qproj(b.wb["w0q"], dt["qtm"], wname="w0q")
            b.kvproj(b.wb["w0kv"], [dt[f"LNXPAIRp{k}"] for k in range(4)],
                     [0, 1], dt["kvK0"], dt["kvV0"], wname="w0kv")
        elif ph == "attn0":
            b.p.wait("gpsimd", b.sLd, b.sLd.n)
            b.attention(0, cast_pend=b.cast_chunks(b._order_l0 + b._order_l1))
        elif ph == "exc0" or ph == "exc1":
            b.exchange_ctx()
        elif ph == "oproj0":
            b.oproj(0, b.wb["w0o"], dt["x0"], dt["xA"], wname="w0o")
        elif ph == "ffn0":
            b.ffn0(dt["xA"], dt["xB"] if cut == 0 else dt["xB_out"])
        elif ph == "ln3":
            b.ln_phase("xB")
        elif ph == "qkv1":
            b.qproj(b.wb["w1q_attn"], dt["qtm"], wname="w1q_attn")
            for bi in range(4):
                b.kvproj(b.wb["w1kv"], [dt[f"LNXALLp{k}"] for k in range(4)],
                         [2 * bi, 2 * bi + 1],
                         dt["kv1K"][128 * bi:128 * (bi + 1), :],
                         dt["kv1V"][128 * bi:128 * (bi + 1), :], wname="w1kv")
        elif ph == "attn1":
            b.attention(1)
        elif ph == "oproj1":
            b.oproj(1, b.wb["w1o"], dt["xB"], dt["xC"], wname="w1o")
        elif ph == "ffn1":
            b.ffn1(dt["xC"], dt["x_out"])
        else:
            raise ValueError(ph)
    b.finish(taps=[(n + "_tap", dt[n]) for n in taps])
    return b.build()


def kernel(**inputs):
    nc = build_program()
    in_maps = [host_prep(inputs, c) for c in range(NCORES)]
    res = run_bass_kernel_spmd(nc, in_maps, core_ids=list(range(NCORES)))
    return assemble_out(res.results)
